# version 15
# baseline (speedup 1.0000x reference)
import math
import numpy as np
import ml_dtypes
import concourse.bass as bass
import concourse.mybir as mybir
from concourse.bass_utils import run_bass_kernel_spmd

F32 = mybir.dt.float32
BF16 = mybir.dt.bfloat16
AF = mybir.ActivationFunctionType
ALU = mybir.AluOpType
AX = mybir.AxisListType
bfdt = ml_dtypes.bfloat16

B, S, D = 4, 4096, 1024
NT = 16
T = 2048
DC = 8
KC = 32
NEG = -1.0e9

H0 = [0, 1, 2, 3, 8, 9, 10, 11]
H64 = [4, 5, 6, 7, 12, 13, 14, 15]


def gqa_pos(h):
    if h in H0:
        return 2 * H0.index(h)
    return 2 * H64.index(h) + 1



def rope_tables(positions, hd, base, conv):
    pos = np.asarray(positions, np.float64)[:, None]
    half = hd // 2
    inv = 1.0 / (base ** (np.arange(half, dtype=np.float64) / half))
    ang = pos * inv[None, :]
    c, s = np.cos(ang), np.sin(ang)
    C = np.zeros((len(positions), hd), np.float32)
    Sg = np.zeros((len(positions), hd), np.float32)
    if conv == "half":
        C[:, :half] = c
        C[:, half:] = c
        Sg[:, :half] = -s
        Sg[:, half:] = s
    else:
        C[:, 0::2] = c
        C[:, 1::2] = c
        Sg[:, 0::2] = -s
        Sg[:, 1::2] = s
    return C, Sg


def core_constants(p):
    out = {}
    qpos = np.concatenate([np.arange(128) + 128 * (2 * i + p) for i in range(NT)])
    mc = np.zeros((4, 8, 128, 512), np.float32)
    for g in range(4):
        qp = qpos[g * 512:(g + 1) * 512]
        for j in range(8):
            kp = np.arange(128) + 128 * (8 * g + j)
            mc[g, j] = np.where(kp[:, None] <= qp[None, :], 1.0, 0.0)
    out["mask_c"] = mc.astype(bfdt)
    WIN = 256
    ml = np.zeros((NT, 4, 128, 128), np.float32)
    for i in range(NT):
        t = 2 * i + p
        qp = qpos[i * 128:(i + 1) * 128]
        for j in range(4):
            c = max(0, 2 * i - 2 + j)
            kp = np.arange(128) + 128 * c
            diff = qp[None, :] - kp[:, None]
            vis = (diff >= 0) & (diff < WIN)
            if c != 2 * i - 2 + j:
                vis = np.zeros_like(vis)
            ml[i, j] = np.where(vis, 1.0, 0.0)
    out["mask_l"] = ml.astype(bfdt)
    for name, base, conv in [("r0", 1e4, "half"), ("rg", 1e6, "inter"), ("rl", 1e4, "inter")]:
        C, Sg = rope_tables(qpos, 64, base, conv)
        out["C" + name] = C.reshape(NT, 128, 64).astype(bfdt)
        out["S" + name] = Sg.reshape(NT, 128, 64).astype(bfdt)
    C, Sg = rope_tables(np.arange(S), 64, 1e6, "inter")
    out["Crgf"] = C.reshape(KC, 128, 64).astype(bfdt)
    out["Srgf"] = Sg.reshape(KC, 128, 64).astype(bfdt)
    out["ident"] = np.eye(128, dtype=bfdt)
    out["ones_col"] = np.ones((128, 1), dtype=bfdt)
    out["sel_lo"] = np.concatenate([np.ones((1, 64)), np.zeros((1, 64))], 1).astype(bfdt)
    out["sel_hi"] = np.concatenate([np.zeros((1, 64)), np.ones((1, 64))], 1).astype(bfdt)
    return out


def shard_x(x_full, b, p):
    rows = np.concatenate([x_full[b, 128 * (2 * i + p):128 * (2 * i + p) + 128] for i in range(NT)], 0)
    return np.ascontiguousarray(rows.T.reshape(DC, 128, T))


def unshard_x(cores_out):
    x = np.zeros((B, S, D), np.float32)
    for c in range(8):
        b, p = c // 2, c % 2
        fm = cores_out[c].reshape(D, T)
        rows = fm.T
        for i in range(NT):
            x[b, 128 * (2 * i + p):128 * (2 * i + p) + 128] = rows[i * 128:(i + 1) * 128]
    return x


def interleave_pair(a0, a1):
    s = np.zeros((S,) + a0.shape[1:], a0.dtype)
    for i in range(NT):
        s[128 * 2 * i: 128 * 2 * i + 128] = a0[i * 128:(i + 1) * 128]
        s[128 * (2 * i + 1): 128 * (2 * i + 1) + 128] = a1[i * 128:(i + 1) * 128]
    return s



class Step:
    __slots__ = ("sem", "val")

    def __init__(self, sem, val):
        self.sem = sem
        self.val = val


class Buf:
    __slots__ = ("t", "writer", "readers", "name")

    def __init__(self, t, name=""):
        self.t = t
        self.writer = None
        self.readers = []
        self.name = name

    def __getitem__(self, idx):
        return self.t[idx]

    @property
    def ap(self):
        return self.t


class Prog:
    NDMA = 8

    def __init__(self, nc):
        self.nc = nc
        self.q = {e: [] for e in ("pe", "act", "dve", "pool", "sp")}
        self.sem_names = ["spe", "sact", "sdve", "spool", "ssp"]
        self.dma_pools = {}
        for eng in ("sp", "act", "pool"):
            names = [f"d{eng}{i}" for i in range(self.NDMA)]
            self.dma_pools[eng] = names
            self.sem_names += names
        self.dma_rr = {eng: 0 for eng in ("sp", "act", "pool")}
        self.counts = {s: 0 for s in self.sem_names}
        self.waited = {e: {} for e in self.q}
        self.sems = {}
        self._cc_sems = []

    def sem_handles(self, ctx):
        for s in self.sem_names:
            self.sems[s] = ctx.enter_context(self.nc.semaphore(s))

    def cc_sem(self, ctx):
        name = f"scc{len(self._cc_sems)}"
        h = ctx.enter_context(self.nc.semaphore(name))
        self._cc_sems.append(name)
        self.sems[name] = h
        self.counts[name] = 0
        self.waited_any = None
        return name

    ENG_SEM = {"pe": "spe", "act": "sact", "dve": "sdve", "pool": "spool", "sp": "ssp"}

    def phase_barrier(self):
        snap = []
        for eng2, sem in self.ENG_SEM.items():
            if self.counts[sem] > 0:
                snap.append((sem, self.counts[sem]))
        for eng2, names in self.dma_pools.items():
            for nm in names:
                if self.counts[nm] > 0:
                    snap.append((nm, self.counts[nm]))
        for eng in ("pe", "act", "dve", "pool", "sp"):
            with self.step(eng) as st:
                for sem, v in snap:
                    st.dep(Step(sem, v))
                if eng in ("dve", "pool"):
                    st.add(lambda e_: e_.engine_nop())
                else:
                    st.add(lambda e_: e_.nop())

    def final_barrier(self):
        with self.step("pool") as st:
            for eng, names in self.dma_pools.items():
                for nm in names:
                    if self.counts[nm] > 0:
                        st.dep(Step(nm, self.counts[nm]))
            st.add(lambda e_: e_.engine_nop())

    def step(self, engine, reads=(), writes=(), dma=False, collective_sem=None):
        return _StepCtx(self, engine, list(reads), list(writes), dma, collective_sem)

    def emit_into_block(self, block):
        qs = self.q
        sems = self.sems

        @block.tensor
        def _(eng):
            for f in qs["pe"]:
                f(eng)

        @block.scalar
        def _(eng):
            for f in qs["act"]:
                f(eng)

        @block.vector
        def _(eng):
            for f in qs["dve"]:
                f(eng)

        @block.gpsimd
        def _(eng):
            for f in qs["pool"]:
                f(eng)

        @block.sync
        def _(eng):
            for f in qs["sp"]:
                f(eng)


class _StepCtx:
    def __init__(self, prog, engine, reads, writes, dma, collective_sem):
        self.p = prog
        self.engine = engine
        self.reads = reads
        self.writes = writes
        self.dma = dma
        self.ccs = collective_sem
        self.fns = []
        self.extra_deps = []

    def dep(self, step):
        if step is not None:
            self.extra_deps.append(step)

    def add(self, fn):
        self.fns.append(fn)
        return self

    def __enter__(self):
        return self

    def __exit__(self, et, ev, tb):
        if et is not None:
            return False
        p, eng = self.p, self.engine
        deps = list(self.extra_deps)
        for b in self.reads:
            if b.writer is not None:
                deps.append(b.writer)
        for b in self.writes:
            if b.writer is not None:
                deps.append(b.writer)
            deps.extend(b.readers)
        waited = p.waited[eng]
        need = {}
        for d in deps:
            if d.val > need.get(d.sem, -1):
                need[d.sem] = d.val
        fns = self.fns
        assert fns, "empty step"
        n = len(fns)
        if self.ccs:
            inc_per, sem_name, issue_wait = 1, self.ccs, None
        elif self.dma:
            rr = p.dma_rr[eng]
            p.dma_rr[eng] = rr + 1
            sem_name = p.dma_pools[eng][rr % p.NDMA]
            inc_per = 16
            issue_wait = (sem_name, p.counts[sem_name])
        else:
            inc_per, sem_name, issue_wait = 1, p.ENG_SEM[eng], None
        base = p.counts[sem_name]
        if self.dma or self.ccs:
            final = base + inc_per * n
        else:
            final = base + 1
        p.counts[sem_name] = final
        wait_list = []
        if issue_wait is not None and issue_wait[1] > 0 and waited.get(issue_wait[0], -1) < issue_wait[1]:
            wait_list.append(issue_wait)
        own_eng_sem = p.ENG_SEM[eng] if eng == "pe" else None
        for s, v in need.items():
            if s == own_eng_sem:
                continue
            if waited.get(s, -1) >= v:
                continue
            wait_list.append((s, v))
        for s, v in wait_list:
            waited[s] = max(waited.get(s, -1), v)
        sems = p.sems

        def emit(e, _wl=wait_list, _fns=fns, _dma=self.dma, _cc=self.ccs,
                 _sem=sem_name, _inc=inc_per, _last_only=not (self.dma or self.ccs)):
            for s, v in _wl:
                e.wait_ge(sems[s], v)
            for i, f in enumerate(_fns):
                inst = f(e)
                if _last_only:
                    if i == len(_fns) - 1:
                        inst.then_inc(sems[_sem], 1)
                else:
                    inst.then_inc(sems[_sem], _inc)

        p.q[eng].append(emit)
        st = Step(sem_name, final)
        for b in self.reads:
            b.readers.append(st)
        for b in self.writes:
            b.writer = st
            b.readers = []
        self.token = st
        return False



def swap_view_half(t, part, nheads):
    pitch = t.t.shape[1] if hasattr(t, "t") else t.shape[1]
    h = t.t if hasattr(t, "t") else t
    return bass.AP(h.tensor if hasattr(h, "tensor") else h, 32,
                   [[h.shape[1], part], [64, nheads], [-32, 2], [1, 32]])


def emit_norm(P, ctx, xr, xn, kind, consts, eps, gain=None, nchunks=DC, scratch=None):
    sc = scratch
    pSum, pSq, pB1, pB2 = sc["pSum"], sc["pSq"], sc["pB1"], sc["pB2"]
    xq = sc["xq"]
    ssum, ssq = sc["ssum"], sc["ssq"]
    srstd, smr = sc["srstd"], sc["smr"]
    stmp = smr
    rb, mrb, cst = sc["rb"], sc["mrb"], sc["cst"]
    onc = consts["ones_col"]
    onr = consts["ones1r"]
    inv = 1.0 / (128 * nchunks)
    stats_only = kind == "ln_stats"
    fp32_inplace = kind == "ln32" or stats_only
    if fp32_inplace:
        kind = "ln"
        xc = sc["xc"]
    else:
        with P.step("dve", reads=[xr], writes=[xn]) as st:
            st.add(lambda e: e.tensor_copy(xn[:, 0:nchunks * T], xr[:, 0:nchunks * T]))
    stat_src = xr if fp32_inplace else xn
    for tr in range(4):
        for ch in range(nchunks):
            s0 = ch * T + tr * 512
            if fp32_inplace:
                with P.step("dve", reads=[xr], writes=[xc]) as st:
                    st.add(lambda e, a=s0: e.tensor_copy(xc[:, :], xr[:, a:a + 512]))
                srcsl = lambda a: xc[:, :]
            else:
                srcsl = lambda a: xn[:, a:a + 512]
            if kind == "ln":
                with P.step("pe", reads=[stat_src if not fp32_inplace else xc, onc], writes=[pSum]) as st:
                    st.add(lambda e, a=s0, c=ch, f=srcsl: e.matmul(
                        pSum[0:1, :], onc[:, 0:1], f(a),
                        start=(c == 0), stop=(c == nchunks - 1)))
            with P.step("dve", reads=[stat_src if not fp32_inplace else xc], writes=[xq]) as st:
                st.add(lambda e, a=s0, f=srcsl: e.tensor_mul(xq[:, :], f(a), f(a)))
            with P.step("pe", reads=[xq, onc], writes=[pSq]) as st:
                st.add(lambda e, c=ch: e.matmul(
                    pSq[0:1, :], onc[:, 0:1], xq[:, :],
                    start=(c == 0), stop=(c == nchunks - 1)))
        if kind == "ln":
            with P.step("dve", reads=[pSum], writes=[ssum]) as st:
                st.add(lambda e, t=tr: e.tensor_copy(ssum[0:1, t * 512:(t + 1) * 512], pSum[0:1, :]))
        with P.step("dve", reads=[pSq], writes=[ssq]) as st:
            st.add(lambda e, t=tr: e.tensor_copy(ssq[0:1, t * 512:(t + 1) * 512], pSq[0:1, :]))
    with P.step("dve", writes=[cst]) as st:
        st.add(lambda e: e.memset(cst[:, 0:1], float(eps)))
    if kind == "ln":
        with P.step("dve", reads=[ssum], writes=[ssum]) as st:
            st.add(lambda e: e.tensor_scalar(ssum[0:1, :], ssum[0:1, :], inv, None, ALU.mult))
        with P.step("dve", reads=[ssum], writes=[stmp]) as st:
            st.add(lambda e: e.tensor_mul(stmp[0:1, :], ssum[0:1, :], ssum[0:1, :]))
        with P.step("dve", reads=[ssq, stmp], writes=[ssq]) as st:
            st.add(lambda e: e.scalar_tensor_tensor(ssq[0:1, :], ssq[0:1, :], inv, stmp[0:1, :], ALU.mult, ALU.subtract))
    else:
        with P.step("dve", reads=[ssq], writes=[ssq]) as st:
            st.add(lambda e: e.tensor_scalar(ssq[0:1, :], ssq[0:1, :], inv, None, ALU.mult))
    with P.step("act", reads=[ssq, cst], writes=[ssq]) as st:
        st.add(lambda e: e.activation(ssq[0:1, :], ssq[0:1, :], AF.Ln, bias=cst[0:1, 0:1]))
    with P.step("act", reads=[ssq], writes=[srstd]) as st:
        st.add(lambda e: e.activation(srstd[0:1, :], ssq[0:1, :], AF.Exp, scale=-0.5))
    if kind == "ln":
        with P.step("dve", reads=[ssum, srstd], writes=[smr]) as st:
            st.add(lambda e: e.tensor_mul(smr[0:1, :], ssum[0:1, :], srstd[0:1, :]))
    if stats_only:
        return
    for tr in range(4):
        with P.step("pe", reads=[srstd, onr], writes=[pB1]) as st:
            st.add(lambda e, t=tr: e.matmul(pB1[:, :], onr[0:1, :], srstd[0:1, t * 512:(t + 1) * 512],
                                            start=True, stop=True))
        with P.step("dve", reads=[pB1], writes=[rb]) as st:
            st.add(lambda e: e.tensor_copy(rb[:, :], pB1[:, :]))
        if kind == "ln":
            with P.step("pe", reads=[smr, onr], writes=[pB2]) as st:
                st.add(lambda e, t=tr: e.matmul(pB2[:, :], onr[0:1, :], smr[0:1, t * 512:(t + 1) * 512],
                                                start=True, stop=True))
            with P.step("dve", reads=[pB2], writes=[mrb]) as st:
                st.add(lambda e: e.tensor_copy(mrb[:, :], pB2[:, :]))
        if kind == "ln":
            tgt = xr if fp32_inplace else xn
            h = tgt.t
            pitch = h.shape[1]
            v3 = bass.AP(h, tr * 512, [[pitch, 128], [T, nchunks], [1, 512]])
            rbb = rb[:, 0:512].rearrange("p (o q) -> p o q", o=1).broadcast_to([128, nchunks, 512])
            mrbb = mrb[:, 0:512].rearrange("p (o q) -> p o q", o=1).broadcast_to([128, nchunks, 512])
            with P.step("dve", reads=[tgt, rb], writes=[tgt]) as st:
                st.add(lambda e, v=v3, b=rbb: e.tensor_mul(v, v, b))
            with P.step("dve", reads=[tgt, mrb], writes=[tgt]) as st:
                st.add(lambda e, v=v3, b=mrbb: e.tensor_sub(v, v, b))
        else:
            h = xn.t
            pitch = h.shape[1]
            v3 = bass.AP(h, tr * 512, [[pitch, 128], [T, nchunks], [1, 512]])
            rbb = rb[:, 0:512].rearrange("p (o q) -> p o q", o=1).broadcast_to([128, nchunks, 512])
            with P.step("dve", reads=[xn, rb], writes=[xn]) as st:
                st.add(lambda e, v=v3, b=rbb: e.tensor_mul(v, v, b))
            if gain is not None:
                gb = gain[:, 0:nchunks].rearrange("p (c o) -> p c o", o=1).broadcast_to([128, nchunks, 512])
                with P.step("dve", reads=[xn, gain], writes=[xn]) as st:
                    st.add(lambda e, v=v3, b=gb: e.tensor_tensor(v, v, b, ALU.mult))


def emit_rope(P, q_tm, tabC, tabS, toff, nheads, conv, scratch):
    r1, r2 = scratch["r1"], scratch["r2"]
    w = nheads * 64
    h = q_tm.t
    pitch = h.shape[1]
    if conv == "half":
        swap = bass.AP(h, 32, [[pitch, 128], [64, nheads], [-32, 2], [1, 32]])
    else:
        swap = bass.AP(h, 1, [[pitch, 128], [64, nheads], [2, 32], [-1, 2]])
    Cb = tabC[:, toff:toff + 64].rearrange("p (o f) -> p o f", o=1).broadcast_to([128, nheads, 64])
    Sb = tabS[:, toff:toff + 64].rearrange("p (o f) -> p o f", o=1).broadcast_to([128, nheads, 64])
    qv = h[:, 0:w].rearrange("p (h f) -> p h f", h=nheads)
    with P.step("dve", reads=[q_tm, tabC], writes=[r1]) as st:
        st.add(lambda e: e.tensor_mul(r1[:, 0:w].rearrange("p (h f) -> p h f", h=nheads), qv, Cb))
    with P.step("dve", reads=[q_tm, tabS], writes=[r2]) as st:
        st.add(lambda e: e.tensor_mul(r2[:, 0:w].rearrange("p (h f) -> p h f", h=nheads), swap, Sb))
    with P.step("dve", reads=[r1, r2], writes=[q_tm]) as st:
        st.add(lambda e: e.tensor_add(h[:, 0:w], r1[:, 0:w], r2[:, 0:w]))


def emit_qknorm(P, q_tm, nheads, scratch, eps=1e-6):
    ss = scratch["ss"]
    h = q_tm.t
    with P.step("dve", reads=[q_tm], writes=[scratch["r1"]]) as st:
        st.add(lambda e: e.tensor_mul(scratch["r1"][:, 0:nheads * 64], h[:, 0:nheads * 64], h[:, 0:nheads * 64]))
    with P.step("dve", reads=[scratch["r1"]], writes=[ss]) as st:
        st.add(lambda e: e.reduce_sum(
            ss[:, 0:nheads].rearrange("p (h o) -> p h o", o=1),
            scratch["r1"][:, 0:nheads * 64].rearrange("p (h f) -> p h f", h=nheads), axis=AX.X))
    cst = scratch["cst"]
    with P.step("dve", writes=[cst]) as st:
        st.add(lambda e: e.memset(cst[:, 1:2], float(eps)))
    with P.step("act", reads=[ss, cst], writes=[ss]) as st:
        st.add(lambda e: e.activation(ss[:, 0:nheads], ss[:, 0:nheads], AF.Ln, bias=cst[:, 1:2], scale=1.0 / 64))
    with P.step("act", reads=[ss], writes=[ss]) as st:
        st.add(lambda e: e.activation(ss[:, 0:nheads], ss[:, 0:nheads], AF.Exp, scale=-0.5))
    ssb = ss[:, 0:nheads].rearrange("p (h o) -> p h o", o=1).broadcast_to([128, nheads, 64])
    with P.step("dve", reads=[q_tm, ss], writes=[q_tm]) as st:
        st.add(lambda e: e.tensor_mul(
            h[:, 0:nheads * 64].rearrange("p (h f) -> p h f", h=nheads),
            h[:, 0:nheads * 64].rearrange("p (h f) -> p h f", h=nheads), ssb))



def norm_consts():
    ones_col = np.ones((128, 1), np.float32).astype(bfdt)
    ones1r = np.ones((1, 128), np.float32).astype(bfdt)
    return ones_col, ones1r


_NSC_UID = [0]


def alloc_norm_scratch(nc, ctx, P, banks=None, with_xc=True):
    sc = {}
    e = ctx.enter_context
    _NSC_UID[0] += 1
    uid = f"{_NSC_UID[0]}_"
    if banks is not None:
        sc["pSum"], sc["pSq"], sc["pB1"], sc["pB2"] = (
            banks["pA"], banks["pB"], banks["pC"], banks["pD"])
    else:
        sc["pSum"] = Buf(e(nc.psum_tensor("pSum", [128, 512], F32)), "pSum")
        sc["pSq"] = Buf(e(nc.psum_tensor("pSq", [128, 512], F32)), "pSq")
        sc["pB1"] = Buf(e(nc.psum_tensor("pB1", [128, 512], F32)), "pB1")
        sc["pB2"] = Buf(e(nc.psum_tensor("pB2", [128, 512], F32)), "pB2")
    sc["xq"] = Buf(e(nc.sbuf_tensor("n" + uid + "xq", [128, 512], BF16)), "xq")
    if with_xc:
        sc["xc"] = Buf(e(nc.sbuf_tensor("n" + uid + "xc", [128, 512], BF16)), "xc")
    for n in ("ssum", "ssq"):
        sc[n] = Buf(e(nc.sbuf_tensor("n" + uid + n, [1, 2048], F32)), n)
    for n in ("srstd", "smr"):
        sc[n] = Buf(e(nc.sbuf_tensor("n" + uid + n, [1, 2048], BF16)), n)
    sc["cst"] = Buf(e(nc.sbuf_tensor("n" + uid + "cst", [128, 4], F32)), "cst")
    sc["rb"] = Buf(e(nc.sbuf_tensor("n" + uid + "rb", [128, 512], BF16)), "rb")
    sc["mrb"] = Buf(e(nc.sbuf_tensor("n" + uid + "mrb", [128, 512], BF16)), "mrb")
    return sc


def build_launch_A():
    nc = bass.Bass("TRN2", target_bir_lowering=False, debug=False)
    xin = nc.dram_tensor("xin", [DC, 128, T], F32, kind="ExternalInput").ap()
    wq = nc.dram_tensor("wq", [1024, 1024], F32, kind="ExternalInput").ap()
    wk = nc.dram_tensor("wk", [1024, 256], F32, kind="ExternalInput").ap()
    wv = nc.dram_tensor("wv", [1024, 256], F32, kind="ExternalInput").ap()
    Cr0 = nc.dram_tensor("Cr0", [NT, 128, 64], BF16, kind="ExternalInput").ap()
    Sr0 = nc.dram_tensor("Sr0", [NT, 128, 64], BF16, kind="ExternalInput").ap()
    onci = nc.dram_tensor("onci", [128, 1], BF16, kind="ExternalInput").ap()
    onri = nc.dram_tensor("onri", [1, 128], BF16, kind="ExternalInput").ap()
    qo = nc.dram_tensor("qo", [NT, 128, 1024], BF16, kind="ExternalOutput").ap()
    ko = nc.dram_tensor("ko", [NT, 128, 256], BF16, kind="ExternalOutput").ap()
    vo = nc.dram_tensor("vo", [NT, 128, 256], BF16, kind="ExternalOutput").ap()

    from contextlib import ExitStack
    with ExitStack() as ctx:
        e = ctx.enter_context
        P = Prog(nc)
        P.sem_handles(ctx)
        xr = Buf(e(nc.sbuf_tensor("xr", [128, DC * T], F32)), "xr")
        xn = Buf(e(nc.sbuf_tensor("xn", [128, DC * T], BF16)), "xn")
        wq_sb = Buf(e(nc.sbuf_tensor("wq_sb", [128, DC * 1024], BF16)), "wq_sb")
        wkv_sb = Buf(e(nc.sbuf_tensor("wkv_sb", [128, DC * 512], BF16)), "wkv_sb")
        q_tm = Buf(e(nc.sbuf_tensor("q_tm", [128, 1024], BF16)), "q_tm")
        k_tm = Buf(e(nc.sbuf_tensor("k_tm", [128, 256], BF16)), "k_tm")
        v_tm = Buf(e(nc.sbuf_tensor("v_tm", [128, 256], BF16)), "v_tm")
        r1 = Buf(e(nc.sbuf_tensor("r1", [128, 1024], BF16)), "r1")
        r2 = Buf(e(nc.sbuf_tensor("r2", [128, 1024], BF16)), "r2")
        tC = Buf(e(nc.sbuf_tensor("tC", [128, NT * 64], BF16)), "tC")
        tS = Buf(e(nc.sbuf_tensor("tS", [128, NT * 64], BF16)), "tS")
        onc = Buf(e(nc.sbuf_tensor("onc", [128, 1], BF16)), "onc")
        onr = Buf(e(nc.sbuf_tensor("onr", [1, 128], BF16)), "onr")
        pQ = Buf(e(nc.psum_tensor("pQ", [128, 512], F32)), "pQ")
        nsc = alloc_norm_scratch(nc, ctx, P)
        consts = {"ones_col": onc, "ones1r": onr}
        rsc = {"r1": r1, "r2": r2}

        with P.step("sp", writes=[xr], dma=True) as st:
            st.add(lambda e_: e_.dma_start(out=xr[:, :].rearrange("p (c t) -> p c t", c=DC), in_=xin.rearrange("c p t -> p c t")))
        with P.step("sp", writes=[tC, tS, onc, onr], dma=True) as st:
            st.add(lambda e_: e_.dma_start(out=tC[:, :].rearrange("p (i f) -> p i f", i=NT), in_=Cr0.rearrange("i p f -> p i f")))
            st.add(lambda e_: e_.dma_start(out=tS[:, :].rearrange("p (i f) -> p i f", i=NT), in_=Sr0.rearrange("i p f -> p i f")))
            st.add(lambda e_: e_.dma_start(out=onc[:, :], in_=onci))
            st.add(lambda e_: e_.dma_start(out=onr[:, :], in_=onri))
        with P.step("pool", writes=[wq_sb], dma=True) as st:
            st.add(lambda e_: e_.dma_start(
                out=wq_sb[:, :].rearrange("p (c n) -> p c n", c=DC),
                in_=wq.rearrange("(c p) n -> p c n", c=DC)))
        with P.step("pool", writes=[wkv_sb], dma=True) as st:
            st.add(lambda e_: e_.dma_start(
                out=bass.AP(wkv_sb.t, 0, [[DC * 512, 128], [512, DC], [1, 256]]),
                in_=wk.rearrange("(c p) n -> p c n", c=DC)))
            st.add(lambda e_: e_.dma_start(
                out=bass.AP(wkv_sb.t, 256, [[DC * 512, 128], [512, DC], [1, 256]]),
                in_=wv.rearrange("(c p) n -> p c n", c=DC)))

        emit_norm(P, ctx, xr, xn, "ln", consts, 1e-5, scratch=nsc)

        for i in range(NT):
            for nr in range(3):
                with P.step("pe", reads=[xn, wq_sb, wkv_sb], writes=[pQ]) as st:
                    for ch in range(DC):
                        if nr < 2:
                            w_ap = wq_sb[:, ch * 1024 + nr * 512: ch * 1024 + nr * 512 + 512]
                        else:
                            w_ap = wkv_sb[:, ch * 512: ch * 512 + 512]
                        st.add(lambda e_, c=ch, w=w_ap, ii=i: e_.matmul(
                            pQ[:, :], xn[:, c * T + ii * 128: c * T + ii * 128 + 128], w,
                            start=(c == 0), stop=(c == DC - 1)))
                if nr < 2:
                    with P.step("dve", reads=[pQ], writes=[q_tm]) as st:
                        st.add(lambda e_, n=nr: e_.tensor_copy(q_tm[:, n * 512:(n + 1) * 512], pQ[:, :]))
                else:
                    with P.step("dve", reads=[pQ], writes=[k_tm, v_tm]) as st:
                        st.add(lambda e_: e_.tensor_copy(k_tm[:, :], pQ[:, 0:256]))
                        st.add(lambda e_: e_.tensor_copy(v_tm[:, :], pQ[:, 256:512]))
            emit_rope(P, q_tm, tC, tS, i * 64, 16, "half", rsc)
            emit_rope(P, k_tm, tC, tS, i * 64, 4, "half", rsc)
            with P.step("sp", reads=[q_tm, k_tm, v_tm], dma=True) as st:
                st.add(lambda e_, ii=i: e_.dma_start(out=qo[ii], in_=q_tm[:, :]))
                st.add(lambda e_, ii=i: e_.dma_start(out=ko[ii], in_=k_tm[:, :]))
                st.add(lambda e_, ii=i: e_.dma_start(out=vo[ii], in_=v_tm[:, :]))

        P.final_barrier()

        blk = e(nc.Block())
        P.emit_into_block(blk)
    return nc



def _resid_add(P, xr, oc, g, pP, w=256):
    with P.step("dve", reads=[pP, xr], writes=[xr]) as st:
        st.add(lambda e: e.tensor_add(
            xr[:, oc * T + g * w: oc * T + g * w + w], pP[:, 0:w],
            xr[:, oc * T + g * w: oc * T + g * w + w]))


def emit_attention_causal(P, cfg):
    heads = cfg["heads"]
    npairs = len(heads) // 2
    bk = cfg["banks"]
    sc = cfg["scratch"]
    pA, pB, pC, pD, pE, pF = bk["pA"], bk["pB"], bk["pC"], bk["pD"], bk["pE"], bk["pF"]
    aA, aB = sc["aA"], sc["aB"]
    outT = sc["outT"]
    cst = sc["cst"]
    defer_tail = "den2" in sc
    if "denP" not in sc:
        if defer_tail:
            sc["denP"] = [sc["den"], sc["den2"]]
            sc["rdenP"] = [sc["rden"], sc["rden2"]]
        else:
            sc["denP"] = [sc["den"], sc["den"]]
            sc["rdenP"] = [sc["rden"], sc["rden"]]
        sc["rbP"] = [sc["rb"], sc["rb"]]
    qkv_reads = cfg["reads_qkv"]
    wo_sb = cfg["wo_sb"]
    sel_lo, sel_hi = cfg["sel_lo"], cfg["sel_hi"]
    scale = cfg["scale"]
    ngroups = cfg["ngroups"]
    QW = cfg["qwidth"]
    NM = cfg["nmask"]
    mload = cfg["mask_load"]

    for g in range(ngroups):
        nch = cfg["nch_of_g"](g)
        mask = mload(P, g)
        tail_pend = None
        for m in range(npairs):
            hL, hH = heads[2 * m], heads[2 * m + 1]

            def emit_av(j, at):
                c = cfg["chunk_of"](g, j)
                with P.step("pe", reads=[at, *qkv_reads], writes=[pC, pD]) as st:
                    st.add(lambda e, h_=hL, c_=c, at_=at, j_=j, n_=nch: e.matmul(
                        pC[0:65, 0:QW], cfg["vslice"](h_, c_), at_[:, 0:QW],
                        start=(j_ == 0), stop=(j_ == n_ - 1)))
                    st.add(lambda e, h_=hH, c_=c, at_=at, j_=j, n_=nch: e.matmul(
                        pD[0:65, 0:QW], cfg["vslice"](h_, c_), at_[:, QW:2 * QW],
                        start=(j_ == 0), stop=(j_ == n_ - 1)))

            av_pend = None
            for j in range(nch):
                c = cfg["chunk_of"](g, j)
                psl = pA if j % 2 == 0 else pB
                psh = pE if j % 2 == 0 else pF
                with P.step("pe", reads=qkv_reads, writes=[psl, psh]) as st:
                    st.add(lambda e, h_=hL, c_=c, g_=g, ps_=psl: e.matmul(
                        ps_[:, 0:QW], cfg["kslice"](h_, c_), cfg["qslice"](h_, g_),
                        start=True, stop=True))
                    st.add(lambda e, h_=hH, c_=c, g_=g, ps_=psh: e.matmul(
                        ps_[:, 0:QW], cfg["kslice"](h_, c_), cfg["qslice"](h_, g_),
                        start=True, stop=True))
                at = aA if j % 2 == 0 else aB
                with P.step("act", reads=[psl, psh], writes=[at]) as st:
                    st.add(lambda e, ps_=psl, at_=at: e.activation(
                        at_[:, 0:QW], ps_[:, 0:QW], AF.Exp, scale=scale))
                    st.add(lambda e, ps_=psh, at_=at: e.activation(
                        at_[:, QW:2 * QW], ps_[:, 0:QW], AF.Exp, scale=scale))
                jm = j - (nch - NM)
                if jm >= 0:
                    mb = mask[:, jm * QW:(jm + 1) * QW].rearrange(
                        "p (o q) -> p o q", o=1).broadcast_to([128, 2, QW])
                    with P.step("dve", reads=[at, mask], writes=[at]) as st:
                        st.add(lambda e, at_=at, mb_=mb: e.tensor_mul(
                            at_[:, 0:2 * QW].rearrange("p (o q) -> p o q", o=2),
                            at_[:, 0:2 * QW].rearrange("p (o q) -> p o q", o=2), mb_))
                if av_pend is not None:
                    emit_av(*av_pend)
                av_pend = (j, at)
            emit_av(*av_pend)

            def pair_tail(m=m, den=sc["denP"][m % 2], rden=sc["rdenP"][m % 2],
                          rb=sc["rbP"][m % 2]):
                with P.step("dve", reads=[pC, pD], writes=[den, outT]) as st:
                    st.add(lambda e: e.tensor_copy(den[0:1, 0:QW], pC[64:65, 0:QW]))
                    st.add(lambda e: e.tensor_copy(den[0:1, QW:2 * QW], pD[64:65, 0:QW]))
                    st.add(lambda e: e.tensor_copy(
                        outT[0:64, m * QW:(m + 1) * QW], pC[0:64, 0:QW]))
                    st.add(lambda e: e.tensor_copy(
                        outT[64:128, m * QW:(m + 1) * QW], pD[0:64, 0:QW]))

            def pair_tail2(m=m, den=sc["denP"][m % 2], rden=sc["rdenP"][m % 2],
                           rb=sc["rbP"][m % 2]):
                with P.step("act", reads=[den, cst], writes=[den]) as st:
                    st.add(lambda e: e.activation(den[0:1, 0:2 * QW], den[0:1, 0:2 * QW], AF.Ln,
                                                  bias=cst[0:1, 3:4]))
                with P.step("act", reads=[den], writes=[rden]) as st:
                    st.add(lambda e: e.activation(rden[0:1, 0:2 * QW], den[0:1, 0:2 * QW], AF.Exp,
                                                  scale=-1.0))
                with P.step("pe", reads=[rden, sel_lo, sel_hi], writes=[pF]) as st:
                    st.add(lambda e: e.matmul(
                        pF[:, 0:QW], sel_lo[0:1, :], rden[0:1, 0:QW],
                        start=True, stop=False))
                    st.add(lambda e: e.matmul(
                        pF[:, 0:QW], sel_hi[0:1, :], rden[0:1, QW:2 * QW],
                        start=False, stop=True))
                with P.step("dve", reads=[pF], writes=[rb]) as st:
                    st.add(lambda e: e.tensor_copy(rb[:, 0:QW], pF[:, 0:QW]))
                with P.step("dve", reads=[outT, rb], writes=[outT]) as st:
                    st.add(lambda e: e.tensor_mul(
                        outT[:, m * QW:(m + 1) * QW], outT[:, m * QW:(m + 1) * QW], rb[:, 0:QW]))

            pair_tail()
            if defer_tail:
                if tail_pend is not None:
                    tail_pend()
                tail_pend = pair_tail2
            else:
                pair_tail2()
        if defer_tail:
            tail_pend()
        pcs = cfg.get("proj_chunks", [(c, c) for c in range(npairs)])
        for oc in range(DC):
            pP = pA if oc % 2 == 0 else pB
            with P.step("pe", reads=[outT, wo_sb], writes=[pP]) as st:
                for ci, (lc, wc) in enumerate(pcs):
                    st.add(lambda e, l_=lc, w_=wc, o_=oc, pP_=pP, i_=ci: e.matmul(
                        pP_[:, 0:QW], wo_sb[:, w_ * 1024 + o_ * 128: w_ * 1024 + o_ * 128 + 128],
                        outT[:, l_ * QW:(l_ + 1) * QW],
                        start=(i_ == 0), stop=(i_ == len(pcs) - 1)))
            cfg["emit_resid"](P, oc, g, pP)


def build_launch_B():
    nc = bass.Bass("TRN2", target_bir_lowering=False, debug=False)
    xin = nc.dram_tensor("xin", [DC, 128, T], F32, kind="ExternalInput").ap()
    qTi = nc.dram_tensor("qTi", [DC, 128, T], BF16, kind="ExternalInput").ap()
    Kfmi = nc.dram_tensor("Kfmi", [2, 128, S], BF16, kind="ExternalInput").ap()
    Vtmi = nc.dram_tensor("Vtmi", [KC, 128, 256], BF16, kind="ExternalInput").ap()
    mci = nc.dram_tensor("mci", [8, 4, 128, 256], BF16, kind="ExternalInput").ap()
    wo = nc.dram_tensor("wo", [1024, 1024], F32, kind="ExternalInput").ap()
    sli = nc.dram_tensor("sli", [1, 128], BF16, kind="ExternalInput").ap()
    shi = nc.dram_tensor("shi", [1, 128], BF16, kind="ExternalInput").ap()
    xo = nc.dram_tensor("xo", [DC, 128, T], F32, kind="ExternalOutput").ap()

    from contextlib import ExitStack
    with ExitStack() as ctx:
        e = ctx.enter_context
        P = Prog(nc)
        P.sem_handles(ctx)
        xr = Buf(e(nc.sbuf_tensor("xr", [128, DC * T], F32)), "xr")
        qT = Buf(e(nc.sbuf_tensor("qT", [128, DC * T], BF16)), "qT")
        Kfm = Buf(e(nc.sbuf_tensor("Kfm", [128, 2 * S], BF16)), "Kfm")
        Vp = Buf(e(nc.sbuf_tensor("Vp", [128, KC * 260], BF16)), "Vp")
        msk = Buf(e(nc.sbuf_tensor("msk", [128, 8 * 4 * 256], BF16)), "msk")
        wo_sb = Buf(e(nc.sbuf_tensor("wo_sb", [128, DC * 1024], BF16)), "wo_sb")
        sel_lo = Buf(e(nc.sbuf_tensor("sel_lo", [1, 128], BF16)), "sel_lo")
        sel_hi = Buf(e(nc.sbuf_tensor("sel_hi", [1, 128], BF16)), "sel_hi")
        aA = Buf(e(nc.sbuf_tensor("aA", [128, 256], BF16)), "aA")
        aB = Buf(e(nc.sbuf_tensor("aB", [128, 256], BF16)), "aB")
        den = Buf(e(nc.sbuf_tensor("den", [1, 512], F32)), "den")
        rden = Buf(e(nc.sbuf_tensor("rden", [1, 512], BF16)), "rden")
        outT = Buf(e(nc.sbuf_tensor("outT", [128, 2048], BF16)), "outT")
        outTn = Buf(e(nc.sbuf_tensor("outTn", [128, 2048], BF16)), "outTn")
        rb = Buf(e(nc.sbuf_tensor("rb", [128, 256], BF16)), "rb")
        cst = Buf(e(nc.sbuf_tensor("cst", [128, 4], F32)), "cst")
        bks = {}
        for i, n in enumerate(("pA", "pB", "pC", "pD", "pE", "pF")):
            bks[n] = Buf(e(nc.psum_tensor(n, [128, 512], F32)), n)

        with P.step("sp", writes=[xr], dma=True) as st:
            st.add(lambda e_: e_.dma_start(out=xr[:, :].rearrange("p (c t) -> p c t", c=DC),
                                           in_=xin.rearrange("c p t -> p c t")))
        with P.step("sp", writes=[qT, Kfm, Vp, msk, sel_lo, sel_hi], dma=True) as st:
            st.add(lambda e_: e_.dma_start(out=qT[:, :].rearrange("p (c t) -> p c t", c=DC),
                                           in_=qTi.rearrange("c p t -> p c t")))
            st.add(lambda e_: e_.dma_start(out=Kfm[:, :].rearrange("p (c s) -> p c s", c=2),
                                           in_=Kfmi.rearrange("c p s -> p c s")))
            for k in range(4):
                st.add(lambda e_, k_=k: e_.dma_start(
                    out=bass.AP(Vp.t, k_ * 65, [[KC * 260, 128], [260, KC], [1, 64]]),
                    in_=Vtmi[:, :, k_ * 64:(k_ + 1) * 64].rearrange("c p f -> p c f")))
            st.add(lambda e_: e_.dma_start(out=msk[:, :].rearrange("p (g j q) -> p g j q", g=8, j=4),
                                           in_=mci.rearrange("g j p q -> p g j q")))
            st.add(lambda e_: e_.dma_start(out=sel_lo[:, :], in_=sli))
            st.add(lambda e_: e_.dma_start(out=sel_hi[:, :], in_=shi))
        with P.step("pool", writes=[wo_sb], dma=True) as st:
            st.add(lambda e_: e_.dma_start(
                out=wo_sb[:, :].rearrange("p (c n) -> p c n", c=DC),
                in_=wo.rearrange("(c p) n -> p c n", c=DC)))
        with P.step("dve", writes=[Vp]) as st:
            st.add(lambda e_: e_.memset(
                bass.AP(Vp.t, 64, [[KC * 260, 128], [260, KC], [65, 4], [1, 1]]), 1.0))
        with P.step("dve", writes=[cst]) as st:
            st.add(lambda e_: e_.memset(cst[:, 3:4], 0.0))

        cfg = {
            "heads": list(range(16)),
            "qslice": lambda h, g: qT[(h % 2) * 64:(h % 2) * 64 + 64,
                                      (h // 2) * T + g * 256:(h // 2) * T + g * 256 + 256],
            "kslice": lambda h, c: Kfm[((h % 4) % 2) * 64:((h % 4) % 2) * 64 + 64,
                                       ((h % 4) // 2) * S + c * 128:((h % 4) // 2) * S + c * 128 + 128],
            "vslice": lambda h, c: Vp[:, c * 260 + (h % 4) * 65: c * 260 + (h % 4) * 65 + 65],
            "mask": msk,
            "mslice": lambda g, j: msk[:, (g * 4 + j) * 256:(g * 4 + j + 1) * 256],
            "reads_qkv": [qT, Kfm, Vp],
            "banks": bks,
            "scratch": {"aA": aA, "aB": aB, "den": den, "rden": rden, "outT": outT,
                        "rb": rb, "cst": cst},
            "sel_lo": sel_lo, "sel_hi": sel_hi,
            "wo_sb": wo_sb,
            "emit_resid": (lambda P_, oc, g, pP: _resid_add(P_, xr, oc, g, pP)),
            "scale": 0.125,
            "nch_of_g": lambda g: 4 * g + 4,
            "chunk_of": lambda g, j: j,
        }
        emit_attention_causal(P, cfg)

        with P.step("sp", reads=[xr], dma=True) as st:
            st.add(lambda e_: e_.dma_start(out=xo.rearrange("c p t -> p c t"), in_=xr[:, :].rearrange("p (c t) -> p c t", c=DC)))
        P.final_barrier()

        blk = e(nc.Block())
        P.emit_into_block(blk)
    return nc


def build_launch_C():
    nc = bass.Bass("TRN2", target_bir_lowering=False, debug=False)
    x1i = nc.dram_tensor("x1i", [DC, 128, T], F32, kind="ExternalInput").ap()
    g1i = nc.dram_tensor("g1i", [1024], F32, kind="ExternalInput").ap()
    wqm = nc.dram_tensor("wqm", [1024, 1024], F32, kind="ExternalInput").ap()
    wdkv = nc.dram_tensor("wdkv", [1024, 512], F32, kind="ExternalInput").ap()
    onci = nc.dram_tensor("onci", [128, 1], BF16, kind="ExternalInput").ap()
    onri = nc.dram_tensor("onri", [1, 128], BF16, kind="ExternalInput").ap()
    ho = nc.dram_tensor("ho", [DC, 128, T], F32, kind="ExternalOutput").ap()
    qmo = nc.dram_tensor("qmo", [NT, 128, 1024], BF16, kind="ExternalOutput").ap()
    lato = nc.dram_tensor("lato", [4, 128, T], BF16, kind="ExternalOutput").ap()

    from contextlib import ExitStack
    with ExitStack() as ctx:
        e = ctx.enter_context
        P = Prog(nc)
        P.sem_handles(ctx)
        xr = Buf(e(nc.sbuf_tensor("xr", [128, DC * T], F32)), "xr")
        xn = Buf(e(nc.sbuf_tensor("xn", [128, DC * T], BF16)), "xn")
        wq_sb = Buf(e(nc.sbuf_tensor("wq_sb", [128, DC * 1024], BF16)), "wq_sb")
        wdkv_sb = Buf(e(nc.sbuf_tensor("wdkv_sb", [128, DC * 512], BF16)), "wdkv_sb")
        g1_sb = Buf(e(nc.sbuf_tensor("g1_sb", [128, DC], F32)), "g1_sb")
        q_tm = Buf(e(nc.sbuf_tensor("q_tm", [128, 1024], BF16)), "q_tm")
        lat_sb = Buf(e(nc.sbuf_tensor("lat_sb", [128, 4 * T], BF16)), "lat_sb")
        onc = Buf(e(nc.sbuf_tensor("onc", [128, 1], BF16)), "onc")
        onr = Buf(e(nc.sbuf_tensor("onr", [1, 128], BF16)), "onr")
        pQ = Buf(e(nc.psum_tensor("pQ", [128, 512], F32)), "pQ")
        nsc = alloc_norm_scratch(nc, ctx, P)
        consts = {"ones_col": onc, "ones1r": onr}

        with P.step("sp", writes=[xr], dma=True) as st:
            st.add(lambda e_: e_.dma_start(out=xr[:, :].rearrange("p (c t) -> p c t", c=DC),
                                           in_=x1i.rearrange("c p t -> p c t")))
        with P.step("sp", writes=[onc, onr, g1_sb], dma=True) as st:
            st.add(lambda e_: e_.dma_start(out=onc[:, :], in_=onci))
            st.add(lambda e_: e_.dma_start(out=onr[:, :], in_=onri))
            for ch in range(DC):
                st.add(lambda e_, c=ch: e_.dma_start(
                    out=g1_sb[:, c:c + 1],
                    in_=g1i[c * 128:(c + 1) * 128].rearrange("(p o) -> p o", o=1)))
        with P.step("pool", writes=[wq_sb], dma=True) as st:
            st.add(lambda e_: e_.dma_start(out=wq_sb[:, :].rearrange("p (c n) -> p c n", c=DC),
                                           in_=wqm.rearrange("(c p) n -> p c n", c=DC)))
        with P.step("pool", writes=[wdkv_sb], dma=True) as st:
            st.add(lambda e_: e_.dma_start(out=wdkv_sb[:, :].rearrange("p (c n) -> p c n", c=DC),
                                           in_=wdkv.rearrange("(c p) n -> p c n", c=DC)))

        emit_norm(P, ctx, xr, xn, "ln32", consts, 1e-5, scratch=nsc)
        with P.step("sp", reads=[xr], dma=True) as st:
            st.add(lambda e_: e_.dma_start(out=ho.rearrange("c p t -> p c t"),
                                           in_=xr[:, :].rearrange("p (c t) -> p c t", c=DC)))
        emit_norm(P, ctx, xr, xn, "rms", consts, 1e-6, gain=g1_sb, scratch=nsc)

        for i in range(NT):
            for nr in range(2):
                with P.step("pe", reads=[xn, wq_sb], writes=[pQ]) as st:
                    for ch in range(DC):
                        st.add(lambda e_, c=ch, n=nr, ii=i: e_.matmul(
                            pQ[:, :], xn[:, c * T + ii * 128: c * T + ii * 128 + 128],
                            wq_sb[:, c * 1024 + n * 512: c * 1024 + n * 512 + 512],
                            start=(c == 0), stop=(c == DC - 1)))
                with P.step("dve", reads=[pQ], writes=[q_tm]) as st:
                    st.add(lambda e_, n=nr: e_.tensor_copy(q_tm[:, n * 512:(n + 1) * 512], pQ[:, :]))
            with P.step("sp", reads=[q_tm], dma=True) as st:
                st.add(lambda e_, ii=i: e_.dma_start(out=qmo[ii], in_=q_tm[:, :]))
        for lc in range(4):
            for r in range(4):
                with P.step("pe", reads=[xn, wdkv_sb], writes=[pQ]) as st:
                    for ch in range(DC):
                        st.add(lambda e_, c=ch, l=lc, r_=r: e_.matmul(
                            pQ[:, :], wdkv_sb[:, c * 512 + l * 128: c * 512 + l * 128 + 128],
                            xn[:, c * T + r_ * 512: c * T + r_ * 512 + 512],
                            start=(c == 0), stop=(c == DC - 1)))
                with P.step("dve", reads=[pQ], writes=[lat_sb]) as st:
                    st.add(lambda e_, l=lc, r_=r: e_.tensor_copy(
                        lat_sb[:, l * T + r_ * 512: l * T + r_ * 512 + 512], pQ[:, :]))
        with P.step("sp", reads=[lat_sb], dma=True) as st:
            st.add(lambda e_: e_.dma_start(out=lato.rearrange("c p t -> p c t"),
                                           in_=lat_sb[:, :].rearrange("p (c t) -> p c t", c=4)))
        P.final_barrier()
        blk = e(nc.Block())
        P.emit_into_block(blk)
    return nc


def build_launch_D():
    nc = bass.Bass("TRN2", target_bir_lowering=False, debug=False)
    hi_ = nc.dram_tensor("hi_", [DC, 128, T], F32, kind="ExternalInput").ap()
    qmTi = nc.dram_tensor("qmTi", [DC, 128, T], BF16, kind="ExternalInput").ap()
    latTfi = nc.dram_tensor("latTfi", [4, 128, S], BF16, kind="ExternalInput").ap()
    wuk = nc.dram_tensor("wuk", [512, 1024], F32, kind="ExternalInput").ap()
    wuv = nc.dram_tensor("wuv", [512, 1024], F32, kind="ExternalInput").ap()
    wom = nc.dram_tensor("wom", [1024, 1024], F32, kind="ExternalInput").ap()
    mci = nc.dram_tensor("mci", [8, 4, 128, 256], BF16, kind="ExternalInput").ap()
    sli = nc.dram_tensor("sli", [1, 128], BF16, kind="ExternalInput").ap()
    shi = nc.dram_tensor("shi", [1, 128], BF16, kind="ExternalInput").ap()
    ho1 = nc.dram_tensor("ho1", [DC, 128, T], F32, kind="ExternalOutput").ap()

    from contextlib import ExitStack
    with ExitStack() as ctx:
        e = ctx.enter_context
        P = Prog(nc)
        P.sem_handles(ctx)
        latTf = Buf(e(nc.sbuf_tensor("latTf", [128, 4 * S], BF16)), "latTf")
        wuk_sb = Buf(e(nc.sbuf_tensor("wuk_sb", [128, 4 * 1024], BF16)), "wuk_sb")
        wuv_sb = Buf(e(nc.sbuf_tensor("wuv_sb", [128, 4 * 1024], BF16)), "wuv_sb")
        wom_sb = Buf(e(nc.sbuf_tensor("wom_sb", [128, DC * 1024], BF16)), "wom_sb")
        qTh = Buf(e(nc.sbuf_tensor("qTh", [128, 4 * T], BF16)), "qTh")
        Kh = Buf(e(nc.sbuf_tensor("Kh", [128, 4 * S], BF16)), "Kh")
        Vh = Buf(e(nc.sbuf_tensor("Vh", [128, KC * 520], BF16)), "Vh")
        msk = Buf(e(nc.sbuf_tensor("msk", [128, 8 * 4 * 256], BF16)), "msk")
        sel_lo = Buf(e(nc.sbuf_tensor("sel_lo", [1, 128], BF16)), "sel_lo")
        sel_hi = Buf(e(nc.sbuf_tensor("sel_hi", [1, 128], BF16)), "sel_hi")
        aA = Buf(e(nc.sbuf_tensor("aA", [128, 256], BF16)), "aA")
        aB = Buf(e(nc.sbuf_tensor("aB", [128, 256], BF16)), "aB")
        den = Buf(e(nc.sbuf_tensor("den", [1, 512], F32)), "den")
        rden = Buf(e(nc.sbuf_tensor("rden", [1, 512], BF16)), "rden")
        outT = Buf(e(nc.sbuf_tensor("outT", [128, 2048], BF16)), "outT")
        outTn = Buf(e(nc.sbuf_tensor("outTn", [128, 2048], BF16)), "outTn")
        rb = Buf(e(nc.sbuf_tensor("rb", [128, 256], BF16)), "rb")
        cst = Buf(e(nc.sbuf_tensor("cst", [128, 4], F32)), "cst")
        hsA = Buf(e(nc.sbuf_tensor("hsA", [128, 256], F32)), "hsA")
        hsB = Buf(e(nc.sbuf_tensor("hsB", [128, 256], F32)), "hsB")
        hin = Buf(hi_, "hin")
        hout = Buf(ho1, "hout")
        bks = {}
        for n in ("pA", "pB", "pC", "pD", "pE", "pF"):
            bks[n] = Buf(e(nc.psum_tensor(n, [128, 512], F32)), n)

        with P.step("sp", writes=[latTf, msk, sel_lo, sel_hi], dma=True) as st:
            st.add(lambda e_: e_.dma_start(out=latTf[:, :].rearrange("p (c s) -> p c s", c=4),
                                           in_=latTfi.rearrange("c p s -> p c s")))
            st.add(lambda e_: e_.dma_start(out=msk[:, :].rearrange("p (g j q) -> p g j q", g=8, j=4),
                                           in_=mci.rearrange("g j p q -> p g j q")))
            st.add(lambda e_: e_.dma_start(out=sel_lo[:, :], in_=sli))
            st.add(lambda e_: e_.dma_start(out=sel_hi[:, :], in_=shi))
        with P.step("pool", writes=[wuk_sb], dma=True) as st:
            st.add(lambda e_: e_.dma_start(out=wuk_sb[:, :].rearrange("p (c n) -> p c n", c=4),
                                           in_=wuk.rearrange("(c p) n -> p c n", c=4)))
        with P.step("pool", writes=[wuv_sb], dma=True) as st:
            st.add(lambda e_: e_.dma_start(out=wuv_sb[:, :].rearrange("p (c n) -> p c n", c=4),
                                           in_=wuv.rearrange("(c p) n -> p c n", c=4)))
        with P.step("pool", writes=[wom_sb], dma=True) as st:
            st.add(lambda e_: e_.dma_start(out=wom_sb[:, :].rearrange("p (c n) -> p c n", c=DC),
                                           in_=wom.rearrange("(c p) n -> p c n", c=DC)))
        with P.step("dve", writes=[cst]) as st:
            st.add(lambda e_: e_.memset(cst[:, 3:4], 0.0))

        pA, pB = bks["pA"], bks["pB"]
        for ps in range(2):
            with P.step("sp", writes=[qTh], dma=True) as st:
                st.add(lambda e_, p_=ps: e_.dma_start(
                    out=qTh[:, :].rearrange("p (c t) -> p c t", c=4),
                    in_=qmTi[p_ * 4:(p_ + 1) * 4].rearrange("c p t -> p c t")))
            for kc in range(4):
                for r in range(8):
                    pk = pA if r % 2 == 0 else pB
                    with P.step("pe", reads=[latTf, wuk_sb], writes=[pk]) as st:
                        for lc in range(4):
                            st.add(lambda e_, l=lc, k_=kc, r_=r, p_=ps, pk_=pk: e_.matmul(
                                pk_[:, :], wuk_sb[:, l * 1024 + (p_ * 4 + k_) * 128: l * 1024 + (p_ * 4 + k_) * 128 + 128],
                                latTf[:, l * S + r_ * 512: l * S + r_ * 512 + 512],
                                start=(l == 0), stop=(l == 3)))
                    with P.step("dve", reads=[pk], writes=[Kh]) as st:
                        st.add(lambda e_, k_=kc, r_=r, pk_=pk: e_.tensor_copy(
                            Kh[:, k_ * S + r_ * 512: k_ * S + r_ * 512 + 512], pk_[:, :]))
            for c in range(KC):
                pk = pA if c % 2 == 0 else pB
                with P.step("pe", reads=[latTf, wuv_sb], writes=[pk]) as st:
                    for lc in range(4):
                        st.add(lambda e_, l=lc, c_=c, p_=ps, pk_=pk: e_.matmul(
                            pk_[:, :], latTf[:, l * S + c_ * 128: l * S + c_ * 128 + 128],
                            wuv_sb[:, l * 1024 + p_ * 512: l * 1024 + p_ * 512 + 512],
                            start=(l == 0), stop=(l == 3)))
                with P.step("dve", reads=[pk], writes=[Vh]) as st:
                    st.add(lambda e_, c_=c, pk_=pk: e_.tensor_copy(
                        bass.AP(Vh.t, c_ * 520, [[KC * 520, 128], [65, 8], [1, 64]]),
                        pk_[:, :].rearrange("p (h f) -> p h f", h=8)))
            with P.step("dve", writes=[Vh]) as st:
                st.add(lambda e_: e_.memset(
                    bass.AP(Vh.t, 64, [[KC * 520, 128], [520, KC], [65, 8], [1, 1]]), 1.0))

            def emit_resid_stream(P_, oc, g, pP, ps_=ps):
                slab = hsA if (oc + g) % 2 == 0 else hsB
                src = hin if ps_ == 0 else hout
                with P_.step("sp", reads=[src], writes=[slab], dma=True) as st:
                    st.add(lambda e_, o_=oc, g_=g, s_=src: e_.dma_start(
                        out=slab[:, :], in_=s_.t[o_, :, g_ * 256:(g_ + 1) * 256]))
                with P_.step("dve", reads=[pP, slab], writes=[slab]) as st:
                    st.add(lambda e_, pP_=pP, sl_=slab: e_.tensor_add(
                        sl_[:, :], pP_[:, 0:256], sl_[:, :]))
                with P_.step("sp", reads=[slab], writes=[hout], dma=True) as st:
                    st.add(lambda e_, o_=oc, g_=g, sl_=slab: e_.dma_start(
                        out=ho1[o_, :, g_ * 256:(g_ + 1) * 256], in_=sl_[:, :]))

            cfg = {
                "heads": list(range(ps * 8, ps * 8 + 8)),
                "qslice": lambda h, g, p_=ps: qTh[(h % 2) * 64:(h % 2) * 64 + 64,
                                                  ((h - 8 * p_) // 2) * T + g * 256:((h - 8 * p_) // 2) * T + g * 256 + 256],
                "kslice": lambda h, c, p_=ps: Kh[(h % 2) * 64:(h % 2) * 64 + 64,
                                                 ((h - 8 * p_) // 2) * S + c * 128:((h - 8 * p_) // 2) * S + c * 128 + 128],
                "vslice": lambda h, c, p_=ps: Vh[:, c * 520 + (h - 8 * p_) * 65: c * 520 + (h - 8 * p_) * 65 + 65],
                "mask": msk,
                "mslice": lambda g, j: msk[:, (g * 4 + j) * 256:(g * 4 + j + 1) * 256],
                "reads_qkv": [qTh, Kh, Vh],
                "banks": bks,
                "scratch": {"aA": aA, "aB": aB, "den": den, "rden": rden, "outT": outT,
                            "rb": rb, "cst": cst},
                "sel_lo": sel_lo, "sel_hi": sel_hi,
                "wo_sb": wom_sb,
                "emit_resid": emit_resid_stream,
                "scale": 0.125,
                "nch_of_g": lambda g: 4 * g + 4,
                "chunk_of": lambda g, j: j,
                "proj_chunks": [(lc, ps * 4 + lc) for lc in range(4)],
            }
            emit_attention_causal(P, cfg)

        P.final_barrier()
        blk = e(nc.Block())
        P.emit_into_block(blk)
    return nc


def build_launch_E():
    nc = bass.Bass("TRN2", target_bir_lowering=False, debug=False)
    h1i = nc.dram_tensor("h1i", [DC, 128, T], F32, kind="ExternalInput").ap()
    x1i = nc.dram_tensor("x1i", [DC, 128, T], F32, kind="ExternalInput").ap()
    g2i = nc.dram_tensor("g2i", [1024], F32, kind="ExternalInput").ap()
    gwi = nc.dram_tensor("gwi", [1024, 8], F32, kind="ExternalInput").ap()
    w1 = nc.dram_tensor("w1", [8, 1024, 2048], F32, kind="ExternalInput").ap()
    w2 = nc.dram_tensor("w2", [8, 2048, 1024], F32, kind="ExternalInput").ap()
    idi = nc.dram_tensor("idi", [128, 128], BF16, kind="ExternalInput").ap()
    onci = nc.dram_tensor("onci", [128, 1], BF16, kind="ExternalInput").ap()
    onri = nc.dram_tensor("onri", [1, 128], BF16, kind="ExternalInput").ap()
    x2o = nc.dram_tensor("x2o", [DC, 128, T], F32, kind="ExternalOutput").ap()

    from contextlib import ExitStack
    with ExitStack() as ctx:
        e = ctx.enter_context
        P = Prog(nc)
        P.sem_handles(ctx)
        xr = Buf(e(nc.sbuf_tensor("xr", [128, DC * T], F32)), "xr")
        xn = Buf(e(nc.sbuf_tensor("xn", [128, DC * T], BF16)), "xn")
        gw_sb = Buf(e(nc.sbuf_tensor("gw_sb", [128, DC * 8], BF16)), "gw_sb")
        g2_sb = Buf(e(nc.sbuf_tensor("g2_sb", [128, DC], F32)), "g2_sb")
        ident = Buf(e(nc.sbuf_tensor("ident", [128, 128], BF16)), "ident")
        w1A = Buf(e(nc.sbuf_tensor("w1A", [128, 8 * 1024], BF16)), "w1A")
        w1B = Buf(e(nc.sbuf_tensor("w1B", [128, 8 * 1024], BF16)), "w1B")
        w2A = Buf(e(nc.sbuf_tensor("w2A", [128, 8 * 1024], BF16)), "w2A")
        w2B = Buf(e(nc.sbuf_tensor("w2B", [128, 8 * 1024], BF16)), "w2B")
        h1T = Buf(e(nc.sbuf_tensor("h1T", [128, 8 * 512], BF16)), "h1T")
        lg = Buf(e(nc.sbuf_tensor("lg", [128, 8], F32)), "lg")
        m1 = Buf(e(nc.sbuf_tensor("m1", [128, 1], F32)), "m1")
        m2 = Buf(e(nc.sbuf_tensor("m2", [128, 1], F32)), "m2")
        eqm = Buf(e(nc.sbuf_tensor("eqm", [128, 8], F32)), "eqm")
        mskd = Buf(e(nc.sbuf_tensor("mskd", [128, 8], F32)), "mskd")
        E8 = Buf(e(nc.sbuf_tensor("E8", [128, 8], F32)), "E8")
        mm12 = Buf(e(nc.sbuf_tensor("mm12", [128, 2], F32)), "mm12")
        rsum = Buf(e(nc.sbuf_tensor("rsum", [128, 1], F32)), "rsum")
        comb = Buf(e(nc.sbuf_tensor("comb", [128, NT * 8], BF16)), "comb")
        combT = Buf(e(nc.sbuf_tensor("combT", [128, T], BF16)), "combT")
        ybf = Buf(e(nc.sbuf_tensor("ybf", [128, 512], BF16)), "ybf")
        ytmp = Buf(e(nc.sbuf_tensor("ytmp", [128, 512], BF16)), "ytmp")
        slab = Buf(e(nc.sbuf_tensor("slab", [128, 512], F32)), "slab")
        slab2 = Buf(e(nc.sbuf_tensor("slab2", [128, 512], F32)), "slab2")
        onc = Buf(e(nc.sbuf_tensor("onc", [128, 1], BF16)), "onc")
        onr = Buf(e(nc.sbuf_tensor("onr", [1, 128], BF16)), "onr")
        pQ = Buf(e(nc.psum_tensor("pQ", [128, 512], F32)), "pQ")
        pY = Buf(e(nc.psum_tensor("pY", [128, 512], F32)), "pY")
        pG = Buf(e(nc.psum_tensor("pG", [128, 512], F32)), "pG")
        nsc = alloc_norm_scratch(nc, ctx, P)
        consts = {"ones_col": onc, "ones1r": onr}

        with P.step("sp", writes=[xr], dma=True) as st:
            st.add(lambda e_: e_.dma_start(out=xr[:, :].rearrange("p (c t) -> p c t", c=DC),
                                           in_=h1i.rearrange("c p t -> p c t")))
        with P.step("sp", writes=[onc, onr, g2_sb, ident], dma=True) as st:
            st.add(lambda e_: e_.dma_start(out=onc[:, :], in_=onci))
            st.add(lambda e_: e_.dma_start(out=onr[:, :], in_=onri))
            st.add(lambda e_: e_.dma_start(out=ident[:, :], in_=idi))
            for ch in range(DC):
                st.add(lambda e_, c=ch: e_.dma_start(
                    out=g2_sb[:, c:c + 1],
                    in_=g2i[c * 128:(c + 1) * 128].rearrange("(p o) -> p o", o=1)))
        with P.step("pool", writes=[gw_sb], dma=True) as st:
            st.add(lambda e_: e_.dma_start(out=gw_sb[:, :].rearrange("p (c n) -> p c n", c=DC),
                                           in_=gwi.rearrange("(c p) n -> p c n", c=DC)))

        emit_norm(P, ctx, xr, xn, "rms", consts, 1e-6, gain=g2_sb, scratch=nsc)

        for i in range(NT):
            with P.step("pe", reads=[xn, gw_sb], writes=[pG]) as st:
                for ch in range(DC):
                    st.add(lambda e_, c=ch, ii=i: e_.matmul(
                        pG[:, 0:8], xn[:, c * T + ii * 128: c * T + ii * 128 + 128],
                        gw_sb[:, c * 8:(c + 1) * 8],
                        start=(c == 0), stop=(c == DC - 1)))
            with P.step("dve", reads=[pG], writes=[lg]) as st:
                st.add(lambda e_: e_.tensor_copy(lg[:, :], pG[:, 0:8]))
            with P.step("dve", reads=[lg], writes=[m1]) as st:
                st.add(lambda e_: e_.reduce_max(m1[:, :], lg[:, :], axis=AX.X))
            with P.step("dve", reads=[lg, m1], writes=[eqm]) as st:
                st.add(lambda e_: e_.tensor_scalar(eqm[:, :], lg[:, :], m1[:, :], None, ALU.is_ge))
            with P.step("dve", reads=[lg, eqm], writes=[mskd]) as st:
                st.add(lambda e_: e_.scalar_tensor_tensor(mskd[:, :], eqm[:, :], -1e30, lg[:, :],
                                                          ALU.mult, ALU.add))
            with P.step("dve", reads=[mskd], writes=[m2]) as st:
                st.add(lambda e_: e_.reduce_max(m2[:, :], mskd[:, :], axis=AX.X))
            with P.step("act", reads=[lg], writes=[E8]) as st:
                st.add(lambda e_: e_.activation(E8[:, :], lg[:, :], AF.Exp))
            with P.step("dve", reads=[m1], writes=[mm12]) as st:
                st.add(lambda e_: e_.tensor_copy(mm12[:, 0:1], m1[:, :]))
            with P.step("dve", reads=[m2], writes=[mm12]) as st:
                st.add(lambda e_: e_.tensor_copy(mm12[:, 1:2], m2[:, :]))
            with P.step("act", reads=[mm12], writes=[mm12]) as st:
                st.add(lambda e_: e_.activation(mm12[:, :], mm12[:, :], AF.Exp))
            with P.step("dve", reads=[mm12], writes=[rsum]) as st:
                st.add(lambda e_: e_.reduce_sum(rsum[:, :], mm12[:, :], axis=AX.X))
            with P.step("dve", reads=[rsum], writes=[rsum]) as st:
                st.add(lambda e_: e_.reciprocal(rsum[:, :], rsum[:, :]))
            with P.step("dve", reads=[lg, m2], writes=[eqm]) as st:
                st.add(lambda e_: e_.tensor_scalar(eqm[:, :], lg[:, :], m2[:, :], None, ALU.is_ge))
            with P.step("dve", reads=[E8, eqm], writes=[E8]) as st:
                st.add(lambda e_: e_.tensor_mul(E8[:, :], E8[:, :], eqm[:, :]))
            with P.step("dve", reads=[E8, rsum, comb], writes=[comb]) as st:
                st.add(lambda e_, ii=i: e_.tensor_scalar(comb[:, ii * 8:(ii + 1) * 8],
                                                         E8[:, :], rsum[:, :], None, ALU.mult))

        wbufs = [(w1A, w2A), (w1B, w2B)]

        def emit_wload(idx):
            ee, half = divmod(idx, 2)
            w1b, w2b = wbufs[idx % 2]
            with P.step("pool", writes=[w1b], dma=True) as st:
                st.add(lambda e_, E_=ee, H_=half: e_.dma_start(
                    out=w1b[:, :].rearrange("p (c n) -> p c n", c=DC),
                    in_=w1[E_, :, H_ * 1024:(H_ + 1) * 1024].rearrange("(c p) n -> p c n", c=DC)))
            with P.step("pool", writes=[w2b], dma=True) as st:
                st.add(lambda e_, E_=ee, H_=half: e_.dma_start(
                    out=w2b[:, :].rearrange("p (c n) -> p c n", c=DC),
                    in_=w2[E_, H_ * 1024:(H_ + 1) * 1024, :].rearrange("(c p) n -> p c n", c=DC)))

        emit_wload(0)
        for idx in range(16):
            ee, half = divmod(idx, 2)
            w1b, w2b = wbufs[idx % 2]
            if idx + 1 < 16:
                emit_wload(idx + 1)
            if half == 0:
                for i in range(NT):
                    with P.step("pe", reads=[comb, ident], writes=[pG]) as st:
                        st.add(lambda e_, ii=i, E_=ee: e_.matmul(
                            pG[:, 0:128], comb[:, ii * 8 + E_: ii * 8 + E_ + 1].broadcast_to([128, 128]),
                            ident[:, :], start=True, stop=True))
                    with P.step("dve", reads=[pG], writes=[combT]) as st:
                        st.add(lambda e_, ii=i: e_.tensor_copy(
                            combT[:, ii * 128:(ii + 1) * 128], pG[:, 0:128]))
            for tr in range(4):
                for ec in range(8):
                    with P.step("pe", reads=[xn, w1b], writes=[pQ]) as st:
                        for dc in range(DC):
                            st.add(lambda e_, d=dc, E2=ec, t_=tr, wb=w1b: e_.matmul(
                                pQ[:, :], wb[:, d * 1024 + E2 * 128: d * 1024 + E2 * 128 + 128],
                                xn[:, d * T + t_ * 512: d * T + t_ * 512 + 512],
                                start=(d == 0), stop=(d == DC - 1)))
                    with P.step("act", reads=[pQ], writes=[h1T]) as st:
                        st.add(lambda e_, E2=ec: e_.activation(
                            h1T[:, E2 * 512:(E2 + 1) * 512], pQ[:, :], AF.Silu))
                for oc in range(DC):
                    with P.step("pe", reads=[h1T, w2b], writes=[pY]) as st:
                        for ec in range(8):
                            st.add(lambda e_, E2=ec, o_=oc, wb=w2b: e_.matmul(
                                pY[:, :], wb[:, E2 * 1024 + o_ * 128: E2 * 1024 + o_ * 128 + 128],
                                h1T[:, E2 * 512:(E2 + 1) * 512],
                                start=(E2 == 0), stop=(E2 == 7)))
                    with P.step("dve", reads=[pY], writes=[ybf]) as st:
                        st.add(lambda e_: e_.tensor_copy(ybf[:, :], pY[:, :]))
                    with P.step("pool", reads=[ybf, combT], writes=[ytmp]) as st:
                        st.add(lambda e_, t_=tr: e_.tensor_mul(
                            ytmp[:, :], ybf[:, :], combT[:, t_ * 512:(t_ + 1) * 512]))
                    with P.step("pool", reads=[ytmp, xr], writes=[xr]) as st:
                        st.add(lambda e_, o_=oc, t_=tr: e_.tensor_add(
                            xr[:, o_ * T + t_ * 512: o_ * T + t_ * 512 + 512], ytmp[:, :],
                            xr[:, o_ * T + t_ * 512: o_ * T + t_ * 512 + 512]))

        for oc in range(DC):
            for tr in range(4):
                sl = slab if (oc * 4 + tr) % 2 == 0 else slab2
                with P.step("sp", writes=[sl], dma=True) as st:
                    st.add(lambda e_, o_=oc, t_=tr, s_=sl: e_.dma_start(
                        out=s_[:, :], in_=x1i[o_, :, t_ * 512:(t_ + 1) * 512]))
                with P.step("dve", reads=[xr, sl], writes=[sl]) as st:
                    st.add(lambda e_, o_=oc, t_=tr, s_=sl: e_.tensor_add(
                        s_[:, :], xr[:, o_ * T + t_ * 512: o_ * T + t_ * 512 + 512], s_[:, :]))
                with P.step("sp", reads=[sl], dma=True) as st:
                    st.add(lambda e_, o_=oc, t_=tr, s_=sl: e_.dma_start(
                        out=x2o[o_, :, t_ * 512:(t_ + 1) * 512], in_=s_[:, :]))
        P.final_barrier()
        blk = e(nc.Block())
        P.emit_into_block(blk)
    return nc


L1_HEADS = [0, 4, 1, 5, 2, 6, 3, 7, 8, 12, 9, 13, 10, 14, 11, 15]


def emit_qperm_evac(P, pQ, q_tm, nr):
    with P.step("dve", reads=[pQ], writes=[q_tm]) as st:
        st.add(lambda e_, n=nr: e_.tensor_copy(
            bass.AP(q_tm.t, n * 512, [[q_tm.t.shape[1], 128], [64, 2], [128, 4], [1, 64]]),
            pQ[:, :].rearrange("p (q j f) -> p q j f", q=2, j=4)))


def emit_wo_perm_load(P, eng_step_kwargs, wo_sb, wo_dram):
    with P.step("pool", writes=[wo_sb], dma=True) as st:
        for m in range(8):
            h0 = H0[m]
            h64 = H64[m]
            st.add(lambda e_, m_=m, h_=h0: e_.dma_start(
                out=wo_sb.t[0:64, m_ * 1024:(m_ + 1) * 1024],
                in_=wo_dram[h_ * 64:(h_ + 1) * 64, :]))
            st.add(lambda e_, m_=m, h_=h64: e_.dma_start(
                out=wo_sb.t[64:128, m_ * 1024:(m_ + 1) * 1024],
                in_=wo_dram[h_ * 64:(h_ + 1) * 64, :]))


def build_launch_F():
    nc = bass.Bass("TRN2", target_bir_lowering=False, debug=False)
    x2i = nc.dram_tensor("x2i", [DC, 128, T], F32, kind="ExternalInput").ap()
    n1wi = nc.dram_tensor("n1wi", [1024], F32, kind="ExternalInput").ap()
    gwq = nc.dram_tensor("gwq", [1024, 1024], F32, kind="ExternalInput").ap()
    gkvd = nc.dram_tensor("gkvd", [1024, 256], F32, kind="ExternalInput").ap()
    Crgi = nc.dram_tensor("Crgi", [NT, 128, 64], BF16, kind="ExternalInput").ap()
    Srgi = nc.dram_tensor("Srgi", [NT, 128, 64], BF16, kind="ExternalInput").ap()
    onci = nc.dram_tensor("onci", [128, 1], BF16, kind="ExternalInput").ap()
    onri = nc.dram_tensor("onri", [1, 128], BF16, kind="ExternalInput").ap()
    hBo = nc.dram_tensor("hBo", [DC, 128, T], F32, kind="ExternalOutput").ap()
    qgo = nc.dram_tensor("qgo", [NT, 128, 1024], BF16, kind="ExternalOutput").ap()
    latgo = nc.dram_tensor("latgo", [2, 128, T], BF16, kind="ExternalOutput").ap()

    from contextlib import ExitStack
    with ExitStack() as ctx:
        e = ctx.enter_context
        P = Prog(nc)
        P.sem_handles(ctx)
        xr = Buf(e(nc.sbuf_tensor("xr", [128, DC * T], F32)), "xr")
        xn = Buf(e(nc.sbuf_tensor("xn", [128, DC * T], BF16)), "xn")
        wq_sb = Buf(e(nc.sbuf_tensor("wq_sb", [128, DC * 1024], BF16)), "wq_sb")
        wkv_sb = Buf(e(nc.sbuf_tensor("wkv_sb", [128, DC * 256], BF16)), "wkv_sb")
        nw_sb = Buf(e(nc.sbuf_tensor("nw_sb", [128, DC], F32)), "nw_sb")
        q_tm = Buf(e(nc.sbuf_tensor("q_tm", [128, 1024], BF16)), "q_tm")
        lat_sb = Buf(e(nc.sbuf_tensor("lat_sb", [128, 2 * T], BF16)), "lat_sb")
        r1 = Buf(e(nc.sbuf_tensor("r1", [128, 1024], BF16)), "r1")
        r2 = Buf(e(nc.sbuf_tensor("r2", [128, 1024], BF16)), "r2")
        ss = Buf(e(nc.sbuf_tensor("ss", [128, 16], F32)), "ss")
        tC = Buf(e(nc.sbuf_tensor("tC", [128, NT * 64], BF16)), "tC")
        tS = Buf(e(nc.sbuf_tensor("tS", [128, NT * 64], BF16)), "tS")
        onc = Buf(e(nc.sbuf_tensor("onc", [128, 1], BF16)), "onc")
        onr = Buf(e(nc.sbuf_tensor("onr", [1, 128], BF16)), "onr")
        pQ = Buf(e(nc.psum_tensor("pQ", [128, 512], F32)), "pQ")
        nsc = alloc_norm_scratch(nc, ctx, P)
        consts = {"ones_col": onc, "ones1r": onr}
        rsc = {"r1": r1, "r2": r2, "ss": ss, "cst": nsc["cst"]}

        with P.step("sp", writes=[xr], dma=True) as st:
            st.add(lambda e_: e_.dma_start(out=xr[:, :].rearrange("p (c t) -> p c t", c=DC),
                                           in_=x2i.rearrange("c p t -> p c t")))
        with P.step("sp", writes=[onc, onr, nw_sb, tC, tS], dma=True) as st:
            st.add(lambda e_: e_.dma_start(out=onc[:, :], in_=onci))
            st.add(lambda e_: e_.dma_start(out=onr[:, :], in_=onri))
            st.add(lambda e_: e_.dma_start(out=tC[:, :].rearrange("p (i f) -> p i f", i=NT),
                                           in_=Crgi.rearrange("i p f -> p i f")))
            st.add(lambda e_: e_.dma_start(out=tS[:, :].rearrange("p (i f) -> p i f", i=NT),
                                           in_=Srgi.rearrange("i p f -> p i f")))
            for ch in range(DC):
                st.add(lambda e_, c=ch: e_.dma_start(
                    out=nw_sb[:, c:c + 1],
                    in_=n1wi[c * 128:(c + 1) * 128].rearrange("(p o) -> p o", o=1)))
        with P.step("pool", writes=[wq_sb], dma=True) as st:
            st.add(lambda e_: e_.dma_start(out=wq_sb[:, :].rearrange("p (c n) -> p c n", c=DC),
                                           in_=gwq.rearrange("(c p) n -> p c n", c=DC)))
        with P.step("pool", writes=[wkv_sb], dma=True) as st:
            st.add(lambda e_: e_.dma_start(out=wkv_sb[:, :].rearrange("p (c n) -> p c n", c=DC),
                                           in_=gkvd.rearrange("(c p) n -> p c n", c=DC)))

        emit_norm(P, ctx, xr, xn, "ln32", consts, 1e-5, scratch=nsc)
        with P.step("sp", reads=[xr], dma=True) as st:
            st.add(lambda e_: e_.dma_start(out=hBo.rearrange("c p t -> p c t"),
                                           in_=xr[:, :].rearrange("p (c t) -> p c t", c=DC)))
        emit_norm(P, ctx, xr, xn, "rms", consts, 1e-6, gain=nw_sb, scratch=nsc)

        for i in range(NT):
            for nr in range(2):
                with P.step("pe", reads=[xn, wq_sb], writes=[pQ]) as st:
                    for ch in range(DC):
                        st.add(lambda e_, c=ch, n=nr, ii=i: e_.matmul(
                            pQ[:, :], xn[:, c * T + ii * 128: c * T + ii * 128 + 128],
                            wq_sb[:, c * 1024 + n * 512: c * 1024 + n * 512 + 512],
                            start=(c == 0), stop=(c == DC - 1)))
                emit_qperm_evac(P, pQ, q_tm, nr)
            emit_rope(P, q_tm, tC, tS, i * 64, 16, "inter", rsc)
            emit_qknorm(P, q_tm, 16, rsc)
            with P.step("sp", reads=[q_tm], dma=True) as st:
                st.add(lambda e_, ii=i: e_.dma_start(out=qgo[ii], in_=q_tm[:, :]))
        for lc in range(2):
            for r in range(4):
                with P.step("pe", reads=[xn, wkv_sb], writes=[pQ]) as st:
                    for ch in range(DC):
                        st.add(lambda e_, c=ch, l=lc, r_=r: e_.matmul(
                            pQ[:, :], wkv_sb[:, c * 256 + l * 128: c * 256 + l * 128 + 128],
                            xn[:, c * T + r_ * 512: c * T + r_ * 512 + 512],
                            start=(c == 0), stop=(c == DC - 1)))
                with P.step("dve", reads=[pQ], writes=[lat_sb]) as st:
                    st.add(lambda e_, l=lc, r_=r: e_.tensor_copy(
                        lat_sb[:, l * T + r_ * 512: l * T + r_ * 512 + 512], pQ[:, :]))
        with P.step("sp", reads=[lat_sb], dma=True) as st:
            st.add(lambda e_: e_.dma_start(out=latgo.rearrange("c p t -> p c t"),
                                           in_=lat_sb[:, :].rearrange("p (c t) -> p c t", c=2)))
        P.final_barrier()
        blk = e(nc.Block())
        P.emit_into_block(blk)
    return nc


def build_launch_G():
    nc = bass.Bass("TRN2", target_bir_lowering=False, debug=False)
    hBi = nc.dram_tensor("hBi", [DC, 128, T], F32, kind="ExternalInput").ap()
    qgTi = nc.dram_tensor("qgTi", [DC, 128, T], BF16, kind="ExternalInput").ap()
    latgfi = nc.dram_tensor("latgfi", [2, 128, S], BF16, kind="ExternalInput").ap()
    gkup = nc.dram_tensor("gkup", [256, 256], F32, kind="ExternalInput").ap()
    gvup = nc.dram_tensor("gvup", [256, 256], F32, kind="ExternalInput").ap()
    gwo = nc.dram_tensor("gwo", [1024, 1024], F32, kind="ExternalInput").ap()
    mci = nc.dram_tensor("mci", [8, 4, 128, 256], BF16, kind="ExternalInput").ap()
    Crgfi = nc.dram_tensor("Crgfi", [KC, 128, 64], BF16, kind="ExternalInput").ap()
    Srgfi = nc.dram_tensor("Srgfi", [KC, 128, 64], BF16, kind="ExternalInput").ap()
    idi = nc.dram_tensor("idi", [128, 128], BF16, kind="ExternalInput").ap()
    sli = nc.dram_tensor("sli", [1, 128], BF16, kind="ExternalInput").ap()
    shi = nc.dram_tensor("shi", [1, 128], BF16, kind="ExternalInput").ap()
    hB1o = nc.dram_tensor("hB1o", [DC, 128, T], F32, kind="ExternalOutput").ap()

    from contextlib import ExitStack
    with ExitStack() as ctx:
        e = ctx.enter_context
        P = Prog(nc)
        P.sem_handles(ctx)
        xr = Buf(e(nc.sbuf_tensor("xr", [128, DC * T], F32)), "xr")
        qT = Buf(e(nc.sbuf_tensor("qT", [128, DC * T], BF16)), "qT")
        latgf = Buf(e(nc.sbuf_tensor("latgf", [128, 2 * S], BF16)), "latgf")
        kup_sb = Buf(e(nc.sbuf_tensor("kup_sb", [128, 2 * 256], BF16)), "kup_sb")
        vup_sb = Buf(e(nc.sbuf_tensor("vup_sb", [128, 2 * 256], BF16)), "vup_sb")
        wo_sb = Buf(e(nc.sbuf_tensor("wo_sb", [128, DC * 1024], BF16)), "wo_sb")
        Kfm = Buf(e(nc.sbuf_tensor("Kfm", [128, 2 * S], BF16)), "Kfm")
        Vp = Buf(e(nc.sbuf_tensor("Vp", [128, KC * 260], BF16)), "Vp")
        msk = Buf(e(nc.sbuf_tensor("msk", [128, 8 * 4 * 256], BF16)), "msk")
        tCf = Buf(e(nc.sbuf_tensor("tCf", [128, KC * 64], BF16)), "tCf")
        tSf = Buf(e(nc.sbuf_tensor("tSf", [128, KC * 64], BF16)), "tSf")
        ident = Buf(e(nc.sbuf_tensor("ident", [128, 128], BF16)), "ident")
        k_tm = Buf(e(nc.sbuf_tensor("k_tm", [128, 256], BF16)), "k_tm")
        v_tm = Buf(e(nc.sbuf_tensor("v_tm", [128, 256], BF16)), "v_tm")
        r1 = Buf(e(nc.sbuf_tensor("r1", [128, 1024], BF16)), "r1")
        r2 = Buf(e(nc.sbuf_tensor("r2", [128, 1024], BF16)), "r2")
        ss = Buf(e(nc.sbuf_tensor("ss", [128, 16], F32)), "ss")
        sel_lo = Buf(e(nc.sbuf_tensor("sel_lo", [1, 128], BF16)), "sel_lo")
        sel_hi = Buf(e(nc.sbuf_tensor("sel_hi", [1, 128], BF16)), "sel_hi")
        aA = Buf(e(nc.sbuf_tensor("aA", [128, 256], BF16)), "aA")
        aB = Buf(e(nc.sbuf_tensor("aB", [128, 256], BF16)), "aB")
        den = Buf(e(nc.sbuf_tensor("den", [1, 512], F32)), "den")
        rden = Buf(e(nc.sbuf_tensor("rden", [1, 512], BF16)), "rden")
        outT = Buf(e(nc.sbuf_tensor("outT", [128, 2048], BF16)), "outT")
        outTn = Buf(e(nc.sbuf_tensor("outTn", [128, 2048], BF16)), "outTn")
        rb = Buf(e(nc.sbuf_tensor("rb", [128, 256], BF16)), "rb")
        cst = Buf(e(nc.sbuf_tensor("cst", [128, 4], F32)), "cst")
        bks = {}
        for n in ("pA", "pB", "pC", "pD", "pE"):
            bks[n] = Buf(e(nc.psum_tensor(n, [128, 512], F32)), n)
        pT = Buf(e(nc.psum_tensor("pT", [128, 1024], BF16)), "pT")
        rsc = {"r1": r1, "r2": r2, "ss": ss, "cst": cst}

        with P.step("sp", writes=[xr], dma=True) as st:
            st.add(lambda e_: e_.dma_start(out=xr[:, :].rearrange("p (c t) -> p c t", c=DC),
                                           in_=hBi.rearrange("c p t -> p c t")))
        with P.step("sp", writes=[qT, latgf, msk, tCf, tSf, ident, sel_lo, sel_hi], dma=True) as st:
            st.add(lambda e_: e_.dma_start(out=qT[:, :].rearrange("p (c t) -> p c t", c=DC),
                                           in_=qgTi.rearrange("c p t -> p c t")))
            st.add(lambda e_: e_.dma_start(out=latgf[:, :].rearrange("p (c s) -> p c s", c=2),
                                           in_=latgfi.rearrange("c p s -> p c s")))
            st.add(lambda e_: e_.dma_start(out=msk[:, :].rearrange("p (g j q) -> p g j q", g=8, j=4),
                                           in_=mci.rearrange("g j p q -> p g j q")))
            st.add(lambda e_: e_.dma_start(out=tCf[:, :].rearrange("p (i f) -> p i f", i=KC),
                                           in_=Crgfi.rearrange("i p f -> p i f")))
            st.add(lambda e_: e_.dma_start(out=tSf[:, :].rearrange("p (i f) -> p i f", i=KC),
                                           in_=Srgfi.rearrange("i p f -> p i f")))
            st.add(lambda e_: e_.dma_start(out=ident[:, :], in_=idi))
            st.add(lambda e_: e_.dma_start(out=sel_lo[:, :], in_=sli))
            st.add(lambda e_: e_.dma_start(out=sel_hi[:, :], in_=shi))
        with P.step("pool", writes=[kup_sb], dma=True) as st:
            st.add(lambda e_: e_.dma_start(out=kup_sb[:, :].rearrange("p (c n) -> p c n", c=2),
                                           in_=gkup.rearrange("(c p) n -> p c n", c=2)))
        with P.step("pool", writes=[vup_sb], dma=True) as st:
            st.add(lambda e_: e_.dma_start(out=vup_sb[:, :].rearrange("p (c n) -> p c n", c=2),
                                           in_=gvup.rearrange("(c p) n -> p c n", c=2)))
        emit_wo_perm_load(P, None, wo_sb, gwo)
        with P.step("dve", writes=[cst]) as st:
            st.add(lambda e_: e_.memset(cst[:, 3:4], 0.0))

        pA = bks["pA"]
        for c in range(KC):
            with P.step("pe", reads=[latgf, kup_sb], writes=[pA]) as st:
                for lc in range(2):
                    st.add(lambda e_, l=lc, c_=c: e_.matmul(
                        pA[:, 0:256], latgf[:, l * S + c_ * 128: l * S + c_ * 128 + 128],
                        kup_sb[:, l * 256:(l + 1) * 256],
                        start=(l == 0), stop=(l == 1)))
            with P.step("dve", reads=[pA], writes=[k_tm]) as st:
                st.add(lambda e_: e_.tensor_copy(k_tm[:, :], pA[:, 0:256]))
            emit_rope(P, k_tm, tCf, tSf, c * 64, 4, "inter", rsc)
            emit_qknorm(P, k_tm, 4, rsc)
            for half in range(2):
                with P.step("pe", reads=[k_tm, ident], writes=[pT]) as st:
                    st.add(lambda e_, h_=half: e_.transpose(
                        pT[:, 0:128], k_tm[:, h_ * 128:(h_ + 1) * 128], ident[:, :]))
                with P.step("dve", reads=[pT], writes=[Kfm]) as st:
                    st.add(lambda e_, h_=half, c_=c: e_.tensor_copy(
                        Kfm[:, h_ * S + c_ * 128:h_ * S + c_ * 128 + 128], pT[:, 0:128]))
            with P.step("pe", reads=[latgf, vup_sb], writes=[pA]) as st:
                for lc in range(2):
                    st.add(lambda e_, l=lc, c_=c: e_.matmul(
                        pA[:, 0:256], latgf[:, l * S + c_ * 128: l * S + c_ * 128 + 128],
                        vup_sb[:, l * 256:(l + 1) * 256],
                        start=(l == 0), stop=(l == 1)))
            with P.step("dve", reads=[pA], writes=[Vp]) as st:
                st.add(lambda e_, c_=c: e_.tensor_copy(
                    bass.AP(Vp.t, c_ * 260, [[KC * 260, 128], [65, 4], [1, 64]]),
                    pA[:, 0:256].rearrange("p (k f) -> p k f", k=4)))
        with P.step("dve", writes=[Vp]) as st:
            st.add(lambda e_: e_.memset(
                bass.AP(Vp.t, 64, [[KC * 260, 128], [260, KC], [65, 4], [1, 1]]), 1.0))

        cfg = {
            "heads": list(L1_HEADS),
            "qslice": lambda h, g: qT[(gqa_pos(h) % 2) * 64:(gqa_pos(h) % 2) * 64 + 64,
                                      (gqa_pos(h) // 2) * T + g * 256:(gqa_pos(h) // 2) * T + g * 256 + 256],
            "kslice": lambda h, c: Kfm[((h // 4) % 2) * 64:((h // 4) % 2) * 64 + 64,
                                       ((h // 4) // 2) * S + c * 128:((h // 4) // 2) * S + c * 128 + 128],
            "vslice": lambda h, c: Vp[:, c * 260 + (h // 4) * 65: c * 260 + (h // 4) * 65 + 65],
            "mask": msk,
            "mslice": lambda g, j: msk[:, (g * 4 + j) * 256:(g * 4 + j + 1) * 256],
            "reads_qkv": [qT, Kfm, Vp],
            "banks": bks,
            "scratch": {"aA": aA, "aB": aB, "den": den, "rden": rden, "outT": outT,
                        "rb": rb, "cst": cst},
            "sel_lo": sel_lo, "sel_hi": sel_hi,
            "wo_sb": wo_sb,
            "emit_resid": (lambda P_, oc, g, pP: _resid_add(P_, xr, oc, g, pP)),
            "scale": 0.125,
            "nch_of_g": lambda g: 4 * g + 4,
            "chunk_of": lambda g, j: j,
        }
        emit_attention_causal(P, cfg)

        with P.step("sp", reads=[xr], dma=True) as st:
            st.add(lambda e_: e_.dma_start(out=hB1o.rearrange("c p t -> p c t"),
                                           in_=xr[:, :].rearrange("p (c t) -> p c t", c=DC)))
        P.final_barrier()
        blk = e(nc.Block())
        P.emit_into_block(blk)
    return nc


def build_launch_H():
    nc = bass.Bass("TRN2", target_bir_lowering=False, debug=False)
    hB1i = nc.dram_tensor("hB1i", [DC, 128, T], F32, kind="ExternalInput").ap()
    n2wi = nc.dram_tensor("n2wi", [1024], F32, kind="ExternalInput").ap()
    lwq = nc.dram_tensor("lwq", [1024, 1024], F32, kind="ExternalInput").ap()
    lwk = nc.dram_tensor("lwk", [1024, 256], F32, kind="ExternalInput").ap()
    lwv = nc.dram_tensor("lwv", [1024, 256], F32, kind="ExternalInput").ap()
    Crli = nc.dram_tensor("Crli", [NT, 128, 64], BF16, kind="ExternalInput").ap()
    Srli = nc.dram_tensor("Srli", [NT, 128, 64], BF16, kind="ExternalInput").ap()
    onci = nc.dram_tensor("onci", [128, 1], BF16, kind="ExternalInput").ap()
    onri = nc.dram_tensor("onri", [1, 128], BF16, kind="ExternalInput").ap()
    qlo = nc.dram_tensor("qlo", [NT, 128, 1024], BF16, kind="ExternalOutput").ap()
    klo = nc.dram_tensor("klo", [NT, 128, 256], BF16, kind="ExternalOutput").ap()
    vlo = nc.dram_tensor("vlo", [NT, 128, 256], BF16, kind="ExternalOutput").ap()

    from contextlib import ExitStack
    with ExitStack() as ctx:
        e = ctx.enter_context
        P = Prog(nc)
        P.sem_handles(ctx)
        xr = Buf(e(nc.sbuf_tensor("xr", [128, DC * T], F32)), "xr")
        xn = Buf(e(nc.sbuf_tensor("xn", [128, DC * T], BF16)), "xn")
        wq_sb = Buf(e(nc.sbuf_tensor("wq_sb", [128, DC * 1024], BF16)), "wq_sb")
        wkv_sb = Buf(e(nc.sbuf_tensor("wkv_sb", [128, DC * 512], BF16)), "wkv_sb")
        nw_sb = Buf(e(nc.sbuf_tensor("nw_sb", [128, DC], F32)), "nw_sb")
        q_tm = Buf(e(nc.sbuf_tensor("q_tm", [128, 1024], BF16)), "q_tm")
        k_tm = Buf(e(nc.sbuf_tensor("k_tm", [128, 256], BF16)), "k_tm")
        v_tm = Buf(e(nc.sbuf_tensor("v_tm", [128, 256], BF16)), "v_tm")
        r1 = Buf(e(nc.sbuf_tensor("r1", [128, 1024], BF16)), "r1")
        r2 = Buf(e(nc.sbuf_tensor("r2", [128, 1024], BF16)), "r2")
        ss = Buf(e(nc.sbuf_tensor("ss", [128, 16], F32)), "ss")
        tC = Buf(e(nc.sbuf_tensor("tC", [128, NT * 64], BF16)), "tC")
        tS = Buf(e(nc.sbuf_tensor("tS", [128, NT * 64], BF16)), "tS")
        onc = Buf(e(nc.sbuf_tensor("onc", [128, 1], BF16)), "onc")
        onr = Buf(e(nc.sbuf_tensor("onr", [1, 128], BF16)), "onr")
        pQ = Buf(e(nc.psum_tensor("pQ", [128, 512], F32)), "pQ")
        nsc = alloc_norm_scratch(nc, ctx, P)
        consts = {"ones_col": onc, "ones1r": onr}
        rsc = {"r1": r1, "r2": r2, "ss": ss, "cst": nsc["cst"]}

        with P.step("sp", writes=[xr], dma=True) as st:
            st.add(lambda e_: e_.dma_start(out=xr[:, :].rearrange("p (c t) -> p c t", c=DC),
                                           in_=hB1i.rearrange("c p t -> p c t")))
        with P.step("sp", writes=[onc, onr, nw_sb, tC, tS], dma=True) as st:
            st.add(lambda e_: e_.dma_start(out=onc[:, :], in_=onci))
            st.add(lambda e_: e_.dma_start(out=onr[:, :], in_=onri))
            st.add(lambda e_: e_.dma_start(out=tC[:, :].rearrange("p (i f) -> p i f", i=NT),
                                           in_=Crli.rearrange("i p f -> p i f")))
            st.add(lambda e_: e_.dma_start(out=tS[:, :].rearrange("p (i f) -> p i f", i=NT),
                                           in_=Srli.rearrange("i p f -> p i f")))
            for ch in range(DC):
                st.add(lambda e_, c=ch: e_.dma_start(
                    out=nw_sb[:, c:c + 1],
                    in_=n2wi[c * 128:(c + 1) * 128].rearrange("(p o) -> p o", o=1)))
        with P.step("pool", writes=[wq_sb], dma=True) as st:
            st.add(lambda e_: e_.dma_start(out=wq_sb[:, :].rearrange("p (c n) -> p c n", c=DC),
                                           in_=lwq.rearrange("(c p) n -> p c n", c=DC)))
        with P.step("pool", writes=[wkv_sb], dma=True) as st:
            st.add(lambda e_: e_.dma_start(
                out=bass.AP(wkv_sb.t, 0, [[DC * 512, 128], [512, DC], [1, 256]]),
                in_=lwk.rearrange("(c p) n -> p c n", c=DC)))
            st.add(lambda e_: e_.dma_start(
                out=bass.AP(wkv_sb.t, 256, [[DC * 512, 128], [512, DC], [1, 256]]),
                in_=lwv.rearrange("(c p) n -> p c n", c=DC)))

        emit_norm(P, ctx, xr, xn, "rms", consts, 1e-6, gain=nw_sb, scratch=nsc)

        for i in range(NT):
            for nr in range(3):
                with P.step("pe", reads=[xn, wq_sb, wkv_sb], writes=[pQ]) as st:
                    for ch in range(DC):
                        if nr < 2:
                            w_ap = wq_sb[:, ch * 1024 + nr * 512: ch * 1024 + nr * 512 + 512]
                        else:
                            w_ap = wkv_sb[:, ch * 512: ch * 512 + 512]
                        st.add(lambda e_, c=ch, w=w_ap, ii=i: e_.matmul(
                            pQ[:, :], xn[:, c * T + ii * 128: c * T + ii * 128 + 128], w,
                            start=(c == 0), stop=(c == DC - 1)))
                if nr < 2:
                    emit_qperm_evac(P, pQ, q_tm, nr)
                else:
                    with P.step("dve", reads=[pQ], writes=[k_tm, v_tm]) as st:
                        st.add(lambda e_: e_.tensor_copy(k_tm[:, :], pQ[:, 0:256]))
                        st.add(lambda e_: e_.tensor_copy(v_tm[:, :], pQ[:, 256:512]))
            emit_rope(P, q_tm, tC, tS, i * 64, 16, "inter", rsc)
            emit_qknorm(P, q_tm, 16, rsc)
            emit_rope(P, k_tm, tC, tS, i * 64, 4, "inter", rsc)
            emit_qknorm(P, k_tm, 4, rsc)
            with P.step("sp", reads=[q_tm, k_tm, v_tm], dma=True) as st:
                st.add(lambda e_, ii=i: e_.dma_start(out=qlo[ii], in_=q_tm[:, :]))
                st.add(lambda e_, ii=i: e_.dma_start(out=klo[ii], in_=k_tm[:, :]))
                st.add(lambda e_, ii=i: e_.dma_start(out=vlo[ii], in_=v_tm[:, :]))
        P.final_barrier()
        blk = e(nc.Block())
        P.emit_into_block(blk)
    return nc


def build_launch_I():
    nc = bass.Bass("TRN2", target_bir_lowering=False, debug=False)
    hB1i = nc.dram_tensor("hB1i", [DC, 128, T], F32, kind="ExternalInput").ap()
    x2i = nc.dram_tensor("x2i", [DC, 128, T], F32, kind="ExternalInput").ap()
    qlTi = nc.dram_tensor("qlTi", [DC, 128, T], BF16, kind="ExternalInput").ap()
    Klfmi = nc.dram_tensor("Klfmi", [2, 128, S], BF16, kind="ExternalInput").ap()
    Vltmi = nc.dram_tensor("Vltmi", [KC, 128, 256], BF16, kind="ExternalInput").ap()
    mli = nc.dram_tensor("mli", [NT, 4, 128, 128], BF16, kind="ExternalInput").ap()
    lwo = nc.dram_tensor("lwo", [1024, 1024], F32, kind="ExternalInput").ap()
    w4i = nc.dram_tensor("w4i", [1024, 1024], F32, kind="ExternalInput").ap()
    idi = nc.dram_tensor("idi", [128, 128], BF16, kind="ExternalInput").ap()
    onci = nc.dram_tensor("onci", [128, 1], BF16, kind="ExternalInput").ap()
    onri = nc.dram_tensor("onri", [1, 128], BF16, kind="ExternalInput").ap()
    sli = nc.dram_tensor("sli", [1, 128], BF16, kind="ExternalInput").ap()
    shi = nc.dram_tensor("shi", [1, 128], BF16, kind="ExternalInput").ap()
    xfo = nc.dram_tensor("xfo", [DC, 128, T], F32, kind="ExternalOutput").ap()

    from contextlib import ExitStack
    with ExitStack() as ctx:
        e = ctx.enter_context
        P = Prog(nc)
        P.sem_handles(ctx)
        xr = Buf(e(nc.sbuf_tensor("xr", [128, DC * T], F32)), "xr")
        xnr = Buf(e(nc.sbuf_tensor("xnr", [128, DC * 512], BF16)), "xnr")
        qT = Buf(e(nc.sbuf_tensor("qT", [128, DC * T], BF16)), "qT")
        Kfm = Buf(e(nc.sbuf_tensor("Kfm", [128, 2 * S], BF16)), "Kfm")
        Vp = Buf(e(nc.sbuf_tensor("Vp", [128, KC * 260], BF16)), "Vp")
        msk = Buf(e(nc.sbuf_tensor("msk", [128, NT * 4 * 128], BF16)), "msk")
        wo_sb = Buf(e(nc.sbuf_tensor("wo_sb", [128, DC * 1024], BF16)), "wo_sb")
        ident = Buf(e(nc.sbuf_tensor("ident", [128, 128], BF16)), "ident")
        sel_lo = Buf(e(nc.sbuf_tensor("sel_lo", [1, 128], BF16)), "sel_lo")
        sel_hi = Buf(e(nc.sbuf_tensor("sel_hi", [1, 128], BF16)), "sel_hi")
        aA = Buf(e(nc.sbuf_tensor("aA", [128, 256], BF16)), "aA")
        aB = Buf(e(nc.sbuf_tensor("aB", [128, 256], BF16)), "aB")
        den = Buf(e(nc.sbuf_tensor("den", [1, 512], F32)), "den")
        rden = Buf(e(nc.sbuf_tensor("rden", [1, 512], BF16)), "rden")
        outT = Buf(e(nc.sbuf_tensor("outT", [128, 1024], BF16)), "outT")
        outTn = Buf(e(nc.sbuf_tensor("outTn", [128, 1024], BF16)), "outTn")
        rb = Buf(e(nc.sbuf_tensor("rb", [128, 256], BF16)), "rb")
        cst = Buf(e(nc.sbuf_tensor("cst", [128, 4], F32)), "cst")
        slab = Buf(e(nc.sbuf_tensor("slab", [128, 512], F32)), "slab")
        slab2 = slab
        onc = Buf(e(nc.sbuf_tensor("onc", [128, 1], BF16)), "onc")
        onr = Buf(e(nc.sbuf_tensor("onr", [1, 128], BF16)), "onr")
        bks = {}
        for n in ("pA", "pB", "pC", "pD", "pE"):
            bks[n] = Buf(e(nc.psum_tensor(n, [128, 512], F32)), n)
        pT = Buf(e(nc.psum_tensor("pT", [128, 1024], BF16)), "pT")
        nsc = alloc_norm_scratch(nc, ctx, P, banks=bks, with_xc=True)
        consts = {"ones_col": onc, "ones1r": onr}

        with P.step("sp", writes=[xr], dma=True) as st:
            st.add(lambda e_: e_.dma_start(out=xr[:, :].rearrange("p (c t) -> p c t", c=DC),
                                           in_=hB1i.rearrange("c p t -> p c t")))
        with P.step("sp", writes=[qT, Kfm, Vp, msk, ident, sel_lo, sel_hi, onc, onr], dma=True) as st:
            st.add(lambda e_: e_.dma_start(out=qT[:, :].rearrange("p (c t) -> p c t", c=DC),
                                           in_=qlTi.rearrange("c p t -> p c t")))
            st.add(lambda e_: e_.dma_start(out=Kfm[:, :].rearrange("p (c s) -> p c s", c=2),
                                           in_=Klfmi.rearrange("c p s -> p c s")))
            for k in range(4):
                st.add(lambda e_, k_=k: e_.dma_start(
                    out=bass.AP(Vp.t, k_ * 65, [[KC * 260, 128], [260, KC], [1, 64]]),
                    in_=Vltmi[:, :, k_ * 64:(k_ + 1) * 64].rearrange("c p f -> p c f")))
            st.add(lambda e_: e_.dma_start(out=msk[:, :].rearrange("p (g j q) -> p g j q", g=NT, j=4),
                                           in_=mli.rearrange("g j p q -> p g j q")))
            st.add(lambda e_: e_.dma_start(out=ident[:, :], in_=idi))
            st.add(lambda e_: e_.dma_start(out=sel_lo[:, :], in_=sli))
            st.add(lambda e_: e_.dma_start(out=sel_hi[:, :], in_=shi))
            st.add(lambda e_: e_.dma_start(out=onc[:, :], in_=onci))
            st.add(lambda e_: e_.dma_start(out=onr[:, :], in_=onri))
        emit_wo_perm_load(P, None, wo_sb, lwo)
        with P.step("dve", writes=[Vp]) as st:
            st.add(lambda e_: e_.memset(
                bass.AP(Vp.t, 64, [[KC * 260, 128], [260, KC], [65, 4], [1, 1]]), 1.0))
        with P.step("dve", writes=[cst]) as st:
            st.add(lambda e_: e_.memset(cst[:, 3:4], 0.0))

        cfg = {
            "heads": list(L1_HEADS),
            "qwidth": 128,
            "ngroups": NT,
            "qslice": lambda h, g: qT[(gqa_pos(h) % 2) * 64:(gqa_pos(h) % 2) * 64 + 64,
                                      (gqa_pos(h) // 2) * T + g * 128:(gqa_pos(h) // 2) * T + g * 128 + 128],
            "kslice": lambda h, c: Kfm[((h // 4) % 2) * 64:((h // 4) % 2) * 64 + 64,
                                       ((h // 4) // 2) * S + c * 128:((h // 4) // 2) * S + c * 128 + 128],
            "vslice": lambda h, c: Vp[:, c * 260 + (h // 4) * 65: c * 260 + (h // 4) * 65 + 65],
            "mask": msk,
            "mslice": lambda g, j: msk[:, (g * 4 + j) * 128:(g * 4 + j + 1) * 128],
            "reads_qkv": [qT, Kfm, Vp],
            "banks": bks,
            "scratch": {"aA": aA, "aB": aB, "den": den, "rden": rden, "outT": outT,
                        "rb": rb, "cst": cst},
            "sel_lo": sel_lo, "sel_hi": sel_hi,
            "wo_sb": wo_sb,
            "emit_resid": (lambda P_, oc, g, pP: _resid_add(P_, xr, oc, g, pP, w=128)),
            "scale": 0.125,
            "nch_of_g": lambda g: 4,
            "chunk_of": lambda g, j: max(0, 2 * g - 2 + j),
        }
        emit_attention_causal(P, cfg)

        for oc in range(DC):
            for tr in range(4):
                sl = slab if (oc * 4 + tr) % 2 == 0 else slab2
                with P.step("sp", writes=[sl], dma=True) as st:
                    st.add(lambda e_, o_=oc, t_=tr, s_=sl: e_.dma_start(
                        out=s_[:, :], in_=x2i[o_, :, t_ * 512:(t_ + 1) * 512]))
                with P.step("dve", reads=[xr, sl], writes=[xr]) as st:
                    st.add(lambda e_, o_=oc, t_=tr, s_=sl: e_.tensor_add(
                        xr[:, o_ * T + t_ * 512: o_ * T + t_ * 512 + 512],
                        xr[:, o_ * T + t_ * 512: o_ * T + t_ * 512 + 512], s_[:, :]))

        emit_norm(P, ctx, xr, None, "ln_stats", consts, 1e-5, scratch=nsc)

        w4_sb, w4T = qT, Kfm
        with P.step("pool", writes=[w4_sb], dma=True) as st:
            st.add(lambda e_: e_.dma_start(out=w4_sb[:, 0:8192].rearrange("p (c n) -> p c n", c=DC),
                                           in_=w4i.rearrange("(c p) n -> p c n", c=DC)))
        for ch in range(DC):
            with P.step("act", reads=[w4_sb], writes=[w4_sb]) as st:
                st.add(lambda e_, c=ch: e_.activation(
                    w4_sb[:, c * 1024:(c + 1) * 1024], w4_sb[:, c * 1024:(c + 1) * 1024], AF.Sign))
        for oc in range(DC):
            for ic in range(DC):
                with P.step("pe", reads=[w4_sb, ident], writes=[pT]) as st:
                    st.add(lambda e_, o_=oc, i_=ic: e_.transpose(
                        pT[:, 0:128], w4_sb[:, o_ * 1024 + i_ * 128: o_ * 1024 + i_ * 128 + 128],
                        ident[:, :]))
                with P.step("dve", reads=[pT], writes=[w4T]) as st:
                    st.add(lambda e_, o_=oc, i_=ic: e_.tensor_copy(
                        w4T[:, i_ * 1024 + o_ * 128: i_ * 1024 + o_ * 128 + 128], pT[:, 0:128]))
        pA, pB = bks["pA"], bks["pB"]
        pB1, pB2 = nsc["pB1"], nsc["pB2"]
        srstd, smr, rbn, mrbn, onrB = nsc["srstd"], nsc["smr"], nsc["rb"], nsc["mrb"], consts["ones1r"]
        for r in range(4):
            with P.step("pe", reads=[srstd, onr], writes=[pB1]) as st:
                st.add(lambda e_, r_=r: e_.matmul(pB1[:, :], onr[0:1, :],
                                                  srstd[0:1, r_ * 512:(r_ + 1) * 512],
                                                  start=True, stop=True))
            with P.step("dve", reads=[pB1], writes=[rbn]) as st:
                st.add(lambda e_: e_.tensor_copy(rbn[:, :], pB1[:, :]))
            with P.step("pe", reads=[smr, onr], writes=[pB2]) as st:
                st.add(lambda e_, r_=r: e_.matmul(pB2[:, :], onr[0:1, :],
                                                  smr[0:1, r_ * 512:(r_ + 1) * 512],
                                                  start=True, stop=True))
            with P.step("dve", reads=[pB2], writes=[mrbn]) as st:
                st.add(lambda e_: e_.tensor_copy(mrbn[:, :], pB2[:, :]))
            for ic in range(DC):
                with P.step("dve", reads=[xr, rbn], writes=[xnr]) as st:
                    st.add(lambda e_, i_=ic, r_=r: e_.tensor_mul(
                        xnr[:, i_ * 512:(i_ + 1) * 512],
                        xr[:, i_ * T + r_ * 512: i_ * T + r_ * 512 + 512], rbn[:, :]))
                with P.step("dve", reads=[xnr, mrbn], writes=[xnr]) as st:
                    st.add(lambda e_, i_=ic: e_.tensor_sub(
                        xnr[:, i_ * 512:(i_ + 1) * 512], xnr[:, i_ * 512:(i_ + 1) * 512], mrbn[:, :]))
            for oc in range(DC):
                pP = pA if oc % 2 == 0 else pB
                with P.step("pe", reads=[xnr, w4T], writes=[pP]) as st:
                    for ic in range(DC):
                        st.add(lambda e_, i_=ic, o_=oc, pP_=pP: e_.matmul(
                            pP_[:, :], w4T[:, i_ * 1024 + o_ * 128: i_ * 1024 + o_ * 128 + 128],
                            xnr[:, i_ * 512:(i_ + 1) * 512],
                            start=(i_ == 0), stop=(i_ == DC - 1)))
                with P.step("dve", reads=[pP, xr], writes=[xr]) as st:
                    st.add(lambda e_, o_=oc, r_=r, pP_=pP: e_.tensor_add(
                        xr[:, o_ * T + r_ * 512: o_ * T + r_ * 512 + 512], pP_[:, :],
                        xr[:, o_ * T + r_ * 512: o_ * T + r_ * 512 + 512]))

        with P.step("sp", reads=[xr], dma=True) as st:
            st.add(lambda e_: e_.dma_start(out=xfo.rearrange("c p t -> p c t"),
                                           in_=xr[:, :].rearrange("p (c t) -> p c t", c=DC)))
        P.final_barrier()
        blk = e(nc.Block())
        P.emit_into_block(blk)
    return nc



def build_fused(stop_after=None):
    nc = bass.Bass("TRN2", target_bir_lowering=False, debug=False)
    A_ = lambda *a, **k: nc.dram_tensor(*a, **k).ap()
    xin = A_("xin", [DC, 128, T], F32, kind="ExternalInput")
    wq0 = A_("wq0", [1024, 1024], F32, kind="ExternalInput")
    wk0 = A_("wk0", [1024, 256], F32, kind="ExternalInput")
    wv0 = A_("wv0", [1024, 256], F32, kind="ExternalInput")
    wo0 = A_("wo0", [1024, 1024], F32, kind="ExternalInput")
    g1i = A_("g1i", [1024], F32, kind="ExternalInput")
    wqm = A_("wqm", [1024, 1024], F32, kind="ExternalInput")
    wdkv = A_("wdkv", [1024, 512], F32, kind="ExternalInput")
    wuk = A_("wuk", [512, 1024], F32, kind="ExternalInput")
    wuv = A_("wuv", [512, 1024], F32, kind="ExternalInput")
    wom = A_("wom", [1024, 1024], F32, kind="ExternalInput")
    g2i = A_("g2i", [1024], F32, kind="ExternalInput")
    gwi = A_("gwi", [1024, 8], F32, kind="ExternalInput")
    w1d = A_("w1d", [8, 1024, 2048], F32, kind="ExternalInput")
    w2d = A_("w2d", [8, 2048, 1024], F32, kind="ExternalInput")
    n1wi = A_("n1wi", [1024], F32, kind="ExternalInput")
    n2wi = A_("n2wi", [1024], F32, kind="ExternalInput")
    gwq = A_("gwq", [1024, 1024], F32, kind="ExternalInput")
    gkvd = A_("gkvd", [1024, 256], F32, kind="ExternalInput")
    gkup = A_("gkup", [256, 256], F32, kind="ExternalInput")
    gvup = A_("gvup", [256, 256], F32, kind="ExternalInput")
    gwo = A_("gwo", [1024, 1024], F32, kind="ExternalInput")
    lwq = A_("lwq", [1024, 1024], F32, kind="ExternalInput")
    lwk = A_("lwk", [1024, 256], F32, kind="ExternalInput")
    lwv = A_("lwv", [1024, 256], F32, kind="ExternalInput")
    lwo = A_("lwo", [1024, 1024], F32, kind="ExternalInput")
    w4i = A_("w4i", [1024, 1024], F32, kind="ExternalInput")
    Cr0i = A_("Cr0i", [NT, 128, 64], BF16, kind="ExternalInput")
    Sr0i = A_("Sr0i", [NT, 128, 64], BF16, kind="ExternalInput")
    Crgi = A_("Crgi", [NT, 128, 64], BF16, kind="ExternalInput")
    Srgi = A_("Srgi", [NT, 128, 64], BF16, kind="ExternalInput")
    Crli = A_("Crli", [NT, 128, 64], BF16, kind="ExternalInput")
    Srli = A_("Srli", [NT, 128, 64], BF16, kind="ExternalInput")
    Crgfi = A_("Crgfi", [KC, 128, 64], BF16, kind="ExternalInput")
    Srgfi = A_("Srgfi", [KC, 128, 64], BF16, kind="ExternalInput")
    mci = A_("mci", [4, 8, 128, 512], BF16, kind="ExternalInput")
    mlii = A_("mlii", [NT, 4, 128, 128], BF16, kind="ExternalInput")
    idi = A_("idi", [128, 128], BF16, kind="ExternalInput")
    onci = A_("onci", [128, 1], BF16, kind="ExternalInput")
    onri = A_("onri", [1, 128], BF16, kind="ExternalInput")
    sli = A_("sli", [1, 128], BF16, kind="ExternalInput")
    shi = A_("shi", [1, 128], BF16, kind="ExternalInput")
    xfo = A_("xfo", [DC, 128, T], F32, kind="ExternalOutput")

    xd1 = Buf(nc.dram_tensor("xd1", [DC, 128, T], F32, kind="Internal").ap(), "xd1")
    xd2 = Buf(nc.dram_tensor("xd2", [DC, 128, T], F32, kind="Internal").ap(), "xd2")
    hd = Buf(nc.dram_tensor("hd", [DC, 128, T], F32, kind="Internal").ap(), "hd")
    hd1 = Buf(nc.dram_tensor("hd1", [DC, 128, T], F32, kind="Internal").ap(), "hd1")
    cc0i = Buf(nc.dram_tensor("cc0i", [T, 512], BF16, kind="Internal").ap(), "cc0i")
    cc0o = Buf(nc.dram_tensor("cc0o", [2 * T, 512], BF16, kind="Internal").ap(), "cc0o")
    ccMi = Buf(nc.dram_tensor("ccMi", [512, T], BF16, kind="Internal").ap(), "ccMi")
    ccMo = Buf(nc.dram_tensor("ccMo", [1024, T], BF16, kind="Internal").ap(), "ccMo")
    ccGi = Buf(nc.dram_tensor("ccGi", [256, T], BF16, kind="Internal").ap(), "ccGi")
    ccGo = Buf(nc.dram_tensor("ccGo", [512, T], BF16, kind="Internal").ap(), "ccGo")
    ccLi = Buf(nc.dram_tensor("ccLi", [T, 512], BF16, kind="Internal").ap(), "ccLi")
    ccLo = Buf(nc.dram_tensor("ccLo", [2 * T, 512], BF16, kind="Internal").ap(), "ccLo")
    PAIRS = [[0, 1], [2, 3], [4, 5], [6, 7]]

    from contextlib import ExitStack
    with ExitStack() as OU:
        e = OU.enter_context
        P = Prog(nc)
        P.sem_handles(OU)
        ccsems = [P.cc_sem(OU) for _ in range(4)]
        xr = Buf(e(nc.sbuf_tensor("xr", [128, DC * T], F32)), "xr")
        qTa = Buf(e(nc.sbuf_tensor("qTa", [128, DC * T], BF16)), "qTa")

        ident = Buf(e(nc.sbuf_tensor("ident", [128, 128], BF16)), "ident")
        onc = Buf(e(nc.sbuf_tensor("onc", [128, 1], BF16)), "onc")
        onr = Buf(e(nc.sbuf_tensor("onr", [1, 128], BF16)), "onr")
        sel_lo = Buf(e(nc.sbuf_tensor("sel_lo", [1, 128], BF16)), "sel_lo")
        sel_hi = Buf(e(nc.sbuf_tensor("sel_hi", [1, 128], BF16)), "sel_hi")
        cst = Buf(e(nc.sbuf_tensor("cst", [128, 4], F32)), "cst")
        bks = {}
        for n in ("pA", "pB", "pC", "pD", "pE", "pF"):
            bks[n] = Buf(e(nc.psum_tensor(n, [128, 512], F32)), n)
        pT = Buf(e(nc.psum_tensor("pT", [128, 1024], BF16)), "pT")
        pT2 = Buf(e(nc.psum_tensor("pT2", [128, 1024], BF16)), "pT2")
        consts = {"ones_col": onc, "ones1r": onr}

        with P.step("sp", writes=[xr], dma=True) as st:
            st.add(lambda e_: e_.dma_start(out=xr[:, :].rearrange("p (c t) -> p c t", c=DC),
                                           in_=xin.rearrange("c p t -> p c t")))
        with P.step("sp", writes=[ident, onc, onr, sel_lo, sel_hi], dma=True) as st:
            st.add(lambda e_: e_.dma_start(out=ident[:, :], in_=idi))
            st.add(lambda e_: e_.dma_start(out=onc[:, :], in_=onci))
            st.add(lambda e_: e_.dma_start(out=onr[:, :], in_=onri))
            st.add(lambda e_: e_.dma_start(out=sel_lo[:, :], in_=sli))
            st.add(lambda e_: e_.dma_start(out=sel_hi[:, :], in_=shi))
        with P.step("dve", writes=[cst]) as st:
            st.add(lambda e_: e_.memset(cst[:, 3:4], 0.0))

        def load_tabs(tabC, tabS, Ci, Si, n=NT):
            with P.step("sp", writes=[tabC, tabS], dma=True) as st:
                st.add(lambda e_: e_.dma_start(
                    out=tabC[:, 0:n * 64].rearrange("p (i f) -> p i f", i=n),
                    in_=Ci.rearrange("i p f -> p i f")))
                st.add(lambda e_: e_.dma_start(
                    out=tabS[:, 0:n * 64].rearrange("p (i f) -> p i f", i=n),
                    in_=Si.rearrange("i p f -> p i f")))


        def load_w(dst, src, nch, ncol):
            with P.step("pool", writes=[dst], dma=True) as st:
                st.add(lambda e_: e_.dma_start(
                    out=dst[:, 0:nch * ncol].rearrange("p (c n) -> p c n", c=nch),
                    in_=src.rearrange("(c p) n -> p c n", c=nch)))

        def load_gain(dst, src):
            with P.step("sp", writes=[dst], dma=True) as st:
                for ch in range(DC):
                    st.add(lambda e_, c=ch: e_.dma_start(
                        out=dst[:, c:c + 1],
                        in_=src[c * 128:(c + 1) * 128].rearrange("(p o) -> p o", o=1)))

        def q_transpose_out(PP, q_tm, i):
            for ch in range(DC):
                pX = pT if ch % 2 == 0 else pT2
                with PP.step("pe", reads=[q_tm, ident], writes=[pX]) as st:
                    st.add(lambda e_, c=ch, p_=pX: e_.transpose(
                        p_[:, 0:128], q_tm[:, c * 128:(c + 1) * 128], ident[:, :]))
                with PP.step("dve", reads=[pX], writes=[qTa]) as st:
                    st.add(lambda e_, c=ch, ii=i, p_=pX: e_.tensor_copy(
                        qTa[:, c * T + ii * 128: c * T + ii * 128 + 128], p_[:, 0:128]))

        def _phase1():
            with ExitStack() as SC:
                es = SC.enter_context
                xn = Buf(es(nc.sbuf_tensor("A_xn", [128, DC * T], BF16)), "xn")
                wq_sb = Buf(es(nc.sbuf_tensor("A_wq", [128, DC * 1024], BF16)), "wq_sb")
                wkv_sb = Buf(es(nc.sbuf_tensor("A_wkv", [128, DC * 512], BF16)), "wkv_sb")
                q_tmP = [Buf(es(nc.sbuf_tensor(f"A_qtm{x}", [128, 1024], BF16)), "q_tm") for x in range(2)]
                k_tmP = [Buf(es(nc.sbuf_tensor(f"A_ktm{x}", [128, 256], BF16)), "k_tm") for x in range(2)]
                v_tmP = [Buf(es(nc.sbuf_tensor(f"A_vtm{x}", [128, 256], BF16)), "v_tm") for x in range(2)]
                rscP = [{"r1": Buf(es(nc.sbuf_tensor(f"A_r1{x}", [128, 1024], BF16)), "r1"),
                         "r2": Buf(es(nc.sbuf_tensor(f"A_r2{x}", [128, 1024], BF16)), "r2"),
                         "cst": cst} for x in range(2)]
                tabC = Buf(es(nc.sbuf_tensor("A_tC", [128, 1024], BF16)), "tabC")
                tabS = Buf(es(nc.sbuf_tensor("A_tS", [128, 1024], BF16)), "tabS")
                nsc = alloc_norm_scratch(nc, SC, P, banks=bks, with_xc=False)
                load_tabs(tabC, tabS, Cr0i, Sr0i)
                load_w(wq_sb, wq0, DC, 1024)
                with P.step("pool", writes=[wkv_sb], dma=True) as st:
                    st.add(lambda e_: e_.dma_start(
                        out=bass.AP(wkv_sb.t, 0, [[DC * 512, 128], [512, DC], [1, 256]]),
                        in_=wk0.rearrange("(c p) n -> p c n", c=DC)))
                    st.add(lambda e_: e_.dma_start(
                        out=bass.AP(wkv_sb.t, 256, [[DC * 512, 128], [512, DC], [1, 256]]),
                        in_=wv0.rearrange("(c p) n -> p c n", c=DC)))
                emit_norm(P, SC, xr, xn, "ln", consts, 1e-5, scratch=nsc)
                for i in range(NT):
                    q_tm, k_tm, v_tm, rsc = q_tmP[i % 2], k_tmP[i % 2], v_tmP[i % 2], rscP[i % 2]
                    for nr in range(3):
                        pQ = bks["pE"] if nr % 2 == 0 else bks["pF"]
                        with P.step("pe", reads=[xn, wq_sb, wkv_sb], writes=[pQ]) as st:
                            for ch in range(DC):
                                if nr < 2:
                                    w_ap = wq_sb[:, ch * 1024 + nr * 512: ch * 1024 + nr * 512 + 512]
                                else:
                                    w_ap = wkv_sb[:, ch * 512: ch * 512 + 512]
                                st.add(lambda e_, c=ch, w=w_ap, ii=i, p_=pQ: e_.matmul(
                                    p_[:, :], xn[:, c * T + ii * 128: c * T + ii * 128 + 128], w,
                                    start=(c == 0), stop=(c == DC - 1)))
                        if nr < 2:
                            with P.step("dve", reads=[pQ], writes=[q_tm]) as st:
                                st.add(lambda e_, n=nr, p_=pQ, q_=q_tm: e_.tensor_copy(
                                    q_[:, n * 512:(n + 1) * 512], p_[:, :]))
                        else:
                            with P.step("dve", reads=[pQ], writes=[k_tm, v_tm]) as st:
                                st.add(lambda e_, p_=pQ, k_=k_tm: e_.tensor_copy(k_[:, :], p_[:, 0:256]))
                                st.add(lambda e_, p_=pQ, v_=v_tm: e_.tensor_copy(v_[:, :], p_[:, 256:512]))
                    emit_rope(P, q_tm, tabC, tabS, i * 64, 16, "half", rsc)
                    emit_rope(P, k_tm, tabC, tabS, i * 64, 4, "half", rsc)
                    q_transpose_out(P, q_tm, i)
                    with P.step("sp", reads=[k_tm, v_tm], writes=[cc0i], dma=True) as st:
                        st.add(lambda e_, ii=i, k_=k_tm: e_.dma_start(
                            out=cc0i.t[ii * 128:(ii + 1) * 128, 0:256], in_=k_[:, :]))
                        st.add(lambda e_, ii=i, v_=v_tm: e_.dma_start(
                            out=cc0i.t[ii * 128:(ii + 1) * 128, 256:512], in_=v_[:, :]))
                with P.step("pool", reads=[cc0i], writes=[cc0o], collective_sem=ccsems[0]) as st:
                    st.add(lambda e_: e_.collective_compute(
                        "AllGather", ALU.bypass, replica_groups=PAIRS,
                        ins=[cc0i.t], outs=[cc0o.t]))
        _phase1()
        P.phase_barrier()
        if stop_after == "A":
            with P.step("sp", reads=[xr], dma=True) as st:
                st.add(lambda e_: e_.dma_start(out=xfo.rearrange("c p t -> p c t"),
                                               in_=xr[:, :].rearrange("p (c t) -> p c t", c=DC)))
            P.final_barrier()
            blk = e(nc.Block())
            P.emit_into_block(blk)
            return nc

        def _phase2():
            with ExitStack() as SC:
                es = SC.enter_context
                Kfm = Buf(es(nc.sbuf_tensor("B_K", [128, 2 * S], BF16)), "Kfm")
                Vp = Buf(es(nc.sbuf_tensor("B_V", [128, KC * 260], BF16)), "Vp")
                wo_sb = Buf(es(nc.sbuf_tensor("B_wo", [128, DC * 1024], BF16)), "wo_sb")
                ktst = Buf(es(nc.sbuf_tensor("B_kt", [128, 256], BF16)), "ktst")
                aP4 = [Buf(es(nc.sbuf_tensor(f"B_a{x}", [128, 1024], BF16)), "a") for x in range(2)]
                aA, aB = aP4[0], aP4[1]
                den = Buf(es(nc.sbuf_tensor("B_den", [1, 1024], F32)), "den")
                den2 = Buf(es(nc.sbuf_tensor("B_den2", [1, 1024], F32)), "den2")
                rden = Buf(es(nc.sbuf_tensor("B_rden", [1, 1024], BF16)), "rden")
                rden2 = Buf(es(nc.sbuf_tensor("B_rden2", [1, 1024], BF16)), "rden2")
                outT = Buf(es(nc.sbuf_tensor("B_oT", [128, 4096], BF16)), "outT")
                rb = Buf(es(nc.sbuf_tensor("B_rb", [128, 512], BF16)), "rb")
                mstA = Buf(es(nc.sbuf_tensor("B_mA", [128, 4096], BF16)), "mstA")
                mstB = Buf(es(nc.sbuf_tensor("B_mB", [128, 4096], BF16)), "mstB")

                def mask_load_c(P_, g):
                    m = mstA if g % 2 == 0 else mstB
                    with P_.step("sp", writes=[m], dma=True) as st:
                        st.add(lambda e_, g_=g, m_=m: e_.dma_start(
                            out=m_[:, :].rearrange("p (j q) -> p j q", j=8),
                            in_=mci[g_].rearrange("j p q -> p j q")))
                    return m
                load_w(wo_sb, wo0, DC, 1024)
                for pr in range(2):
                    for i in range(NT):
                        t = 2 * i + pr
                        with P.step("sp", reads=[cc0o], writes=[ktst], dma=True) as st:
                            st.add(lambda e_, p_=pr, ii=i: e_.dma_start(
                                out=ktst[:, :], in_=cc0o.t[p_ * T + ii * 128: p_ * T + ii * 128 + 128, 0:256]))
                        for half in range(2):
                            with P.step("pe", reads=[ktst, ident], writes=[pT]) as st:
                                st.add(lambda e_, h_=half: e_.transpose(
                                    pT[:, 0:128], ktst[:, h_ * 128:(h_ + 1) * 128], ident[:, :]))
                            with P.step("dve", reads=[pT], writes=[Kfm]) as st:
                                st.add(lambda e_, h_=half, t_=t: e_.tensor_copy(
                                    Kfm[:, h_ * S + t_ * 128:h_ * S + t_ * 128 + 128], pT[:, 0:128]))
                    for kv in range(4):
                        with P.step("sp", reads=[cc0o], writes=[Vp], dma=True) as st:
                            st.add(lambda e_, p_=pr, k_=kv: e_.dma_start(
                                out=bass.AP(Vp.t, p_ * 260 + k_ * 65, [[KC * 260, 128], [2 * 260, NT], [1, 64]]),
                                in_=cc0o.t[p_ * T:(p_ + 1) * T, 256 + k_ * 64: 256 + (k_ + 1) * 64]
                                .rearrange("(i p) f -> p i f", p=128)))
                with P.step("dve", writes=[Vp]) as st:
                    st.add(lambda e_: e_.memset(
                        bass.AP(Vp.t, 64, [[KC * 260, 128], [260, KC], [65, 4], [1, 1]]), 1.0))
                cfg = {
                    "heads": list(range(16)),
                    "qwidth": 512, "ngroups": 4, "nmask": 8,
                    "qslice": lambda h, g: qTa[(h % 2) * 64:(h % 2) * 64 + 64,
                                               (h // 2) * T + g * 512:(h // 2) * T + g * 512 + 512],
                    "kslice": lambda h, c: Kfm[((h % 4) % 2) * 64:((h % 4) % 2) * 64 + 64,
                                               ((h % 4) // 2) * S + c * 128:((h % 4) // 2) * S + c * 128 + 128],
                    "vslice": lambda h, c: Vp[:, c * 260 + (h % 4) * 65: c * 260 + (h % 4) * 65 + 65],
                    "mask_load": mask_load_c,
                    "reads_qkv": [qTa, Kfm, Vp],
                    "banks": bks,
                    "scratch": {"aA": aA, "aB": aB, "den": den, "den2": den2, "rden": rden,
                                "rden2": rden2, "outT": outT, "rb": rb, "cst": cst},
                    "sel_lo": sel_lo, "sel_hi": sel_hi,
                    "wo_sb": wo_sb,
                    "emit_resid": (lambda P_, oc, g, pP: _resid_add(P_, xr, oc, g, pP, w=512)),
                    "scale": 0.125,
                    "nch_of_g": lambda g: 8 * g + 8,
                    "chunk_of": lambda g, j: j,
                }
                emit_attention_causal(P, cfg)
                with P.step("sp", reads=[xr], writes=[xd1], dma=True) as st:
                    st.add(lambda e_: e_.dma_start(out=xd1.t.rearrange("c p t -> p c t"),
                                                   in_=xr[:, :].rearrange("p (c t) -> p c t", c=DC)))
        _phase2()
        P.phase_barrier()
        if stop_after == "B":
            with P.step("sp", reads=[xr], dma=True) as st:
                st.add(lambda e_: e_.dma_start(out=xfo.rearrange("c p t -> p c t"),
                                               in_=xr[:, :].rearrange("p (c t) -> p c t", c=DC)))
            P.final_barrier()
            blk = e(nc.Block())
            P.emit_into_block(blk)
            return nc

        def _phase3():
            with ExitStack() as SC:
                es = SC.enter_context
                xn = Buf(es(nc.sbuf_tensor("C_xn", [128, DC * T], BF16)), "xn")
                wq_sb = Buf(es(nc.sbuf_tensor("C_wq", [128, DC * 1024], BF16)), "wq_sb")
                wdkv_sb = Buf(es(nc.sbuf_tensor("C_wdkv", [128, DC * 512], BF16)), "wdkv_sb")
                g1_sb = Buf(es(nc.sbuf_tensor("C_g1", [128, DC], F32)), "g1_sb")
                q_tmP = [Buf(es(nc.sbuf_tensor(f"C_qtm{x}", [128, 1024], BF16)), "q_tm") for x in range(2)]
                lat_sbP = [Buf(es(nc.sbuf_tensor(f"C_lat{x}", [128, 512], BF16)), "lat_sb") for x in range(2)]
                nsc = alloc_norm_scratch(nc, SC, P, banks=bks, with_xc=True)
                load_w(wq_sb, wqm, DC, 1024)
                load_w(wdkv_sb, wdkv, DC, 512)
                load_gain(g1_sb, g1i)
                emit_norm(P, SC, xr, xn, "ln32", consts, 1e-5, scratch=nsc)
                with P.step("sp", reads=[xr], writes=[hd], dma=True) as st:
                    st.add(lambda e_: e_.dma_start(out=hd.t.rearrange("c p t -> p c t"),
                                                   in_=xr[:, :].rearrange("p (c t) -> p c t", c=DC)))
                emit_norm(P, ctx if False else SC, xr, xn, "rms", consts, 1e-6, gain=g1_sb, scratch=nsc)
                for qc in range(DC):
                    for r in range(4):
                        pQ = bks["pE"] if (qc * 4 + r) % 2 == 0 else bks["pF"]
                        with P.step("pe", reads=[xn, wq_sb], writes=[pQ]) as st:
                            for ch in range(DC):
                                st.add(lambda e_, c=ch, q_=qc, r_=r, p_=pQ: e_.matmul(
                                    p_[:, :], wq_sb[:, c * 1024 + q_ * 128: c * 1024 + q_ * 128 + 128],
                                    xn[:, c * T + r_ * 512: c * T + r_ * 512 + 512],
                                    start=(c == 0), stop=(c == DC - 1)))
                        with P.step("dve", reads=[pQ], writes=[qTa]) as st:
                            st.add(lambda e_, q_=qc, r_=r, p_=pQ: e_.tensor_copy(
                                qTa[:, q_ * T + r_ * 512: q_ * T + r_ * 512 + 512], p_[:, :]))
                for lc in range(4):
                    for r in range(4):
                        pr_ = (lc * 4 + r) % 2
                        pQ = bks["pE"] if pr_ == 0 else bks["pF"]
                        lat_sb = lat_sbP[pr_]
                        with P.step("pe", reads=[xn, wdkv_sb], writes=[pQ]) as st:
                            for ch in range(DC):
                                st.add(lambda e_, c=ch, l=lc, r_=r, p_=pQ: e_.matmul(
                                    p_[:, :], wdkv_sb[:, c * 512 + l * 128: c * 512 + l * 128 + 128],
                                    xn[:, c * T + r_ * 512: c * T + r_ * 512 + 512],
                                    start=(c == 0), stop=(c == DC - 1)))
                        with P.step("dve", reads=[pQ], writes=[lat_sb]) as st:
                            st.add(lambda e_, p_=pQ, l_=lat_sb: e_.tensor_copy(l_[:, :], p_[:, :]))
                        with P.step("sp", reads=[lat_sb], writes=[ccMi], dma=True) as st:
                            st.add(lambda e_, l=lc, r_=r, l_=lat_sb: e_.dma_start(
                                out=ccMi.t[l * 128:(l + 1) * 128, r_ * 512:(r_ + 1) * 512],
                                in_=l_[:, :]))
                with P.step("pool", reads=[ccMi], writes=[ccMo], collective_sem=ccsems[1]) as st:
                    st.add(lambda e_: e_.collective_compute(
                        "AllGather", ALU.bypass, replica_groups=PAIRS,
                        ins=[ccMi.t], outs=[ccMo.t]))
        _phase3()
        P.phase_barrier()
        if stop_after == "C":
            with P.step("sp", reads=[xr], dma=True) as st:
                st.add(lambda e_: e_.dma_start(out=xfo.rearrange("c p t -> p c t"),
                                               in_=xr[:, :].rearrange("p (c t) -> p c t", c=DC)))
            P.final_barrier()
            blk = e(nc.Block())
            P.emit_into_block(blk)
            return nc

        for ps in range(2):
            def _phase4(ps=ps):
                with ExitStack() as SC:
                    es = SC.enter_context
                    Kh = Buf(es(nc.sbuf_tensor(f"D{ps}_Kh", [128, 4 * S], BF16)), "Kh")
                    Vh = Buf(es(nc.sbuf_tensor(f"D{ps}_Vh", [128, KC * 520], BF16)), "Vh")
                    aA = Buf(es(nc.sbuf_tensor(f"D{ps}_aA", [128, 1024], BF16)), "aA")
                    aB = Buf(es(nc.sbuf_tensor(f"D{ps}_aB", [128, 1024], BF16)), "aB")
                    den = Buf(es(nc.sbuf_tensor(f"D{ps}_den", [1, 1024], F32)), "den")
                    rden = Buf(es(nc.sbuf_tensor(f"D{ps}_rden", [1, 1024], BF16)), "rden")
                    outT = Buf(es(nc.sbuf_tensor(f"D{ps}_oT", [128, 2048], BF16)), "outT")
                    rb = Buf(es(nc.sbuf_tensor(f"D{ps}_rb", [128, 512], BF16)), "rb")

                    mstA = Buf(es(nc.sbuf_tensor(f"D{ps}_mA", [128, 4096], BF16)), "mstA")
                    mstB = mstA

                    def mask_load_c(P_, g):
                        m = mstA if g % 2 == 0 else mstB
                        with P_.step("sp", writes=[m], dma=True) as st:
                            st.add(lambda e_, g_=g, m_=m: e_.dma_start(
                                out=m_[:, :].rearrange("p (j q) -> p j q", j=8),
                                in_=mci[g_].rearrange("j p q -> p j q")))
                        return m
                    SCW = es(ExitStack())
                    wuk_sb = Buf(SCW.enter_context(nc.sbuf_tensor(f"D{ps}_wuk", [128, 4 * 1024], BF16)), "wuk_sb")
                    wuv_sb = Buf(SCW.enter_context(nc.sbuf_tensor(f"D{ps}_wuv", [128, 4 * 1024], BF16)), "wuv_sb")
                    latstK = Buf(SCW.enter_context(nc.sbuf_tensor(f"D{ps}_lstK", [128, 2048], BF16)), "latst")
                    latstP = [latstK,
                              Buf(SCW.enter_context(nc.sbuf_tensor(f"D{ps}_lstV", [128, 512], BF16)), "latstV")]
                    load_w(wuk_sb, wuk, 4, 1024)
                    load_w(wuv_sb, wuv, 4, 1024)
                    pQ = bks["pE"]
                    for r in range(8):
                        latst = latstK
                        with P.step("sp", reads=[ccMo], writes=[latst], dma=True) as st:
                            for lc in range(4):
                                for j in range(4):
                                    t = r * 4 + j
                                    st.add(lambda e_, l=lc, j_=j, t_=t, L_=latst: e_.dma_start(
                                        out=L_[:, l * 512 + j_ * 128: l * 512 + j_ * 128 + 128],
                                        in_=ccMo.t[(t_ % 2) * 512 + l * 128: (t_ % 2) * 512 + l * 128 + 128,
                                                   (t_ // 2) * 128:(t_ // 2) * 128 + 128]))
                        for kc in range(4):
                            pQ2 = bks["pE"] if kc % 2 == 0 else bks["pF"]
                            with P.step("pe", reads=[latst, wuk_sb], writes=[pQ2]) as st:
                                for lc in range(4):
                                    st.add(lambda e_, l=lc, k_=kc, p_=ps, L_=latst, q_=pQ2: e_.matmul(
                                        q_[:, :],
                                        wuk_sb[:, l * 1024 + (p_ * 4 + k_) * 128: l * 1024 + (p_ * 4 + k_) * 128 + 128],
                                        L_[:, l * 512:(l + 1) * 512],
                                        start=(l == 0), stop=(l == 3)))
                            with P.step("dve", reads=[pQ2], writes=[Kh]) as st:
                                st.add(lambda e_, k_=kc, r_=r, q_=pQ2: e_.tensor_copy(
                                    Kh[:, k_ * S + r_ * 512: k_ * S + r_ * 512 + 512], q_[:, :]))
                    for c in range(KC):
                        latst = latstP[c % 2]
                        with P.step("sp", reads=[ccMo], writes=[latst], dma=True) as st:
                            for lc in range(4):
                                st.add(lambda e_, l=lc, c_=c, L_=latst: e_.dma_start(
                                    out=L_[:, l * 128:(l + 1) * 128],
                                    in_=ccMo.t[(c_ % 2) * 512 + l * 128: (c_ % 2) * 512 + l * 128 + 128,
                                               (c_ // 2) * 128:(c_ // 2) * 128 + 128]))
                        pQ2 = bks["pE"] if c % 2 == 0 else bks["pF"]
                        with P.step("pe", reads=[latst, wuv_sb], writes=[pQ2]) as st:
                            for lc in range(4):
                                st.add(lambda e_, l=lc, p_=ps, L_=latst, q_=pQ2: e_.matmul(
                                    q_[:, :], L_[:, l * 128:(l + 1) * 128],
                                    wuv_sb[:, l * 1024 + p_ * 512: l * 1024 + p_ * 512 + 512],
                                    start=(l == 0), stop=(l == 3)))
                        with P.step("dve", reads=[pQ2], writes=[Vh]) as st:
                            st.add(lambda e_, c_=c, q_=pQ2: e_.tensor_copy(
                                bass.AP(Vh.t, c_ * 520, [[KC * 520, 128], [65, 8], [1, 64]]),
                                q_[:, :].rearrange("p (h f) -> p h f", h=8)))
                    with P.step("dve", writes=[Vh]) as st:
                        st.add(lambda e_: e_.memset(
                            bass.AP(Vh.t, 64, [[KC * 520, 128], [520, KC], [65, 8], [1, 1]]), 1.0))
                    SCW.close()
                    P.phase_barrier()
                    wom_sb = Buf(es(nc.sbuf_tensor(f"D{ps}_wom", [128, DC * 1024], BF16)), "wom_sb")
                    hsA = Buf(es(nc.sbuf_tensor(f"D{ps}_hsA", [128, 512], F32)), "hsA")
                    hsB = hsA
                    load_w(wom_sb, wom, DC, 1024)

                    def emit_resid_stream(P_, oc, g, pP, ps_=ps, hsA_=hsA, hsB_=hsB):
                        slab = hsA_ if (oc + g) % 2 == 0 else hsB_
                        src = hd if ps_ == 0 else hd1
                        with P_.step("sp", reads=[src], writes=[slab], dma=True) as st:
                            st.add(lambda e_, o_=oc, g_=g, s_=src, sl_=slab: e_.dma_start(
                                out=sl_[:, :], in_=s_.t[o_, :, g_ * 512:(g_ + 1) * 512]))
                        with P_.step("dve", reads=[pP, slab], writes=[slab]) as st:
                            st.add(lambda e_, pP_=pP, sl_=slab: e_.tensor_add(
                                sl_[:, :], pP_[:, 0:512], sl_[:, :]))
                        with P_.step("sp", reads=[slab], writes=[hd1], dma=True) as st:
                            st.add(lambda e_, o_=oc, g_=g, sl_=slab: e_.dma_start(
                                out=hd1.t[o_, :, g_ * 512:(g_ + 1) * 512], in_=sl_[:, :]))

                    cfg = {
                        "heads": list(range(ps * 8, ps * 8 + 8)),
                        "qwidth": 512, "ngroups": 4, "nmask": 8,
                        "qslice": lambda h, g, p_=ps: qTa[(h % 2) * 64:(h % 2) * 64 + 64,
                                                          (h // 2) * T + g * 512:(h // 2) * T + g * 512 + 512],
                        "kslice": lambda h, c, p_=ps: Kh[(h % 2) * 64:(h % 2) * 64 + 64,
                                                         ((h - 8 * p_) // 2) * S + c * 128:((h - 8 * p_) // 2) * S + c * 128 + 128],
                        "vslice": lambda h, c, p_=ps: Vh[:, c * 520 + (h - 8 * p_) * 65: c * 520 + (h - 8 * p_) * 65 + 65],
                        "mask_load": mask_load_c,
                        "reads_qkv": [qTa, Kh, Vh],
                        "banks": bks,
                        "scratch": {"aA": aA, "aB": aB, "den": den, "rden": rden,
                                    "outT": outT, "rb": rb, "cst": cst},
                        "sel_lo": sel_lo, "sel_hi": sel_hi,
                        "wo_sb": wom_sb,
                        "emit_resid": emit_resid_stream,
                        "scale": 0.125,
                        "nch_of_g": lambda g: 8 * g + 8,
                        "chunk_of": lambda g, j: j,
                        "proj_chunks": [(lc, ps * 4 + lc) for lc in range(4)],
                    }
                    emit_attention_causal(P, cfg)
            _phase4()
            P.phase_barrier()
        if stop_after == "D":
            with P.step("sp", reads=[xr], dma=True) as st:
                st.add(lambda e_: e_.dma_start(out=xfo.rearrange("c p t -> p c t"),
                                               in_=xr[:, :].rearrange("p (c t) -> p c t", c=DC)))
            P.final_barrier()
            blk = e(nc.Block())
            P.emit_into_block(blk)
            return nc

        def _phase5():
            with ExitStack() as SC:
                es = SC.enter_context
                xn = Buf(es(nc.sbuf_tensor("E_xn", [128, DC * T], BF16)), "xn")
                w1A = Buf(es(nc.sbuf_tensor("E_w1A", [128, 8 * 1024], BF16)), "w1A")
                w1B = w1A
                w2A = Buf(es(nc.sbuf_tensor("E_w2A", [128, 8 * 1024], BF16)), "w2A")
                h1T = Buf(es(nc.sbuf_tensor("E_h1T", [128, 8 * 512], BF16)), "h1T")
                gw_sb = Buf(es(nc.sbuf_tensor("E_gw", [128, DC * 8], BF16)), "gw_sb")
                g2_sb = Buf(es(nc.sbuf_tensor("E_g2", [128, DC], F32)), "g2_sb")
                lg = Buf(es(nc.sbuf_tensor("E_lg", [128, 8], F32)), "lg")
                m1 = Buf(es(nc.sbuf_tensor("E_m1", [128, 1], F32)), "m1")
                m2 = Buf(es(nc.sbuf_tensor("E_m2", [128, 1], F32)), "m2")
                eqm = Buf(es(nc.sbuf_tensor("E_eqm", [128, 8], F32)), "eqm")
                mskd = Buf(es(nc.sbuf_tensor("E_mskd", [128, 8], F32)), "mskd")
                E8 = Buf(es(nc.sbuf_tensor("E_E8", [128, 8], F32)), "E8")
                mm12 = Buf(es(nc.sbuf_tensor("E_mm12", [128, 2], F32)), "mm12")
                rsum = Buf(es(nc.sbuf_tensor("E_rsum", [128, 1], F32)), "rsum")
                comb = Buf(es(nc.sbuf_tensor("E_comb", [128, NT * 8], BF16)), "comb")
                combT = Buf(es(nc.sbuf_tensor("E_combT", [128, T], BF16)), "combT")
                ybf = Buf(es(nc.sbuf_tensor("E_ybf", [128, 512], BF16)), "ybf")
                ybf2 = Buf(es(nc.sbuf_tensor("E_ybf2", [128, 512], BF16)), "ybf2")
                slab = Buf(es(nc.sbuf_tensor("E_slab", [128, 512], F32)), "slab")
                nsc = alloc_norm_scratch(nc, SC, P, banks=bks, with_xc=False)
                with P.step("sp", writes=[xr], dma=True) as st:
                    st.add(lambda e_: e_.dma_start(out=xr[:, :].rearrange("p (c t) -> p c t", c=DC),
                                                   in_=hd1.t.rearrange("c p t -> p c t")))
                xr.writer.val = xr.writer.val
                load_gain(g2_sb, g2i)
                load_w(gw_sb, gwi, DC, 8)
                emit_norm(P, SC, xr, xn, "rms", consts, 1e-6, gain=g2_sb, scratch=nsc)
                pG = bks["pF"]
                for i in range(NT):
                    with P.step("pe", reads=[xn, gw_sb], writes=[pG]) as st:
                        for ch in range(DC):
                            st.add(lambda e_, c=ch, ii=i: e_.matmul(
                                pG[:, 0:8], xn[:, c * T + ii * 128: c * T + ii * 128 + 128],
                                gw_sb[:, c * 8:(c + 1) * 8],
                                start=(c == 0), stop=(c == DC - 1)))
                    with P.step("dve", reads=[pG], writes=[lg]) as st:
                        st.add(lambda e_: e_.tensor_copy(lg[:, :], pG[:, 0:8]))
                    with P.step("dve", reads=[lg], writes=[m1]) as st:
                        st.add(lambda e_: e_.reduce_max(m1[:, :], lg[:, :], axis=AX.X))
                    with P.step("dve", reads=[lg, m1], writes=[eqm]) as st:
                        st.add(lambda e_: e_.tensor_scalar(eqm[:, :], lg[:, :], m1[:, :], None, ALU.is_ge))
                    with P.step("dve", reads=[lg, eqm], writes=[mskd]) as st:
                        st.add(lambda e_: e_.scalar_tensor_tensor(mskd[:, :], eqm[:, :], -1e30, lg[:, :],
                                                                  ALU.mult, ALU.add))
                    with P.step("dve", reads=[mskd], writes=[m2]) as st:
                        st.add(lambda e_: e_.reduce_max(m2[:, :], mskd[:, :], axis=AX.X))
                    with P.step("act", reads=[lg], writes=[E8]) as st:
                        st.add(lambda e_: e_.activation(E8[:, :], lg[:, :], AF.Exp))
                    with P.step("dve", reads=[m1], writes=[mm12]) as st:
                        st.add(lambda e_: e_.tensor_copy(mm12[:, 0:1], m1[:, :]))
                    with P.step("dve", reads=[m2], writes=[mm12]) as st:
                        st.add(lambda e_: e_.tensor_copy(mm12[:, 1:2], m2[:, :]))
                    with P.step("act", reads=[mm12], writes=[mm12]) as st:
                        st.add(lambda e_: e_.activation(mm12[:, :], mm12[:, :], AF.Exp))
                    with P.step("dve", reads=[mm12], writes=[rsum]) as st:
                        st.add(lambda e_: e_.reduce_sum(rsum[:, :], mm12[:, :], axis=AX.X))
                    with P.step("dve", reads=[rsum], writes=[rsum]) as st:
                        st.add(lambda e_: e_.reciprocal(rsum[:, :], rsum[:, :]))
                    with P.step("dve", reads=[lg, m2], writes=[eqm]) as st:
                        st.add(lambda e_: e_.tensor_scalar(eqm[:, :], lg[:, :], m2[:, :], None, ALU.is_ge))
                    with P.step("dve", reads=[E8, eqm], writes=[E8]) as st:
                        st.add(lambda e_: e_.tensor_mul(E8[:, :], E8[:, :], eqm[:, :]))
                    with P.step("dve", reads=[E8, rsum, comb], writes=[comb]) as st:
                        st.add(lambda e_, ii=i: e_.tensor_scalar(comb[:, ii * 8:(ii + 1) * 8],
                                                                 E8[:, :], rsum[:, :], None, ALU.mult))
                w1bufs = [w1A, w1B]

                def emit_w1load(idx):
                    ee, half = divmod(idx, 2)
                    wb = w1bufs[idx % 2]
                    with P.step("pool", writes=[wb], dma=True) as st:
                        st.add(lambda e_, E_=ee, H_=half, w_=wb: e_.dma_start(
                            out=w_[:, :].rearrange("p (c n) -> p c n", c=DC),
                            in_=w1d[E_, :, H_ * 1024:(H_ + 1) * 1024].rearrange("(c p) n -> p c n", c=DC)))

                for idx in range(16):
                    ee, half = divmod(idx, 2)
                    w1b = w1bufs[idx % 2]
                    emit_w1load(idx)
                    with P.step("pool", writes=[w2A], dma=True) as st:
                        st.add(lambda e_, E_=ee, H_=half: e_.dma_start(
                            out=w2A[:, :].rearrange("p (c n) -> p c n", c=DC),
                            in_=w2d[E_, H_ * 1024:(H_ + 1) * 1024, :].rearrange("(c p) n -> p c n", c=DC)))
                    if half == 0:
                        for i in range(NT):
                            with P.step("pe", reads=[comb, ident], writes=[pG]) as st:
                                st.add(lambda e_, ii=i, E_=ee: e_.matmul(
                                    pG[:, 0:128], comb[:, ii * 8 + E_: ii * 8 + E_ + 1].broadcast_to([128, 128]),
                                    ident[:, :], start=True, stop=True))
                            with P.step("dve", reads=[pG], writes=[combT]) as st:
                                st.add(lambda e_, ii=i: e_.tensor_copy(
                                    combT[:, ii * 128:(ii + 1) * 128], pG[:, 0:128]))
                    for tr in range(4):
                        for ec in range(8):
                            pQ = bks["pE"] if ec % 2 == 0 else bks["pF"]
                            with P.step("pe", reads=[xn, w1b], writes=[pQ]) as st:
                                for dc in range(DC):
                                    st.add(lambda e_, d=dc, E2=ec, t_=tr, wb=w1b, p_=pQ: e_.matmul(
                                        p_[:, :], wb[:, d * 1024 + E2 * 128: d * 1024 + E2 * 128 + 128],
                                        xn[:, d * T + t_ * 512: d * T + t_ * 512 + 512],
                                        start=(d == 0), stop=(d == DC - 1)))
                            with P.step("act", reads=[pQ], writes=[h1T]) as st:
                                st.add(lambda e_, E2=ec, p_=pQ: e_.activation(
                                    h1T[:, E2 * 512:(E2 + 1) * 512], p_[:, :], AF.Silu))
                        for oc in range(DC):
                            pY = bks["pC"] if oc % 2 == 0 else bks["pD"]
                            yb = ybf if oc % 2 == 0 else ybf2
                            with P.step("pe", reads=[h1T, w2A], writes=[pY]) as st:
                                for ec in range(8):
                                    st.add(lambda e_, E2=ec, o_=oc, p_=pY: e_.matmul(
                                        p_[:, :], w2A[:, E2 * 1024 + o_ * 128: E2 * 1024 + o_ * 128 + 128],
                                        h1T[:, E2 * 512:(E2 + 1) * 512],
                                        start=(E2 == 0), stop=(E2 == 7)))
                            with P.step("dve", reads=[pY], writes=[yb]) as st:
                                st.add(lambda e_, p_=pY, y_=yb: e_.tensor_copy(y_[:, :], p_[:, :]))
                            with P.step("pool", reads=[yb, combT], writes=[yb]) as st:
                                st.add(lambda e_, t_=tr, y_=yb: e_.tensor_mul(
                                    y_[:, :], y_[:, :], combT[:, t_ * 512:(t_ + 1) * 512]))
                            with P.step("pool", reads=[yb, xr], writes=[xr]) as st:
                                st.add(lambda e_, o_=oc, t_=tr, y_=yb: e_.tensor_add(
                                    xr[:, o_ * T + t_ * 512: o_ * T + t_ * 512 + 512], y_[:, :],
                                    xr[:, o_ * T + t_ * 512: o_ * T + t_ * 512 + 512]))
                for oc in range(DC):
                    for tr in range(4):
                        with P.step("sp", reads=[xd1], writes=[slab], dma=True) as st:
                            st.add(lambda e_, o_=oc, t_=tr: e_.dma_start(
                                out=slab[:, :], in_=xd1.t[o_, :, t_ * 512:(t_ + 1) * 512]))
                        with P.step("dve", reads=[xr, slab], writes=[xr]) as st:
                            st.add(lambda e_, o_=oc, t_=tr: e_.tensor_add(
                                xr[:, o_ * T + t_ * 512: o_ * T + t_ * 512 + 512],
                                xr[:, o_ * T + t_ * 512: o_ * T + t_ * 512 + 512], slab[:, :]))
                        with P.step("sp", reads=[xr], writes=[xd2], dma=True) as st:
                            st.add(lambda e_, o_=oc, t_=tr: e_.dma_start(
                                out=xd2.t[o_, :, t_ * 512:(t_ + 1) * 512],
                                in_=xr[:, o_ * T + t_ * 512: o_ * T + t_ * 512 + 512]))
        _phase5()
        P.phase_barrier()
        if stop_after == "E":
            with P.step("sp", reads=[xr], dma=True) as st:
                st.add(lambda e_: e_.dma_start(out=xfo.rearrange("c p t -> p c t"),
                                               in_=xr[:, :].rearrange("p (c t) -> p c t", c=DC)))
            P.final_barrier()
            blk = e(nc.Block())
            P.emit_into_block(blk)
            return nc

        def _phase6():
            with ExitStack() as SC:
                es = SC.enter_context
                xn = Buf(es(nc.sbuf_tensor("F_xn", [128, DC * T], BF16)), "xn")
                wq_sb = Buf(es(nc.sbuf_tensor("F_wq", [128, DC * 1024], BF16)), "wq_sb")
                wkv_sb = Buf(es(nc.sbuf_tensor("F_wkv", [128, DC * 256], BF16)), "wkv_sb")
                nw_sb = Buf(es(nc.sbuf_tensor("F_nw", [128, DC], F32)), "nw_sb")
                q_tmP = [Buf(es(nc.sbuf_tensor(f"F_qtm{x}", [128, 1024], BF16)), "q_tm") for x in range(2)]
                lat_sbP = [Buf(es(nc.sbuf_tensor(f"F_lat{x}", [128, 512], BF16)), "lat_sb") for x in range(2)]
                rscP = [{"r1": Buf(es(nc.sbuf_tensor(f"F_r1{x}", [128, 1024], BF16)), "r1"),
                         "r2": Buf(es(nc.sbuf_tensor(f"F_r2{x}", [128, 1024], BF16)), "r2"),
                         "ss": Buf(es(nc.sbuf_tensor(f"F_ss{x}", [128, 16], F32)), "ss"),
                         "cst": cst} for x in range(2)]
                tabC = Buf(es(nc.sbuf_tensor("F_tC", [128, 1024], BF16)), "tabC")
                tabS = Buf(es(nc.sbuf_tensor("F_tS", [128, 1024], BF16)), "tabS")
                nsc = alloc_norm_scratch(nc, SC, P, banks=bks, with_xc=True)
                load_tabs(tabC, tabS, Crgi, Srgi)
                load_w(wq_sb, gwq, DC, 1024)
                load_w(wkv_sb, gkvd, DC, 256)
                load_gain(nw_sb, n1wi)
                emit_norm(P, SC, xr, xn, "ln32", consts, 1e-5, scratch=nsc)
                emit_norm(P, SC, xr, xn, "rms", consts, 1e-6, gain=nw_sb, scratch=nsc)
                for i in range(NT):
                    q_tm, rsc = q_tmP[i % 2], rscP[i % 2]
                    for nr in range(2):
                        pQ = bks["pE"] if nr % 2 == 0 else bks["pF"]
                        with P.step("pe", reads=[xn, wq_sb], writes=[pQ]) as st:
                            for ch in range(DC):
                                st.add(lambda e_, c=ch, n=nr, ii=i, p_=pQ: e_.matmul(
                                    p_[:, :], xn[:, c * T + ii * 128: c * T + ii * 128 + 128],
                                    wq_sb[:, c * 1024 + n * 512: c * 1024 + n * 512 + 512],
                                    start=(c == 0), stop=(c == DC - 1)))
                        emit_qperm_evac(P, pQ, q_tm, nr)
                    emit_rope(P, q_tm, tabC, tabS, i * 64, 16, "inter", rsc)
                    emit_qknorm(P, q_tm, 16, rsc)
                    q_transpose_out(P, q_tm, i)
                for lc in range(2):
                    for r in range(4):
                        pr_ = (lc * 4 + r) % 2
                        pQ = bks["pE"] if pr_ == 0 else bks["pF"]
                        lat_sb = lat_sbP[pr_]
                        with P.step("pe", reads=[xn, wkv_sb], writes=[pQ]) as st:
                            for ch in range(DC):
                                st.add(lambda e_, c=ch, l=lc, r_=r, p_=pQ: e_.matmul(
                                    p_[:, :], wkv_sb[:, c * 256 + l * 128: c * 256 + l * 128 + 128],
                                    xn[:, c * T + r_ * 512: c * T + r_ * 512 + 512],
                                    start=(c == 0), stop=(c == DC - 1)))
                        with P.step("dve", reads=[pQ], writes=[lat_sb]) as st:
                            st.add(lambda e_, p_=pQ, l_=lat_sb: e_.tensor_copy(l_[:, :], p_[:, :]))
                        with P.step("sp", reads=[lat_sb], writes=[ccGi], dma=True) as st:
                            st.add(lambda e_, l=lc, r_=r, l_=lat_sb: e_.dma_start(
                                out=ccGi.t[l * 128:(l + 1) * 128, r_ * 512:(r_ + 1) * 512],
                                in_=l_[:, :]))
                with P.step("pool", reads=[ccGi], writes=[ccGo], collective_sem=ccsems[2]) as st:
                    st.add(lambda e_: e_.collective_compute(
                        "AllGather", ALU.bypass, replica_groups=PAIRS,
                        ins=[ccGi.t], outs=[ccGo.t]))
        _phase6()
        P.phase_barrier()
        if stop_after == "F":
            with P.step("sp", reads=[xr], dma=True) as st:
                st.add(lambda e_: e_.dma_start(out=xfo.rearrange("c p t -> p c t"),
                                               in_=xr[:, :].rearrange("p (c t) -> p c t", c=DC)))
            P.final_barrier()
            blk = e(nc.Block())
            P.emit_into_block(blk)
            return nc

        def _phase7():
            with ExitStack() as SC:
                es = SC.enter_context
                Kfm = Buf(es(nc.sbuf_tensor("G_K", [128, 2 * S], BF16)), "Kfm")
                Vp = Buf(es(nc.sbuf_tensor("G_V", [128, KC * 260], BF16)), "Vp")
                latgf = Buf(es(nc.sbuf_tensor("G_lat", [128, 2 * S], BF16)), "latgf")
                kup_sb = Buf(es(nc.sbuf_tensor("G_kup", [128, 2 * 256], BF16)), "kup_sb")
                vup_sb = Buf(es(nc.sbuf_tensor("G_vup", [128, 2 * 256], BF16)), "vup_sb")
                wo_sb = Buf(es(nc.sbuf_tensor("G_wo", [128, DC * 1024], BF16)), "wo_sb")
                k_tmP = [Buf(es(nc.sbuf_tensor(f"G_ktm{x}", [128, 256], BF16)), "k_tm") for x in range(2)]
                rscP = [{"r1": Buf(es(nc.sbuf_tensor(f"G_r1{x}", [128, 256], BF16)), "r1"),
                         "r2": Buf(es(nc.sbuf_tensor(f"G_r2{x}", [128, 256], BF16)), "r2"),
                         "ss": Buf(es(nc.sbuf_tensor(f"G_ss{x}", [128, 16], F32)), "ss"),
                         "cst": cst} for x in range(2)]
                aP4 = [Buf(es(nc.sbuf_tensor(f"G_a{x}", [128, 1024], BF16)), "a") for x in range(2)]
                aA, aB = aP4[0], aP4[1]
                den = Buf(es(nc.sbuf_tensor("G_den", [1, 1024], F32)), "den")
                den2 = Buf(es(nc.sbuf_tensor("G_den2", [1, 1024], F32)), "den2")
                rden = Buf(es(nc.sbuf_tensor("G_rden", [1, 1024], BF16)), "rden")
                rden2 = Buf(es(nc.sbuf_tensor("G_rden2", [1, 1024], BF16)), "rden2")
                outT = Buf(es(nc.sbuf_tensor("G_oT", [128, 4096], BF16)), "outT")
                rb = Buf(es(nc.sbuf_tensor("G_rb", [128, 512], BF16)), "rb")
                mstA = Buf(es(nc.sbuf_tensor("G_mA", [128, 4096], BF16)), "mstA")
                mstB = mstA

                def mask_load_c(P_, g):
                    m = mstA if g % 2 == 0 else mstB
                    with P_.step("sp", writes=[m], dma=True) as st:
                        st.add(lambda e_, g_=g, m_=m: e_.dma_start(
                            out=m_[:, :].rearrange("p (j q) -> p j q", j=8),
                            in_=mci[g_].rearrange("j p q -> p j q")))
                    return m
                tabC = Buf(es(nc.sbuf_tensor("G_tC", [128, 2048], BF16)), "tabC")
                tabS = Buf(es(nc.sbuf_tensor("G_tS", [128, 2048], BF16)), "tabS")
                load_tabs(tabC, tabS, Crgfi, Srgfi, n=KC)
                with P.step("sp", reads=[ccGo], writes=[latgf], dma=True) as st:
                    for pr in range(2):
                        for lc in range(2):
                            st.add(lambda e_, p_=pr, l=lc: e_.dma_start(
                                out=bass.AP(latgf.t, l * S + p_ * 128, [[2 * S, 128], [256, NT], [1, 128]]),
                                in_=ccGo.t[p_ * 256 + l * 128: p_ * 256 + l * 128 + 128, :]
                                .rearrange("p (i q) -> p i q", i=NT)))
                load_w(kup_sb, gkup, 2, 256)
                load_w(vup_sb, gvup, 2, 256)
                emit_wo_perm_load(P, None, wo_sb, gwo)
                for c in range(KC):
                    k_tm, rsc = k_tmP[c % 2], rscP[c % 2]
                    pK = bks["pE"] if c % 2 == 0 else bks["pF"]
                    pV = bks["pC"] if c % 2 == 0 else bks["pD"]
                    pX = pT if c % 2 == 0 else pT2
                    with P.step("pe", reads=[latgf, kup_sb], writes=[pK]) as st:
                        for lc in range(2):
                            st.add(lambda e_, l=lc, c_=c, p_=pK: e_.matmul(
                                p_[:, 0:256], latgf[:, l * S + c_ * 128: l * S + c_ * 128 + 128],
                                kup_sb[:, l * 256:(l + 1) * 256],
                                start=(l == 0), stop=(l == 1)))
                    with P.step("dve", reads=[pK], writes=[k_tm]) as st:
                        st.add(lambda e_, p_=pK, k_=k_tm: e_.tensor_copy(k_[:, :], p_[:, 0:256]))
                    emit_rope(P, k_tm, tabC, tabS, c * 64, 4, "inter", rsc)
                    emit_qknorm(P, k_tm, 4, rsc)
                    for half in range(2):
                        with P.step("pe", reads=[k_tm, ident], writes=[pX]) as st:
                            st.add(lambda e_, h_=half, k_=k_tm, p_=pX: e_.transpose(
                                p_[:, 0:128], k_[:, h_ * 128:(h_ + 1) * 128], ident[:, :]))
                        with P.step("dve", reads=[pX], writes=[Kfm]) as st:
                            st.add(lambda e_, h_=half, c_=c, p_=pX: e_.tensor_copy(
                                Kfm[:, h_ * S + c_ * 128:h_ * S + c_ * 128 + 128], p_[:, 0:128]))
                    with P.step("pe", reads=[latgf, vup_sb], writes=[pV]) as st:
                        for lc in range(2):
                            st.add(lambda e_, l=lc, c_=c, p_=pV: e_.matmul(
                                p_[:, 0:256], latgf[:, l * S + c_ * 128: l * S + c_ * 128 + 128],
                                vup_sb[:, l * 256:(l + 1) * 256],
                                start=(l == 0), stop=(l == 1)))
                    with P.step("dve", reads=[pV], writes=[Vp]) as st:
                        st.add(lambda e_, c_=c, p_=pV: e_.tensor_copy(
                            bass.AP(Vp.t, c_ * 260, [[KC * 260, 128], [65, 4], [1, 64]]),
                            p_[:, 0:256].rearrange("p (k f) -> p k f", k=4)))
                with P.step("dve", writes=[Vp]) as st:
                    st.add(lambda e_: e_.memset(
                        bass.AP(Vp.t, 64, [[KC * 260, 128], [260, KC], [65, 4], [1, 1]]), 1.0))
                cfg = {
                    "heads": list(L1_HEADS),
                    "qwidth": 512, "ngroups": 4, "nmask": 8,
                    "qslice": lambda h, g: qTa[(gqa_pos(h) % 2) * 64:(gqa_pos(h) % 2) * 64 + 64,
                                               (gqa_pos(h) // 2) * T + g * 512:(gqa_pos(h) // 2) * T + g * 512 + 512],
                    "kslice": lambda h, c: Kfm[((h // 4) % 2) * 64:((h // 4) % 2) * 64 + 64,
                                               ((h // 4) // 2) * S + c * 128:((h // 4) // 2) * S + c * 128 + 128],
                    "vslice": lambda h, c: Vp[:, c * 260 + (h // 4) * 65: c * 260 + (h // 4) * 65 + 65],
                    "mask_load": mask_load_c,
                    "reads_qkv": [qTa, Kfm, Vp],
                    "banks": bks,
                    "scratch": {"aA": aA, "aB": aB, "den": den, "den2": den2, "rden": rden,
                                "rden2": rden2, "outT": outT, "rb": rb, "cst": cst},
                    "sel_lo": sel_lo, "sel_hi": sel_hi,
                    "wo_sb": wo_sb,
                    "emit_resid": (lambda P_, oc, g, pP: _resid_add(P_, xr, oc, g, pP, w=512)),
                    "scale": 0.125,
                    "nch_of_g": lambda g: 8 * g + 8,
                    "chunk_of": lambda g, j: j,
                }
                emit_attention_causal(P, cfg)
        _phase7()
        P.phase_barrier()
        if stop_after == "G":
            with P.step("sp", reads=[xr], dma=True) as st:
                st.add(lambda e_: e_.dma_start(out=xfo.rearrange("c p t -> p c t"),
                                               in_=xr[:, :].rearrange("p (c t) -> p c t", c=DC)))
            P.final_barrier()
            blk = e(nc.Block())
            P.emit_into_block(blk)
            return nc

        def _phase8():
            with ExitStack() as SC:
                es = SC.enter_context
                xn = Buf(es(nc.sbuf_tensor("H_xn", [128, DC * T], BF16)), "xn")
                wq_sb = Buf(es(nc.sbuf_tensor("H_wq", [128, DC * 1024], BF16)), "wq_sb")
                wkv_sb = Buf(es(nc.sbuf_tensor("H_wkv", [128, DC * 512], BF16)), "wkv_sb")
                nw_sb = Buf(es(nc.sbuf_tensor("H_nw", [128, DC], F32)), "nw_sb")
                q_tmP = [Buf(es(nc.sbuf_tensor(f"H_qtm{x}", [128, 1024], BF16)), "q_tm") for x in range(2)]
                k_tmP = [Buf(es(nc.sbuf_tensor(f"H_ktm{x}", [128, 256], BF16)), "k_tm") for x in range(2)]
                v_tmP = [Buf(es(nc.sbuf_tensor(f"H_vtm{x}", [128, 256], BF16)), "v_tm") for x in range(2)]
                rscP = [{"r1": Buf(es(nc.sbuf_tensor(f"H_r1{x}", [128, 1024], BF16)), "r1"),
                         "r2": Buf(es(nc.sbuf_tensor(f"H_r2{x}", [128, 1024], BF16)), "r2"),
                         "ss": Buf(es(nc.sbuf_tensor(f"H_ss{x}", [128, 16], F32)), "ss"),
                         "cst": cst} for x in range(2)]
                tabC = Buf(es(nc.sbuf_tensor("H_tC", [128, 1024], BF16)), "tabC")
                tabS = Buf(es(nc.sbuf_tensor("H_tS", [128, 1024], BF16)), "tabS")
                nsc = alloc_norm_scratch(nc, SC, P, banks=bks, with_xc=False)
                load_tabs(tabC, tabS, Crli, Srli)
                load_w(wq_sb, lwq, DC, 1024)
                with P.step("pool", writes=[wkv_sb], dma=True) as st:
                    st.add(lambda e_: e_.dma_start(
                        out=bass.AP(wkv_sb.t, 0, [[DC * 512, 128], [512, DC], [1, 256]]),
                        in_=lwk.rearrange("(c p) n -> p c n", c=DC)))
                    st.add(lambda e_: e_.dma_start(
                        out=bass.AP(wkv_sb.t, 256, [[DC * 512, 128], [512, DC], [1, 256]]),
                        in_=lwv.rearrange("(c p) n -> p c n", c=DC)))
                load_gain(nw_sb, n2wi)
                emit_norm(P, SC, xr, xn, "rms", consts, 1e-6, gain=nw_sb, scratch=nsc)
                for i in range(NT):
                    q_tm, k_tm, v_tm, rsc = q_tmP[i % 2], k_tmP[i % 2], v_tmP[i % 2], rscP[i % 2]
                    for nr in range(3):
                        pQ = bks["pE"] if nr % 2 == 0 else bks["pF"]
                        with P.step("pe", reads=[xn, wq_sb, wkv_sb], writes=[pQ]) as st:
                            for ch in range(DC):
                                if nr < 2:
                                    w_ap = wq_sb[:, ch * 1024 + nr * 512: ch * 1024 + nr * 512 + 512]
                                else:
                                    w_ap = wkv_sb[:, ch * 512: ch * 512 + 512]
                                st.add(lambda e_, c=ch, w=w_ap, ii=i, p_=pQ: e_.matmul(
                                    p_[:, :], xn[:, c * T + ii * 128: c * T + ii * 128 + 128], w,
                                    start=(c == 0), stop=(c == DC - 1)))
                        if nr < 2:
                            emit_qperm_evac(P, pQ, q_tm, nr)
                        else:
                            with P.step("dve", reads=[pQ], writes=[k_tm, v_tm]) as st:
                                st.add(lambda e_, p_=pQ, k_=k_tm: e_.tensor_copy(k_[:, :], p_[:, 0:256]))
                                st.add(lambda e_, p_=pQ, v_=v_tm: e_.tensor_copy(v_[:, :], p_[:, 256:512]))
                    emit_rope(P, q_tm, tabC, tabS, i * 64, 16, "inter", rsc)
                    emit_qknorm(P, q_tm, 16, rsc)
                    emit_rope(P, k_tm, tabC, tabS, i * 64, 4, "inter", rsc)
                    emit_qknorm(P, k_tm, 4, rsc)
                    q_transpose_out(P, q_tm, i)
                    with P.step("sp", reads=[k_tm, v_tm], writes=[ccLi], dma=True) as st:
                        st.add(lambda e_, ii=i, k_=k_tm: e_.dma_start(
                            out=ccLi.t[ii * 128:(ii + 1) * 128, 0:256], in_=k_[:, :]))
                        st.add(lambda e_, ii=i, v_=v_tm: e_.dma_start(
                            out=ccLi.t[ii * 128:(ii + 1) * 128, 256:512], in_=v_[:, :]))
                with P.step("pool", reads=[ccLi], writes=[ccLo], collective_sem=ccsems[3]) as st:
                    st.add(lambda e_: e_.collective_compute(
                        "AllGather", ALU.bypass, replica_groups=PAIRS,
                        ins=[ccLi.t], outs=[ccLo.t]))
        _phase8()
        P.phase_barrier()
        if stop_after == "H":
            with P.step("sp", reads=[xr], dma=True) as st:
                st.add(lambda e_: e_.dma_start(out=xfo.rearrange("c p t -> p c t"),
                                               in_=xr[:, :].rearrange("p (c t) -> p c t", c=DC)))
            P.final_barrier()
            blk = e(nc.Block())
            P.emit_into_block(blk)
            return nc

        def _phase9():
            with ExitStack() as SC:
                es = SC.enter_context
                Kfm = Buf(es(nc.sbuf_tensor("I_K", [128, 2 * S], BF16)), "Kfm")
                Vp = Buf(es(nc.sbuf_tensor("I_V", [128, KC * 260], BF16)), "Vp")
                wo_sb = Buf(es(nc.sbuf_tensor("I_wo", [128, DC * 1024], BF16)), "wo_sb")
                ktst = Buf(es(nc.sbuf_tensor("I_kt", [128, 256], BF16)), "ktst")
                aP4 = [Buf(es(nc.sbuf_tensor(f"I_a{x}", [128, 512], BF16)), "a") for x in range(2)]
                aA, aB = aP4[0], aP4[1]
                den = Buf(es(nc.sbuf_tensor("I_den", [1, 512], F32)), "den")
                den2 = Buf(es(nc.sbuf_tensor("I_den2", [1, 512], F32)), "den2")
                rden = Buf(es(nc.sbuf_tensor("I_rden", [1, 512], BF16)), "rden")
                rden2 = Buf(es(nc.sbuf_tensor("I_rden2", [1, 512], BF16)), "rden2")
                outT = Buf(es(nc.sbuf_tensor("I_oT", [128, 1024], BF16)), "outT")
                rb = Buf(es(nc.sbuf_tensor("I_rb", [128, 256], BF16)), "rb")
                xnr = Buf(es(nc.sbuf_tensor("I_xnr", [128, DC * 512], BF16)), "xnr")
                mstA = Buf(es(nc.sbuf_tensor("I_mA", [128, 512], BF16)), "mstA")
                mstB = Buf(es(nc.sbuf_tensor("I_mB", [128, 512], BF16)), "mstB")

                def mask_load_l(P_, g):
                    m = mstA if g % 2 == 0 else mstB
                    with P_.step("sp", writes=[m], dma=True) as st:
                        st.add(lambda e_, g_=g, m_=m: e_.dma_start(
                            out=m_[:, 0:512].rearrange("p (j q) -> p j q", j=4),
                            in_=mlii[g_].rearrange("j p q -> p j q")))
                    return m
                slab = Buf(es(nc.sbuf_tensor("I_slab", [128, 512], F32)), "slab")
                nsc = alloc_norm_scratch(nc, SC, P, banks=bks, with_xc=True)
                emit_wo_perm_load(P, None, wo_sb, lwo)
                for pr in range(2):
                    for i in range(NT):
                        t = 2 * i + pr
                        with P.step("sp", reads=[ccLo], writes=[ktst], dma=True) as st:
                            st.add(lambda e_, p_=pr, ii=i: e_.dma_start(
                                out=ktst[:, :], in_=ccLo.t[p_ * T + ii * 128: p_ * T + ii * 128 + 128, 0:256]))
                        for half in range(2):
                            with P.step("pe", reads=[ktst, ident], writes=[pT]) as st:
                                st.add(lambda e_, h_=half: e_.transpose(
                                    pT[:, 0:128], ktst[:, h_ * 128:(h_ + 1) * 128], ident[:, :]))
                            with P.step("dve", reads=[pT], writes=[Kfm]) as st:
                                st.add(lambda e_, h_=half, t_=t: e_.tensor_copy(
                                    Kfm[:, h_ * S + t_ * 128:h_ * S + t_ * 128 + 128], pT[:, 0:128]))
                    for kv in range(4):
                        with P.step("sp", reads=[ccLo], writes=[Vp], dma=True) as st:
                            st.add(lambda e_, p_=pr, k_=kv: e_.dma_start(
                                out=bass.AP(Vp.t, p_ * 260 + k_ * 65, [[KC * 260, 128], [2 * 260, NT], [1, 64]]),
                                in_=ccLo.t[p_ * T:(p_ + 1) * T, 256 + k_ * 64: 256 + (k_ + 1) * 64]
                                .rearrange("(i p) f -> p i f", p=128)))
                with P.step("dve", writes=[Vp]) as st:
                    st.add(lambda e_: e_.memset(
                        bass.AP(Vp.t, 64, [[KC * 260, 128], [260, KC], [65, 4], [1, 1]]), 1.0))
                cfg = {
                    "heads": list(L1_HEADS),
                    "qwidth": 128,
                    "ngroups": NT, "nmask": 4,
                    "qslice": lambda h, g: qTa[(gqa_pos(h) % 2) * 64:(gqa_pos(h) % 2) * 64 + 64,
                                               (gqa_pos(h) // 2) * T + g * 128:(gqa_pos(h) // 2) * T + g * 128 + 128],
                    "kslice": lambda h, c: Kfm[((h // 4) % 2) * 64:((h // 4) % 2) * 64 + 64,
                                               ((h // 4) // 2) * S + c * 128:((h // 4) // 2) * S + c * 128 + 128],
                    "vslice": lambda h, c: Vp[:, c * 260 + (h // 4) * 65: c * 260 + (h // 4) * 65 + 65],
                    "mask_load": mask_load_l,
                    "reads_qkv": [qTa, Kfm, Vp],
                    "banks": bks,
                    "scratch": {"aA": aA, "aB": aB, "den": den, "den2": den2, "rden": rden,
                                "rden2": rden2, "outT": outT, "rb": rb, "cst": cst},
                    "sel_lo": sel_lo, "sel_hi": sel_hi,
                    "wo_sb": wo_sb,
                    "emit_resid": (lambda P_, oc, g, pP: _resid_add(P_, xr, oc, g, pP, w=128)),
                    "scale": 0.125,
                    "nch_of_g": lambda g: 4,
                    "chunk_of": lambda g, j: max(0, 2 * g - 2 + j),
                }
                emit_attention_causal(P, cfg)
                for oc in range(DC):
                    for tr in range(4):
                        with P.step("sp", reads=[xd2], writes=[slab], dma=True) as st:
                            st.add(lambda e_, o_=oc, t_=tr: e_.dma_start(
                                out=slab[:, :], in_=xd2.t[o_, :, t_ * 512:(t_ + 1) * 512]))
                        with P.step("dve", reads=[xr, slab], writes=[xr]) as st:
                            st.add(lambda e_, o_=oc, t_=tr: e_.tensor_add(
                                xr[:, o_ * T + t_ * 512: o_ * T + t_ * 512 + 512],
                                xr[:, o_ * T + t_ * 512: o_ * T + t_ * 512 + 512], slab[:, :]))
                emit_norm(P, SC, xr, None, "ln_stats", consts, 1e-5, scratch=nsc)
                w4h = Buf(qTa.t, "w4h")
                w4h.writer = qTa.writer
                w4h.readers = qTa.readers
                with P.step("pool", reads=[qTa], writes=[qTa], dma=True) as st:
                    st.add(lambda e_: e_.dma_start(
                        out=qTa[:, 0:DC * 1024].rearrange("p (c n) -> p c n", c=DC),
                        in_=w4i.rearrange("(c p) n -> p c n", c=DC)))
                for ch in range(DC):
                    with P.step("act", reads=[qTa], writes=[qTa]) as st:
                        st.add(lambda e_, c=ch: e_.activation(
                            qTa[:, c * 1024:(c + 1) * 1024], qTa[:, c * 1024:(c + 1) * 1024], AF.Sign))
                for oc in range(DC):
                    for ic in range(DC):
                        with P.step("pe", reads=[qTa, ident], writes=[pT]) as st:
                            st.add(lambda e_, o_=oc, i_=ic: e_.transpose(
                                pT[:, 0:128], qTa[:, o_ * 1024 + i_ * 128: o_ * 1024 + i_ * 128 + 128],
                                ident[:, :]))
                        with P.step("dve", reads=[pT], writes=[qTa]) as st:
                            st.add(lambda e_, o_=oc, i_=ic: e_.tensor_copy(
                                qTa[:, 8192 + i_ * 1024 + o_ * 128: 8192 + i_ * 1024 + o_ * 128 + 128],
                                pT[:, 0:128]))
                pA_, pB_ = bks["pA"], bks["pB"]
                pB1, pB2 = nsc["pB1"], nsc["pB2"]
                srstd, smr, rbn, mrbn = nsc["srstd"], nsc["smr"], nsc["rb"], nsc["mrb"]
                for r in range(4):
                    with P.step("pe", reads=[srstd, onr], writes=[pB1]) as st:
                        st.add(lambda e_, r_=r: e_.matmul(pB1[:, :], onr[0:1, :],
                                                          srstd[0:1, r_ * 512:(r_ + 1) * 512],
                                                          start=True, stop=True))
                    with P.step("dve", reads=[pB1], writes=[rbn]) as st:
                        st.add(lambda e_: e_.tensor_copy(rbn[:, :], pB1[:, :]))
                    with P.step("pe", reads=[smr, onr], writes=[pB2]) as st:
                        st.add(lambda e_, r_=r: e_.matmul(pB2[:, :], onr[0:1, :],
                                                          smr[0:1, r_ * 512:(r_ + 1) * 512],
                                                          start=True, stop=True))
                    with P.step("dve", reads=[pB2], writes=[mrbn]) as st:
                        st.add(lambda e_: e_.tensor_copy(mrbn[:, :], pB2[:, :]))
                    for ic in range(DC):
                        with P.step("dve", reads=[xr, rbn], writes=[xnr]) as st:
                            st.add(lambda e_, i_=ic, r_=r: e_.tensor_mul(
                                xnr[:, i_ * 512:(i_ + 1) * 512],
                                xr[:, i_ * T + r_ * 512: i_ * T + r_ * 512 + 512], rbn[:, :]))
                        with P.step("dve", reads=[xnr, mrbn], writes=[xnr]) as st:
                            st.add(lambda e_, i_=ic: e_.tensor_sub(
                                xnr[:, i_ * 512:(i_ + 1) * 512], xnr[:, i_ * 512:(i_ + 1) * 512], mrbn[:, :]))
                    for oc in range(DC):
                        pP = pA_ if oc % 2 == 0 else pB_
                        with P.step("pe", reads=[xnr, qTa], writes=[pP]) as st:
                            for ic in range(DC):
                                st.add(lambda e_, i_=ic, o_=oc, pP_=pP: e_.matmul(
                                    pP_[:, :], qTa[:, 8192 + i_ * 1024 + o_ * 128: 8192 + i_ * 1024 + o_ * 128 + 128],
                                    xnr[:, i_ * 512:(i_ + 1) * 512],
                                    start=(i_ == 0), stop=(i_ == DC - 1)))
                        with P.step("dve", reads=[pP, xr], writes=[xr]) as st:
                            st.add(lambda e_, o_=oc, r_=r, pP_=pP: e_.tensor_add(
                                xr[:, o_ * T + r_ * 512: o_ * T + r_ * 512 + 512], pP_[:, :],
                                xr[:, o_ * T + r_ * 512: o_ * T + r_ * 512 + 512]))
                with P.step("sp", reads=[xr], dma=True) as st:
                    st.add(lambda e_: e_.dma_start(out=xfo.rearrange("c p t -> p c t"),
                                                   in_=xr[:, :].rearrange("p (c t) -> p c t", c=DC)))
        _phase9()
        P.final_barrier()
        blk = e(nc.Block())
        P.emit_into_block(blk)
    return nc


def fused_in_maps(x0, W, CST, onc, onr):
    maps = []
    for c in range(8):
        b, p = c // 2, c % 2
        K = CST[p]
        maps.append(dict(
            xin=shard_x(x0, b, p),
            wq0=W["wq"], wk0=W["wk"], wv0=W["wv"], wo0=W["wo"],
            g1i=W["g1"], wqm=W["mWq"], wdkv=W["mWdkv"], wuk=W["mWuk"],
            wuv=W["mWuv"], wom=W["mWo"], g2i=W["g2"], gwi=W["gate"],
            w1d=W["w1"], w2d=W["w2"], n1wi=W["n1w"], n2wi=W["n2w"],
            gwq=W["gwq"], gkvd=W["gkvd"], gkup=W["gkup"], gvup=W["gvup"],
            gwo=W["gwo"], lwq=W["lwq"], lwk=W["lwk"], lwv=W["lwv"],
            lwo=W["lwo"], w4i=W["w4"],
            Cr0i=K["Cr0"], Sr0i=K["Sr0"], Crgi=K["Crg"], Srgi=K["Srg"],
            Crli=K["Crl"], Srli=K["Srl"], Crgfi=K["Crgf"], Srgfi=K["Srgf"],
            mci=K["mask_c"], mlii=K["mask_l"], idi=K["ident"],
            onci=onc, onri=onr, sli=K["sel_lo"], shi=K["sel_hi"],
        ))
    return maps


def kernel_fused(L0_attn_x, L0_attn_wq, L0_attn_wk, L0_attn_wv, L0_attn_wo,
                 L0_ffn_g1, L0_ffn_Wq, L0_ffn_Wdkv, L0_ffn_Wuk, L0_ffn_Wuv,
                 L0_ffn_Wo, L0_ffn_g2, L0_ffn_gate_w, L0_ffn_w1, L0_ffn_w2,
                 L1_attn_norm1_w, L1_attn_norm2_w, L1_attn_g_wq, L1_attn_g_kv_down,
                 L1_attn_g_k_up, L1_attn_g_v_up, L1_attn_g_wo, L1_attn_l_wq,
                 L1_attn_l_wk, L1_attn_l_wv, L1_attn_l_wo, L1_ffn_weight):
    f32 = np.float32
    x0 = np.ascontiguousarray(np.asarray(L0_attn_x, f32))
    W = {k: np.ascontiguousarray(np.asarray(v, f32)) for k, v in dict(
        wq=L0_attn_wq, wk=L0_attn_wk, wv=L0_attn_wv, wo=L0_attn_wo,
        g1=L0_ffn_g1, mWq=L0_ffn_Wq, mWdkv=L0_ffn_Wdkv, mWuk=L0_ffn_Wuk,
        mWuv=L0_ffn_Wuv, mWo=L0_ffn_Wo, g2=L0_ffn_g2, gate=L0_ffn_gate_w,
        w1=L0_ffn_w1, w2=L0_ffn_w2, n1w=L1_attn_norm1_w, n2w=L1_attn_norm2_w,
        gwq=L1_attn_g_wq, gkvd=L1_attn_g_kv_down, gkup=L1_attn_g_k_up,
        gvup=L1_attn_g_v_up, gwo=L1_attn_g_wo, lwq=L1_attn_l_wq,
        lwk=L1_attn_l_wk, lwv=L1_attn_l_wv, lwo=L1_attn_l_wo,
        w4=L1_ffn_weight).items()}
    onc, onr = norm_consts()
    CST = {p: core_constants(p) for p in (0, 1)}
    nc = build_fused()
    res = run_bass_kernel_spmd(nc, fused_in_maps(x0, W, CST, onc, onr),
                               core_ids=list(range(8)))
    outs = [np.asarray(res.results[c]["xfo"]).reshape(DC, 128, T) for c in range(8)]
    return unshard_x(outs)


kernel = kernel_fused


# version 16
# speedup vs baseline: 1.0477x; 1.0477x over previous
import math
import numpy as np
import ml_dtypes
import concourse.bass as bass
import concourse.mybir as mybir
from concourse.bass_utils import run_bass_kernel_spmd

F32 = mybir.dt.float32
BF16 = mybir.dt.bfloat16
AF = mybir.ActivationFunctionType
ALU = mybir.AluOpType
AX = mybir.AxisListType
bfdt = ml_dtypes.bfloat16

B, S, D = 4, 4096, 1024
NT = 16
T = 2048
DC = 8
KC = 32
NEG = -1.0e9

H0 = [0, 1, 2, 3, 8, 9, 10, 11]
H64 = [4, 5, 6, 7, 12, 13, 14, 15]


def gqa_pos(h):
    if h in H0:
        return 2 * H0.index(h)
    return 2 * H64.index(h) + 1



def rope_tables(positions, hd, base, conv):
    pos = np.asarray(positions, np.float64)[:, None]
    half = hd // 2
    inv = 1.0 / (base ** (np.arange(half, dtype=np.float64) / half))
    ang = pos * inv[None, :]
    c, s = np.cos(ang), np.sin(ang)
    C = np.zeros((len(positions), hd), np.float32)
    Sg = np.zeros((len(positions), hd), np.float32)
    if conv == "half":
        C[:, :half] = c
        C[:, half:] = c
        Sg[:, :half] = -s
        Sg[:, half:] = s
    else:
        C[:, 0::2] = c
        C[:, 1::2] = c
        Sg[:, 0::2] = -s
        Sg[:, 1::2] = s
    return C, Sg


def core_constants(p):
    out = {}
    qpos = np.concatenate([np.arange(128) + 128 * (2 * i + p) for i in range(NT)])
    mc = np.zeros((4, 8, 128, 512), np.float32)
    for g in range(4):
        qp = qpos[g * 512:(g + 1) * 512]
        for j in range(8):
            kp = np.arange(128) + 128 * (8 * g + j)
            mc[g, j] = np.where(kp[:, None] <= qp[None, :], 1.0, 0.0)
    out["mask_c"] = mc.astype(bfdt)
    WIN = 256
    ml = np.zeros((NT, 4, 128, 128), np.float32)
    for i in range(NT):
        t = 2 * i + p
        qp = qpos[i * 128:(i + 1) * 128]
        for j in range(4):
            c = max(0, 2 * i - 2 + j)
            kp = np.arange(128) + 128 * c
            diff = qp[None, :] - kp[:, None]
            vis = (diff >= 0) & (diff < WIN)
            if c != 2 * i - 2 + j:
                vis = np.zeros_like(vis)
            ml[i, j] = np.where(vis, 1.0, 0.0)
    out["mask_l"] = ml.astype(bfdt)
    for name, base, conv in [("r0", 1e4, "half"), ("rg", 1e6, "inter"), ("rl", 1e4, "inter")]:
        C, Sg = rope_tables(qpos, 64, base, conv)
        out["C" + name] = C.reshape(NT, 128, 64).astype(bfdt)
        out["S" + name] = Sg.reshape(NT, 128, 64).astype(bfdt)
    C, Sg = rope_tables(np.arange(S), 64, 1e6, "inter")
    out["Crgf"] = C.reshape(KC, 128, 64).astype(bfdt)
    out["Srgf"] = Sg.reshape(KC, 128, 64).astype(bfdt)
    out["ident"] = np.eye(128, dtype=bfdt)
    out["ones_col"] = np.ones((128, 1), dtype=bfdt)
    out["sel_lo"] = np.concatenate([np.ones((1, 64)), np.zeros((1, 64))], 1).astype(bfdt)
    out["sel_hi"] = np.concatenate([np.zeros((1, 64)), np.ones((1, 64))], 1).astype(bfdt)
    return out


def shard_x(x_full, b, p):
    rows = np.concatenate([x_full[b, 128 * (2 * i + p):128 * (2 * i + p) + 128] for i in range(NT)], 0)
    return np.ascontiguousarray(rows.T.reshape(DC, 128, T))


def unshard_x(cores_out):
    x = np.zeros((B, S, D), np.float32)
    for c in range(8):
        b, p = c // 2, c % 2
        fm = cores_out[c].reshape(D, T)
        rows = fm.T
        for i in range(NT):
            x[b, 128 * (2 * i + p):128 * (2 * i + p) + 128] = rows[i * 128:(i + 1) * 128]
    return x


def interleave_pair(a0, a1):
    s = np.zeros((S,) + a0.shape[1:], a0.dtype)
    for i in range(NT):
        s[128 * 2 * i: 128 * 2 * i + 128] = a0[i * 128:(i + 1) * 128]
        s[128 * (2 * i + 1): 128 * (2 * i + 1) + 128] = a1[i * 128:(i + 1) * 128]
    return s



class Step:
    __slots__ = ("sem", "val")

    def __init__(self, sem, val):
        self.sem = sem
        self.val = val


class Buf:
    __slots__ = ("t", "writer", "readers", "name")

    def __init__(self, t, name=""):
        self.t = t
        self.writer = None
        self.readers = []
        self.name = name

    def __getitem__(self, idx):
        return self.t[idx]

    @property
    def ap(self):
        return self.t


class Prog:
    NDMA = 8

    def __init__(self, nc):
        self.nc = nc
        self.q = {e: [] for e in ("pe", "act", "dve", "pool", "sp")}
        self.sem_names = ["spe", "sact", "sdve", "spool", "ssp"]
        self.dma_pools = {}
        for eng in ("sp", "act", "pool"):
            names = [f"d{eng}{i}" for i in range(self.NDMA)]
            self.dma_pools[eng] = names
            self.sem_names += names
        self.dma_rr = {eng: 0 for eng in ("sp", "act", "pool")}
        self.counts = {s: 0 for s in self.sem_names}
        self.waited = {e: {} for e in self.q}
        self.sems = {}
        self._cc_sems = []

    def sem_handles(self, ctx):
        for s in self.sem_names:
            self.sems[s] = ctx.enter_context(self.nc.semaphore(s))

    def cc_sem(self, ctx):
        name = f"scc{len(self._cc_sems)}"
        h = ctx.enter_context(self.nc.semaphore(name))
        self._cc_sems.append(name)
        self.sems[name] = h
        self.counts[name] = 0
        self.waited_any = None
        return name

    ENG_SEM = {"pe": "spe", "act": "sact", "dve": "sdve", "pool": "spool", "sp": "ssp"}

    def phase_barrier(self):
        snap = []
        for eng2, sem in self.ENG_SEM.items():
            if self.counts[sem] > 0:
                snap.append((sem, self.counts[sem]))
        for eng2, names in self.dma_pools.items():
            for nm in names:
                if self.counts[nm] > 0:
                    snap.append((nm, self.counts[nm]))
        for eng in ("pe", "act", "dve", "pool", "sp"):
            with self.step(eng) as st:
                for sem, v in snap:
                    st.dep(Step(sem, v))
                if eng in ("dve", "pool"):
                    st.add(lambda e_: e_.engine_nop())
                else:
                    st.add(lambda e_: e_.nop())

    def final_barrier(self):
        with self.step("pool") as st:
            for eng, names in self.dma_pools.items():
                for nm in names:
                    if self.counts[nm] > 0:
                        st.dep(Step(nm, self.counts[nm]))
            st.add(lambda e_: e_.engine_nop())

    def step(self, engine, reads=(), writes=(), dma=False, collective_sem=None):
        return _StepCtx(self, engine, list(reads), list(writes), dma, collective_sem)

    def emit_into_block(self, block):
        qs = self.q
        sems = self.sems

        @block.tensor
        def _(eng):
            for f in qs["pe"]:
                f(eng)

        @block.scalar
        def _(eng):
            for f in qs["act"]:
                f(eng)

        @block.vector
        def _(eng):
            for f in qs["dve"]:
                f(eng)

        @block.gpsimd
        def _(eng):
            for f in qs["pool"]:
                f(eng)

        @block.sync
        def _(eng):
            for f in qs["sp"]:
                f(eng)


class _StepCtx:
    def __init__(self, prog, engine, reads, writes, dma, collective_sem):
        self.p = prog
        self.engine = engine
        self.reads = reads
        self.writes = writes
        self.dma = dma
        self.ccs = collective_sem
        self.fns = []
        self.extra_deps = []

    def dep(self, step):
        if step is not None:
            self.extra_deps.append(step)

    def add(self, fn):
        self.fns.append(fn)
        return self

    def __enter__(self):
        return self

    def __exit__(self, et, ev, tb):
        if et is not None:
            return False
        p, eng = self.p, self.engine
        deps = list(self.extra_deps)
        for b in self.reads:
            if b.writer is not None:
                deps.append(b.writer)
        for b in self.writes:
            if b.writer is not None:
                deps.append(b.writer)
            deps.extend(b.readers)
        waited = p.waited[eng]
        need = {}
        for d in deps:
            if d.val > need.get(d.sem, -1):
                need[d.sem] = d.val
        fns = self.fns
        assert fns, "empty step"
        n = len(fns)
        if self.ccs:
            inc_per, sem_name, issue_wait = 1, self.ccs, None
        elif self.dma:
            rr = p.dma_rr[eng]
            p.dma_rr[eng] = rr + 1
            sem_name = p.dma_pools[eng][rr % p.NDMA]
            inc_per = 16
            issue_wait = (sem_name, p.counts[sem_name])
        else:
            inc_per, sem_name, issue_wait = 1, p.ENG_SEM[eng], None
        base = p.counts[sem_name]
        if self.dma or self.ccs:
            final = base + inc_per * n
        else:
            final = base + 1
        p.counts[sem_name] = final
        wait_list = []
        if issue_wait is not None and issue_wait[1] > 0 and waited.get(issue_wait[0], -1) < issue_wait[1]:
            wait_list.append(issue_wait)
        own_eng_sem = p.ENG_SEM[eng] if eng == "pe" else None
        for s, v in need.items():
            if s == own_eng_sem:
                continue
            if waited.get(s, -1) >= v:
                continue
            wait_list.append((s, v))
        for s, v in wait_list:
            waited[s] = max(waited.get(s, -1), v)
        sems = p.sems

        def emit(e, _wl=wait_list, _fns=fns, _dma=self.dma, _cc=self.ccs,
                 _sem=sem_name, _inc=inc_per, _last_only=not (self.dma or self.ccs)):
            for s, v in _wl:
                e.wait_ge(sems[s], v)
            for i, f in enumerate(_fns):
                inst = f(e)
                if _last_only:
                    if i == len(_fns) - 1:
                        inst.then_inc(sems[_sem], 1)
                else:
                    inst.then_inc(sems[_sem], _inc)

        p.q[eng].append(emit)
        st = Step(sem_name, final)
        for b in self.reads:
            b.readers.append(st)
        for b in self.writes:
            b.writer = st
            b.readers = []
        self.token = st
        return False



def swap_view_half(t, part, nheads):
    pitch = t.t.shape[1] if hasattr(t, "t") else t.shape[1]
    h = t.t if hasattr(t, "t") else t
    return bass.AP(h.tensor if hasattr(h, "tensor") else h, 32,
                   [[h.shape[1], part], [64, nheads], [-32, 2], [1, 32]])


def emit_norm(P, ctx, xr, xn, kind, consts, eps, gain=None, nchunks=DC, scratch=None):
    sc = scratch
    pSum, pSq, pB1, pB2 = sc["pSum"], sc["pSq"], sc["pB1"], sc["pB2"]
    xq = sc["xq"]
    ssum, ssq = sc["ssum"], sc["ssq"]
    srstd, smr = sc["srstd"], sc["smr"]
    stmp = smr
    rb, mrb, cst = sc["rb"], sc["mrb"], sc["cst"]
    onc = consts["ones_col"]
    onr = consts["ones1r"]
    inv = 1.0 / (128 * nchunks)
    stats_only = kind == "ln_stats"
    fp32_inplace = kind == "ln32" or stats_only
    if fp32_inplace:
        kind = "ln"
        xc = sc["xc"]
    else:
        with P.step("dve", reads=[xr], writes=[xn]) as st:
            st.add(lambda e: e.tensor_copy(xn[:, 0:nchunks * T], xr[:, 0:nchunks * T]))
    stat_src = xr if fp32_inplace else xn
    for tr in range(4):
        for ch in range(nchunks):
            s0 = ch * T + tr * 512
            if fp32_inplace:
                with P.step("dve", reads=[xr], writes=[xc]) as st:
                    st.add(lambda e, a=s0: e.tensor_copy(xc[:, :], xr[:, a:a + 512]))
                srcsl = lambda a: xc[:, :]
            else:
                srcsl = lambda a: xn[:, a:a + 512]
            if kind == "ln":
                with P.step("pe", reads=[stat_src if not fp32_inplace else xc, onc], writes=[pSum]) as st:
                    st.add(lambda e, a=s0, c=ch, f=srcsl: e.matmul(
                        pSum[0:1, :], onc[:, 0:1], f(a),
                        start=(c == 0), stop=(c == nchunks - 1)))
            with P.step("dve", reads=[stat_src if not fp32_inplace else xc], writes=[xq]) as st:
                st.add(lambda e, a=s0, f=srcsl: e.tensor_mul(xq[:, :], f(a), f(a)))
            with P.step("pe", reads=[xq, onc], writes=[pSq]) as st:
                st.add(lambda e, c=ch: e.matmul(
                    pSq[0:1, :], onc[:, 0:1], xq[:, :],
                    start=(c == 0), stop=(c == nchunks - 1)))
        if kind == "ln":
            with P.step("dve", reads=[pSum], writes=[ssum]) as st:
                st.add(lambda e, t=tr: e.tensor_copy(ssum[0:1, t * 512:(t + 1) * 512], pSum[0:1, :]))
        with P.step("dve", reads=[pSq], writes=[ssq]) as st:
            st.add(lambda e, t=tr: e.tensor_copy(ssq[0:1, t * 512:(t + 1) * 512], pSq[0:1, :]))
    with P.step("dve", writes=[cst]) as st:
        st.add(lambda e: e.memset(cst[:, 0:1], float(eps)))
    if kind == "ln":
        with P.step("dve", reads=[ssum], writes=[ssum]) as st:
            st.add(lambda e: e.tensor_scalar(ssum[0:1, :], ssum[0:1, :], inv, None, ALU.mult))
        with P.step("dve", reads=[ssum], writes=[stmp]) as st:
            st.add(lambda e: e.tensor_mul(stmp[0:1, :], ssum[0:1, :], ssum[0:1, :]))
        with P.step("dve", reads=[ssq, stmp], writes=[ssq]) as st:
            st.add(lambda e: e.scalar_tensor_tensor(ssq[0:1, :], ssq[0:1, :], inv, stmp[0:1, :], ALU.mult, ALU.subtract))
    else:
        with P.step("dve", reads=[ssq], writes=[ssq]) as st:
            st.add(lambda e: e.tensor_scalar(ssq[0:1, :], ssq[0:1, :], inv, None, ALU.mult))
    with P.step("act", reads=[ssq, cst], writes=[ssq]) as st:
        st.add(lambda e: e.activation(ssq[0:1, :], ssq[0:1, :], AF.Ln, bias=cst[0:1, 0:1]))
    with P.step("act", reads=[ssq], writes=[srstd]) as st:
        st.add(lambda e: e.activation(srstd[0:1, :], ssq[0:1, :], AF.Exp, scale=-0.5))
    if kind == "ln":
        with P.step("dve", reads=[ssum, srstd], writes=[smr]) as st:
            st.add(lambda e: e.tensor_mul(smr[0:1, :], ssum[0:1, :], srstd[0:1, :]))
    if stats_only:
        return
    for tr in range(4):
        with P.step("pe", reads=[srstd, onr], writes=[pB1]) as st:
            st.add(lambda e, t=tr: e.matmul(pB1[:, :], onr[0:1, :], srstd[0:1, t * 512:(t + 1) * 512],
                                            start=True, stop=True))
        with P.step("dve", reads=[pB1], writes=[rb]) as st:
            st.add(lambda e: e.tensor_copy(rb[:, :], pB1[:, :]))
        if kind == "ln":
            with P.step("pe", reads=[smr, onr], writes=[pB2]) as st:
                st.add(lambda e, t=tr: e.matmul(pB2[:, :], onr[0:1, :], smr[0:1, t * 512:(t + 1) * 512],
                                                start=True, stop=True))
            with P.step("dve", reads=[pB2], writes=[mrb]) as st:
                st.add(lambda e: e.tensor_copy(mrb[:, :], pB2[:, :]))
        if kind == "ln":
            tgt = xr if fp32_inplace else xn
            h = tgt.t
            pitch = h.shape[1]
            v3 = bass.AP(h, tr * 512, [[pitch, 128], [T, nchunks], [1, 512]])
            rbb = rb[:, 0:512].rearrange("p (o q) -> p o q", o=1).broadcast_to([128, nchunks, 512])
            mrbb = mrb[:, 0:512].rearrange("p (o q) -> p o q", o=1).broadcast_to([128, nchunks, 512])
            with P.step("dve", reads=[tgt, rb], writes=[tgt]) as st:
                st.add(lambda e, v=v3, b=rbb: e.tensor_mul(v, v, b))
            with P.step("dve", reads=[tgt, mrb], writes=[tgt]) as st:
                st.add(lambda e, v=v3, b=mrbb: e.tensor_sub(v, v, b))
        else:
            h = xn.t
            pitch = h.shape[1]
            v3 = bass.AP(h, tr * 512, [[pitch, 128], [T, nchunks], [1, 512]])
            rbb = rb[:, 0:512].rearrange("p (o q) -> p o q", o=1).broadcast_to([128, nchunks, 512])
            with P.step("dve", reads=[xn, rb], writes=[xn]) as st:
                st.add(lambda e, v=v3, b=rbb: e.tensor_mul(v, v, b))
            if gain is not None:
                gb = gain[:, 0:nchunks].rearrange("p (c o) -> p c o", o=1).broadcast_to([128, nchunks, 512])
                with P.step("dve", reads=[xn, gain], writes=[xn]) as st:
                    st.add(lambda e, v=v3, b=gb: e.tensor_tensor(v, v, b, ALU.mult))


def emit_rope(P, q_tm, tabC, tabS, toff, nheads, conv, scratch):
    r1, r2 = scratch["r1"], scratch["r2"]
    w = nheads * 64
    h = q_tm.t
    pitch = h.shape[1]
    if conv == "half":
        swap = bass.AP(h, 32, [[pitch, 128], [64, nheads], [-32, 2], [1, 32]])
    else:
        swap = bass.AP(h, 1, [[pitch, 128], [64, nheads], [2, 32], [-1, 2]])
    Cb = tabC[:, toff:toff + 64].rearrange("p (o f) -> p o f", o=1).broadcast_to([128, nheads, 64])
    Sb = tabS[:, toff:toff + 64].rearrange("p (o f) -> p o f", o=1).broadcast_to([128, nheads, 64])
    qv = h[:, 0:w].rearrange("p (h f) -> p h f", h=nheads)
    with P.step("dve", reads=[q_tm, tabC], writes=[r1]) as st:
        st.add(lambda e: e.tensor_mul(r1[:, 0:w].rearrange("p (h f) -> p h f", h=nheads), qv, Cb))
    with P.step("dve", reads=[q_tm, tabS], writes=[r2]) as st:
        st.add(lambda e: e.tensor_mul(r2[:, 0:w].rearrange("p (h f) -> p h f", h=nheads), swap, Sb))
    with P.step("dve", reads=[r1, r2], writes=[q_tm]) as st:
        st.add(lambda e: e.tensor_add(h[:, 0:w], r1[:, 0:w], r2[:, 0:w]))


def emit_qknorm(P, q_tm, nheads, scratch, eps=1e-6):
    ss = scratch["ss"]
    h = q_tm.t
    with P.step("dve", reads=[q_tm], writes=[scratch["r1"]]) as st:
        st.add(lambda e: e.tensor_mul(scratch["r1"][:, 0:nheads * 64], h[:, 0:nheads * 64], h[:, 0:nheads * 64]))
    with P.step("dve", reads=[scratch["r1"]], writes=[ss]) as st:
        st.add(lambda e: e.reduce_sum(
            ss[:, 0:nheads].rearrange("p (h o) -> p h o", o=1),
            scratch["r1"][:, 0:nheads * 64].rearrange("p (h f) -> p h f", h=nheads), axis=AX.X))
    cst = scratch["cst"]
    with P.step("dve", writes=[cst]) as st:
        st.add(lambda e: e.memset(cst[:, 1:2], float(eps)))
    with P.step("act", reads=[ss, cst], writes=[ss]) as st:
        st.add(lambda e: e.activation(ss[:, 0:nheads], ss[:, 0:nheads], AF.Ln, bias=cst[:, 1:2], scale=1.0 / 64))
    with P.step("act", reads=[ss], writes=[ss]) as st:
        st.add(lambda e: e.activation(ss[:, 0:nheads], ss[:, 0:nheads], AF.Exp, scale=-0.5))
    ssb = ss[:, 0:nheads].rearrange("p (h o) -> p h o", o=1).broadcast_to([128, nheads, 64])
    with P.step("dve", reads=[q_tm, ss], writes=[q_tm]) as st:
        st.add(lambda e: e.tensor_mul(
            h[:, 0:nheads * 64].rearrange("p (h f) -> p h f", h=nheads),
            h[:, 0:nheads * 64].rearrange("p (h f) -> p h f", h=nheads), ssb))



def norm_consts():
    ones_col = np.ones((128, 1), np.float32).astype(bfdt)
    ones1r = np.ones((1, 128), np.float32).astype(bfdt)
    return ones_col, ones1r


_NSC_UID = [0]


def alloc_norm_scratch(nc, ctx, P, banks=None, with_xc=True):
    sc = {}
    e = ctx.enter_context
    _NSC_UID[0] += 1
    uid = f"{_NSC_UID[0]}_"
    if banks is not None:
        sc["pSum"], sc["pSq"], sc["pB1"], sc["pB2"] = (
            banks["pA"], banks["pB"], banks["pC"], banks["pD"])
    else:
        sc["pSum"] = Buf(e(nc.psum_tensor("pSum", [128, 512], F32)), "pSum")
        sc["pSq"] = Buf(e(nc.psum_tensor("pSq", [128, 512], F32)), "pSq")
        sc["pB1"] = Buf(e(nc.psum_tensor("pB1", [128, 512], F32)), "pB1")
        sc["pB2"] = Buf(e(nc.psum_tensor("pB2", [128, 512], F32)), "pB2")
    sc["xq"] = Buf(e(nc.sbuf_tensor("n" + uid + "xq", [128, 512], BF16)), "xq")
    if with_xc:
        sc["xc"] = Buf(e(nc.sbuf_tensor("n" + uid + "xc", [128, 512], BF16)), "xc")
    for n in ("ssum", "ssq"):
        sc[n] = Buf(e(nc.sbuf_tensor("n" + uid + n, [1, 2048], F32)), n)
    for n in ("srstd", "smr"):
        sc[n] = Buf(e(nc.sbuf_tensor("n" + uid + n, [1, 2048], BF16)), n)
    sc["cst"] = Buf(e(nc.sbuf_tensor("n" + uid + "cst", [128, 4], F32)), "cst")
    sc["rb"] = Buf(e(nc.sbuf_tensor("n" + uid + "rb", [128, 512], BF16)), "rb")
    sc["mrb"] = Buf(e(nc.sbuf_tensor("n" + uid + "mrb", [128, 512], BF16)), "mrb")
    return sc


def build_launch_A():
    nc = bass.Bass("TRN2", target_bir_lowering=False, debug=False)
    xin = nc.dram_tensor("xin", [DC, 128, T], F32, kind="ExternalInput").ap()
    wq = nc.dram_tensor("wq", [1024, 1024], F32, kind="ExternalInput").ap()
    wk = nc.dram_tensor("wk", [1024, 256], F32, kind="ExternalInput").ap()
    wv = nc.dram_tensor("wv", [1024, 256], F32, kind="ExternalInput").ap()
    Cr0 = nc.dram_tensor("Cr0", [NT, 128, 64], BF16, kind="ExternalInput").ap()
    Sr0 = nc.dram_tensor("Sr0", [NT, 128, 64], BF16, kind="ExternalInput").ap()
    onci = nc.dram_tensor("onci", [128, 1], BF16, kind="ExternalInput").ap()
    onri = nc.dram_tensor("onri", [1, 128], BF16, kind="ExternalInput").ap()
    qo = nc.dram_tensor("qo", [NT, 128, 1024], BF16, kind="ExternalOutput").ap()
    ko = nc.dram_tensor("ko", [NT, 128, 256], BF16, kind="ExternalOutput").ap()
    vo = nc.dram_tensor("vo", [NT, 128, 256], BF16, kind="ExternalOutput").ap()

    from contextlib import ExitStack
    with ExitStack() as ctx:
        e = ctx.enter_context
        P = Prog(nc)
        P.sem_handles(ctx)
        xr = Buf(e(nc.sbuf_tensor("xr", [128, DC * T], F32)), "xr")
        xn = Buf(e(nc.sbuf_tensor("xn", [128, DC * T], BF16)), "xn")
        wq_sb = Buf(e(nc.sbuf_tensor("wq_sb", [128, DC * 1024], BF16)), "wq_sb")
        wkv_sb = Buf(e(nc.sbuf_tensor("wkv_sb", [128, DC * 512], BF16)), "wkv_sb")
        q_tm = Buf(e(nc.sbuf_tensor("q_tm", [128, 1024], BF16)), "q_tm")
        k_tm = Buf(e(nc.sbuf_tensor("k_tm", [128, 256], BF16)), "k_tm")
        v_tm = Buf(e(nc.sbuf_tensor("v_tm", [128, 256], BF16)), "v_tm")
        r1 = Buf(e(nc.sbuf_tensor("r1", [128, 1024], BF16)), "r1")
        r2 = Buf(e(nc.sbuf_tensor("r2", [128, 1024], BF16)), "r2")
        tC = Buf(e(nc.sbuf_tensor("tC", [128, NT * 64], BF16)), "tC")
        tS = Buf(e(nc.sbuf_tensor("tS", [128, NT * 64], BF16)), "tS")
        onc = Buf(e(nc.sbuf_tensor("onc", [128, 1], BF16)), "onc")
        onr = Buf(e(nc.sbuf_tensor("onr", [1, 128], BF16)), "onr")
        pQ = Buf(e(nc.psum_tensor("pQ", [128, 512], F32)), "pQ")
        nsc = alloc_norm_scratch(nc, ctx, P)
        consts = {"ones_col": onc, "ones1r": onr}
        rsc = {"r1": r1, "r2": r2}

        with P.step("sp", writes=[xr], dma=True) as st:
            st.add(lambda e_: e_.dma_start(out=xr[:, :].rearrange("p (c t) -> p c t", c=DC), in_=xin.rearrange("c p t -> p c t")))
        with P.step("sp", writes=[tC, tS, onc, onr], dma=True) as st:
            st.add(lambda e_: e_.dma_start(out=tC[:, :].rearrange("p (i f) -> p i f", i=NT), in_=Cr0.rearrange("i p f -> p i f")))
            st.add(lambda e_: e_.dma_start(out=tS[:, :].rearrange("p (i f) -> p i f", i=NT), in_=Sr0.rearrange("i p f -> p i f")))
            st.add(lambda e_: e_.dma_start(out=onc[:, :], in_=onci))
            st.add(lambda e_: e_.dma_start(out=onr[:, :], in_=onri))
        with P.step("pool", writes=[wq_sb], dma=True) as st:
            st.add(lambda e_: e_.dma_start(
                out=wq_sb[:, :].rearrange("p (c n) -> p c n", c=DC),
                in_=wq.rearrange("(c p) n -> p c n", c=DC)))
        with P.step("pool", writes=[wkv_sb], dma=True) as st:
            st.add(lambda e_: e_.dma_start(
                out=bass.AP(wkv_sb.t, 0, [[DC * 512, 128], [512, DC], [1, 256]]),
                in_=wk.rearrange("(c p) n -> p c n", c=DC)))
            st.add(lambda e_: e_.dma_start(
                out=bass.AP(wkv_sb.t, 256, [[DC * 512, 128], [512, DC], [1, 256]]),
                in_=wv.rearrange("(c p) n -> p c n", c=DC)))

        emit_norm(P, ctx, xr, xn, "ln", consts, 1e-5, scratch=nsc)

        for i in range(NT):
            for nr in range(3):
                with P.step("pe", reads=[xn, wq_sb, wkv_sb], writes=[pQ]) as st:
                    for ch in range(DC):
                        if nr < 2:
                            w_ap = wq_sb[:, ch * 1024 + nr * 512: ch * 1024 + nr * 512 + 512]
                        else:
                            w_ap = wkv_sb[:, ch * 512: ch * 512 + 512]
                        st.add(lambda e_, c=ch, w=w_ap, ii=i: e_.matmul(
                            pQ[:, :], xn[:, c * T + ii * 128: c * T + ii * 128 + 128], w,
                            start=(c == 0), stop=(c == DC - 1)))
                if nr < 2:
                    with P.step("dve", reads=[pQ], writes=[q_tm]) as st:
                        st.add(lambda e_, n=nr: e_.tensor_copy(q_tm[:, n * 512:(n + 1) * 512], pQ[:, :]))
                else:
                    with P.step("dve", reads=[pQ], writes=[k_tm, v_tm]) as st:
                        st.add(lambda e_: e_.tensor_copy(k_tm[:, :], pQ[:, 0:256]))
                        st.add(lambda e_: e_.tensor_copy(v_tm[:, :], pQ[:, 256:512]))
            emit_rope(P, q_tm, tC, tS, i * 64, 16, "half", rsc)
            emit_rope(P, k_tm, tC, tS, i * 64, 4, "half", rsc)
            with P.step("sp", reads=[q_tm, k_tm, v_tm], dma=True) as st:
                st.add(lambda e_, ii=i: e_.dma_start(out=qo[ii], in_=q_tm[:, :]))
                st.add(lambda e_, ii=i: e_.dma_start(out=ko[ii], in_=k_tm[:, :]))
                st.add(lambda e_, ii=i: e_.dma_start(out=vo[ii], in_=v_tm[:, :]))

        P.final_barrier()

        blk = e(nc.Block())
        P.emit_into_block(blk)
    return nc



def _resid_add(P, xr, oc, g, pP, w=256):
    with P.step("dve", reads=[pP, xr], writes=[xr]) as st:
        st.add(lambda e: e.tensor_add(
            xr[:, oc * T + g * w: oc * T + g * w + w], pP[:, 0:w],
            xr[:, oc * T + g * w: oc * T + g * w + w]))


def emit_attention_causal(P, cfg):
    heads = cfg["heads"]
    npairs = len(heads) // 2
    bk = cfg["banks"]
    sc = cfg["scratch"]
    pA, pB, pC, pD, pE, pF = bk["pA"], bk["pB"], bk["pC"], bk["pD"], bk["pE"], bk["pF"]
    aA, aB = sc["aA"], sc["aB"]
    outT = sc["outT"]
    cst = sc["cst"]
    defer_tail = "den2" in sc
    if "denP" not in sc:
        if defer_tail:
            sc["denP"] = [sc["den"], sc["den2"]]
            sc["rdenP"] = [sc["rden"], sc["rden2"]]
        else:
            sc["denP"] = [sc["den"], sc["den"]]
            sc["rdenP"] = [sc["rden"], sc["rden"]]
        sc["rbP"] = [sc["rb"], sc["rb"]]
    qkv_reads = cfg["reads_qkv"]
    wo_sb = cfg["wo_sb"]
    sel_lo, sel_hi = cfg["sel_lo"], cfg["sel_hi"]
    scale = cfg["scale"]
    ngroups = cfg["ngroups"]
    QW = cfg["qwidth"]
    NM = cfg["nmask"]
    mload = cfg["mask_load"]

    for g in range(ngroups):
        nch = cfg["nch_of_g"](g)
        mask = mload(P, g)
        tail_pend = None
        for m in range(npairs):
            hL, hH = heads[2 * m], heads[2 * m + 1]

            def emit_av(j, at):
                c = cfg["chunk_of"](g, j)
                with P.step("pe", reads=[at, *qkv_reads], writes=[pC]) as st:
                    st.add(lambda e, h_=hL, c_=c, at_=at, j_=j, n_=nch: e.matmul(
                        pC[0:65, 0:QW], cfg["vslice"](h_, c_), at_[:, 0:QW],
                        start=(j_ == 0), stop=(j_ == n_ - 1)))
                with P.step("pe", reads=[at, *qkv_reads], writes=[pD]) as st:
                    st.add(lambda e, h_=hH, c_=c, at_=at, j_=j, n_=nch: e.matmul(
                        pD[0:65, 0:QW], cfg["vslice"](h_, c_), at_[:, QW:2 * QW],
                        start=(j_ == 0), stop=(j_ == n_ - 1)))

            atP = [aA, aB] + ([sc["aC"]] if "aC" in sc else [])
            nat = len(atP)
            av_q = []
            for j in range(nch):
                c = cfg["chunk_of"](g, j)
                psl = pA if j % 2 == 0 else pB
                psh = pE if j % 2 == 0 else pF
                with P.step("pe", reads=qkv_reads, writes=[psl, psh]) as st:
                    st.add(lambda e, h_=hL, c_=c, g_=g, ps_=psl: e.matmul(
                        ps_[:, 0:QW], cfg["kslice"](h_, c_), cfg["qslice"](h_, g_),
                        start=True, stop=True))
                    st.add(lambda e, h_=hH, c_=c, g_=g, ps_=psh: e.matmul(
                        ps_[:, 0:QW], cfg["kslice"](h_, c_), cfg["qslice"](h_, g_),
                        start=True, stop=True))
                at = atP[j % nat]
                with P.step("act", reads=[psl, psh], writes=[at]) as st:
                    st.add(lambda e, ps_=psl, at_=at: e.activation(
                        at_[:, 0:QW], ps_[:, 0:QW], AF.Exp, scale=scale))
                    st.add(lambda e, ps_=psh, at_=at: e.activation(
                        at_[:, QW:2 * QW], ps_[:, 0:QW], AF.Exp, scale=scale))
                jm = j - (nch - NM)
                if jm >= 0:
                    mb = mask[:, jm * QW:(jm + 1) * QW].rearrange(
                        "p (o q) -> p o q", o=1).broadcast_to([128, 2, QW])
                    with P.step("dve", reads=[at, mask], writes=[at]) as st:
                        st.add(lambda e, at_=at, mb_=mb: e.tensor_mul(
                            at_[:, 0:2 * QW].rearrange("p (o q) -> p o q", o=2),
                            at_[:, 0:2 * QW].rearrange("p (o q) -> p o q", o=2), mb_))
                av_q.append((j, at))
                if len(av_q) >= nat:
                    emit_av(*av_q.pop(0))
            for it in av_q:
                emit_av(*it)

            def pair_tail(m=m, den=sc["denP"][m % 2], rden=sc["rdenP"][m % 2],
                          rb=sc["rbP"][m % 2]):
                with P.step("dve", reads=[pC, pD], writes=[den]) as st:
                    st.add(lambda e: e.tensor_copy(den[0:1, 0:QW], pC[64:65, 0:QW]))
                    st.add(lambda e: e.tensor_copy(den[0:1, QW:2 * QW], pD[64:65, 0:QW]))
                with P.step("dve", reads=[pC, pD], writes=[outT]) as st:
                    st.add(lambda e: e.tensor_copy(
                        outT[0:64, m * QW:(m + 1) * QW], pC[0:64, 0:QW]))
                    st.add(lambda e: e.tensor_copy(
                        outT[64:128, m * QW:(m + 1) * QW], pD[0:64, 0:QW]))

            def pair_tail2(m=m, den=sc["denP"][m % 2], rden=sc["rdenP"][m % 2],
                           rb=sc["rbP"][m % 2]):
                with P.step("act", reads=[den, cst], writes=[den]) as st:
                    st.add(lambda e: e.activation(den[0:1, 0:2 * QW], den[0:1, 0:2 * QW], AF.Ln,
                                                  bias=cst[0:1, 3:4]))
                with P.step("act", reads=[den], writes=[rden]) as st:
                    st.add(lambda e: e.activation(rden[0:1, 0:2 * QW], den[0:1, 0:2 * QW], AF.Exp,
                                                  scale=-1.0))
                with P.step("pe", reads=[rden, sel_lo, sel_hi], writes=[pF]) as st:
                    st.add(lambda e: e.matmul(
                        pF[:, 0:QW], sel_lo[0:1, :], rden[0:1, 0:QW],
                        start=True, stop=False))
                    st.add(lambda e: e.matmul(
                        pF[:, 0:QW], sel_hi[0:1, :], rden[0:1, QW:2 * QW],
                        start=False, stop=True))
                with P.step("dve", reads=[pF], writes=[rb]) as st:
                    st.add(lambda e: e.tensor_copy(rb[:, 0:QW], pF[:, 0:QW]))
                with P.step("dve", reads=[outT, rb], writes=[outT]) as st:
                    st.add(lambda e: e.tensor_mul(
                        outT[:, m * QW:(m + 1) * QW], outT[:, m * QW:(m + 1) * QW], rb[:, 0:QW]))

            pair_tail()
            if defer_tail:
                if tail_pend is not None:
                    tail_pend()
                tail_pend = pair_tail2
            else:
                pair_tail2()
        if defer_tail:
            tail_pend()
        pcs = cfg.get("proj_chunks", [(c, c) for c in range(npairs)])
        for oc in range(DC):
            pP = pA if oc % 2 == 0 else pB
            with P.step("pe", reads=[outT, wo_sb], writes=[pP]) as st:
                for ci, (lc, wc) in enumerate(pcs):
                    st.add(lambda e, l_=lc, w_=wc, o_=oc, pP_=pP, i_=ci: e.matmul(
                        pP_[:, 0:QW], wo_sb[:, w_ * 1024 + o_ * 128: w_ * 1024 + o_ * 128 + 128],
                        outT[:, l_ * QW:(l_ + 1) * QW],
                        start=(i_ == 0), stop=(i_ == len(pcs) - 1)))
            cfg["emit_resid"](P, oc, g, pP)


def build_launch_B():
    nc = bass.Bass("TRN2", target_bir_lowering=False, debug=False)
    xin = nc.dram_tensor("xin", [DC, 128, T], F32, kind="ExternalInput").ap()
    qTi = nc.dram_tensor("qTi", [DC, 128, T], BF16, kind="ExternalInput").ap()
    Kfmi = nc.dram_tensor("Kfmi", [2, 128, S], BF16, kind="ExternalInput").ap()
    Vtmi = nc.dram_tensor("Vtmi", [KC, 128, 256], BF16, kind="ExternalInput").ap()
    mci = nc.dram_tensor("mci", [8, 4, 128, 256], BF16, kind="ExternalInput").ap()
    wo = nc.dram_tensor("wo", [1024, 1024], F32, kind="ExternalInput").ap()
    sli = nc.dram_tensor("sli", [1, 128], BF16, kind="ExternalInput").ap()
    shi = nc.dram_tensor("shi", [1, 128], BF16, kind="ExternalInput").ap()
    xo = nc.dram_tensor("xo", [DC, 128, T], F32, kind="ExternalOutput").ap()

    from contextlib import ExitStack
    with ExitStack() as ctx:
        e = ctx.enter_context
        P = Prog(nc)
        P.sem_handles(ctx)
        xr = Buf(e(nc.sbuf_tensor("xr", [128, DC * T], F32)), "xr")
        qT = Buf(e(nc.sbuf_tensor("qT", [128, DC * T], BF16)), "qT")
        Kfm = Buf(e(nc.sbuf_tensor("Kfm", [128, 2 * S], BF16)), "Kfm")
        Vp = Buf(e(nc.sbuf_tensor("Vp", [128, KC * 260], BF16)), "Vp")
        msk = Buf(e(nc.sbuf_tensor("msk", [128, 8 * 4 * 256], BF16)), "msk")
        wo_sb = Buf(e(nc.sbuf_tensor("wo_sb", [128, DC * 1024], BF16)), "wo_sb")
        sel_lo = Buf(e(nc.sbuf_tensor("sel_lo", [1, 128], BF16)), "sel_lo")
        sel_hi = Buf(e(nc.sbuf_tensor("sel_hi", [1, 128], BF16)), "sel_hi")
        aA = Buf(e(nc.sbuf_tensor("aA", [128, 256], BF16)), "aA")
        aB = Buf(e(nc.sbuf_tensor("aB", [128, 256], BF16)), "aB")
        den = Buf(e(nc.sbuf_tensor("den", [1, 512], F32)), "den")
        rden = Buf(e(nc.sbuf_tensor("rden", [1, 512], BF16)), "rden")
        outT = Buf(e(nc.sbuf_tensor("outT", [128, 2048], BF16)), "outT")
        outTn = Buf(e(nc.sbuf_tensor("outTn", [128, 2048], BF16)), "outTn")
        rb = Buf(e(nc.sbuf_tensor("rb", [128, 256], BF16)), "rb")
        cst = Buf(e(nc.sbuf_tensor("cst", [128, 4], F32)), "cst")
        bks = {}
        for i, n in enumerate(("pA", "pB", "pC", "pD", "pE", "pF")):
            bks[n] = Buf(e(nc.psum_tensor(n, [128, 512], F32)), n)

        with P.step("sp", writes=[xr], dma=True) as st:
            st.add(lambda e_: e_.dma_start(out=xr[:, :].rearrange("p (c t) -> p c t", c=DC),
                                           in_=xin.rearrange("c p t -> p c t")))
        with P.step("sp", writes=[qT, Kfm, Vp, msk, sel_lo, sel_hi], dma=True) as st:
            st.add(lambda e_: e_.dma_start(out=qT[:, :].rearrange("p (c t) -> p c t", c=DC),
                                           in_=qTi.rearrange("c p t -> p c t")))
            st.add(lambda e_: e_.dma_start(out=Kfm[:, :].rearrange("p (c s) -> p c s", c=2),
                                           in_=Kfmi.rearrange("c p s -> p c s")))
            for k in range(4):
                st.add(lambda e_, k_=k: e_.dma_start(
                    out=bass.AP(Vp.t, k_ * 65, [[KC * 260, 128], [260, KC], [1, 64]]),
                    in_=Vtmi[:, :, k_ * 64:(k_ + 1) * 64].rearrange("c p f -> p c f")))
            st.add(lambda e_: e_.dma_start(out=msk[:, :].rearrange("p (g j q) -> p g j q", g=8, j=4),
                                           in_=mci.rearrange("g j p q -> p g j q")))
            st.add(lambda e_: e_.dma_start(out=sel_lo[:, :], in_=sli))
            st.add(lambda e_: e_.dma_start(out=sel_hi[:, :], in_=shi))
        with P.step("pool", writes=[wo_sb], dma=True) as st:
            st.add(lambda e_: e_.dma_start(
                out=wo_sb[:, :].rearrange("p (c n) -> p c n", c=DC),
                in_=wo.rearrange("(c p) n -> p c n", c=DC)))
        with P.step("dve", writes=[Vp]) as st:
            st.add(lambda e_: e_.memset(
                bass.AP(Vp.t, 64, [[KC * 260, 128], [260, KC], [65, 4], [1, 1]]), 1.0))
        with P.step("dve", writes=[cst]) as st:
            st.add(lambda e_: e_.memset(cst[:, 3:4], 0.0))

        cfg = {
            "heads": list(range(16)),
            "qslice": lambda h, g: qT[(h % 2) * 64:(h % 2) * 64 + 64,
                                      (h // 2) * T + g * 256:(h // 2) * T + g * 256 + 256],
            "kslice": lambda h, c: Kfm[((h % 4) % 2) * 64:((h % 4) % 2) * 64 + 64,
                                       ((h % 4) // 2) * S + c * 128:((h % 4) // 2) * S + c * 128 + 128],
            "vslice": lambda h, c: Vp[:, c * 260 + (h % 4) * 65: c * 260 + (h % 4) * 65 + 65],
            "mask": msk,
            "mslice": lambda g, j: msk[:, (g * 4 + j) * 256:(g * 4 + j + 1) * 256],
            "reads_qkv": [qT, Kfm, Vp],
            "banks": bks,
            "scratch": {"aA": aA, "aB": aB, "den": den, "rden": rden, "outT": outT,
                        "rb": rb, "cst": cst},
            "sel_lo": sel_lo, "sel_hi": sel_hi,
            "wo_sb": wo_sb,
            "emit_resid": (lambda P_, oc, g, pP: _resid_add(P_, xr, oc, g, pP)),
            "scale": 0.125,
            "nch_of_g": lambda g: 4 * g + 4,
            "chunk_of": lambda g, j: j,
        }
        emit_attention_causal(P, cfg)

        with P.step("sp", reads=[xr], dma=True) as st:
            st.add(lambda e_: e_.dma_start(out=xo.rearrange("c p t -> p c t"), in_=xr[:, :].rearrange("p (c t) -> p c t", c=DC)))
        P.final_barrier()

        blk = e(nc.Block())
        P.emit_into_block(blk)
    return nc


def build_launch_C():
    nc = bass.Bass("TRN2", target_bir_lowering=False, debug=False)
    x1i = nc.dram_tensor("x1i", [DC, 128, T], F32, kind="ExternalInput").ap()
    g1i = nc.dram_tensor("g1i", [1024], F32, kind="ExternalInput").ap()
    wqm = nc.dram_tensor("wqm", [1024, 1024], F32, kind="ExternalInput").ap()
    wdkv = nc.dram_tensor("wdkv", [1024, 512], F32, kind="ExternalInput").ap()
    onci = nc.dram_tensor("onci", [128, 1], BF16, kind="ExternalInput").ap()
    onri = nc.dram_tensor("onri", [1, 128], BF16, kind="ExternalInput").ap()
    ho = nc.dram_tensor("ho", [DC, 128, T], F32, kind="ExternalOutput").ap()
    qmo = nc.dram_tensor("qmo", [NT, 128, 1024], BF16, kind="ExternalOutput").ap()
    lato = nc.dram_tensor("lato", [4, 128, T], BF16, kind="ExternalOutput").ap()

    from contextlib import ExitStack
    with ExitStack() as ctx:
        e = ctx.enter_context
        P = Prog(nc)
        P.sem_handles(ctx)
        xr = Buf(e(nc.sbuf_tensor("xr", [128, DC * T], F32)), "xr")
        xn = Buf(e(nc.sbuf_tensor("xn", [128, DC * T], BF16)), "xn")
        wq_sb = Buf(e(nc.sbuf_tensor("wq_sb", [128, DC * 1024], BF16)), "wq_sb")
        wdkv_sb = Buf(e(nc.sbuf_tensor("wdkv_sb", [128, DC * 512], BF16)), "wdkv_sb")
        g1_sb = Buf(e(nc.sbuf_tensor("g1_sb", [128, DC], F32)), "g1_sb")
        q_tm = Buf(e(nc.sbuf_tensor("q_tm", [128, 1024], BF16)), "q_tm")
        lat_sb = Buf(e(nc.sbuf_tensor("lat_sb", [128, 4 * T], BF16)), "lat_sb")
        onc = Buf(e(nc.sbuf_tensor("onc", [128, 1], BF16)), "onc")
        onr = Buf(e(nc.sbuf_tensor("onr", [1, 128], BF16)), "onr")
        pQ = Buf(e(nc.psum_tensor("pQ", [128, 512], F32)), "pQ")
        nsc = alloc_norm_scratch(nc, ctx, P)
        consts = {"ones_col": onc, "ones1r": onr}

        with P.step("sp", writes=[xr], dma=True) as st:
            st.add(lambda e_: e_.dma_start(out=xr[:, :].rearrange("p (c t) -> p c t", c=DC),
                                           in_=x1i.rearrange("c p t -> p c t")))
        with P.step("sp", writes=[onc, onr, g1_sb], dma=True) as st:
            st.add(lambda e_: e_.dma_start(out=onc[:, :], in_=onci))
            st.add(lambda e_: e_.dma_start(out=onr[:, :], in_=onri))
            for ch in range(DC):
                st.add(lambda e_, c=ch: e_.dma_start(
                    out=g1_sb[:, c:c + 1],
                    in_=g1i[c * 128:(c + 1) * 128].rearrange("(p o) -> p o", o=1)))
        with P.step("pool", writes=[wq_sb], dma=True) as st:
            st.add(lambda e_: e_.dma_start(out=wq_sb[:, :].rearrange("p (c n) -> p c n", c=DC),
                                           in_=wqm.rearrange("(c p) n -> p c n", c=DC)))
        with P.step("pool", writes=[wdkv_sb], dma=True) as st:
            st.add(lambda e_: e_.dma_start(out=wdkv_sb[:, :].rearrange("p (c n) -> p c n", c=DC),
                                           in_=wdkv.rearrange("(c p) n -> p c n", c=DC)))

        emit_norm(P, ctx, xr, xn, "ln32", consts, 1e-5, scratch=nsc)
        with P.step("sp", reads=[xr], dma=True) as st:
            st.add(lambda e_: e_.dma_start(out=ho.rearrange("c p t -> p c t"),
                                           in_=xr[:, :].rearrange("p (c t) -> p c t", c=DC)))
        emit_norm(P, ctx, xr, xn, "rms", consts, 1e-6, gain=g1_sb, scratch=nsc)

        for i in range(NT):
            for nr in range(2):
                with P.step("pe", reads=[xn, wq_sb], writes=[pQ]) as st:
                    for ch in range(DC):
                        st.add(lambda e_, c=ch, n=nr, ii=i: e_.matmul(
                            pQ[:, :], xn[:, c * T + ii * 128: c * T + ii * 128 + 128],
                            wq_sb[:, c * 1024 + n * 512: c * 1024 + n * 512 + 512],
                            start=(c == 0), stop=(c == DC - 1)))
                with P.step("dve", reads=[pQ], writes=[q_tm]) as st:
                    st.add(lambda e_, n=nr: e_.tensor_copy(q_tm[:, n * 512:(n + 1) * 512], pQ[:, :]))
            with P.step("sp", reads=[q_tm], dma=True) as st:
                st.add(lambda e_, ii=i: e_.dma_start(out=qmo[ii], in_=q_tm[:, :]))
        for lc in range(4):
            for r in range(4):
                with P.step("pe", reads=[xn, wdkv_sb], writes=[pQ]) as st:
                    for ch in range(DC):
                        st.add(lambda e_, c=ch, l=lc, r_=r: e_.matmul(
                            pQ[:, :], wdkv_sb[:, c * 512 + l * 128: c * 512 + l * 128 + 128],
                            xn[:, c * T + r_ * 512: c * T + r_ * 512 + 512],
                            start=(c == 0), stop=(c == DC - 1)))
                with P.step("dve", reads=[pQ], writes=[lat_sb]) as st:
                    st.add(lambda e_, l=lc, r_=r: e_.tensor_copy(
                        lat_sb[:, l * T + r_ * 512: l * T + r_ * 512 + 512], pQ[:, :]))
        with P.step("sp", reads=[lat_sb], dma=True) as st:
            st.add(lambda e_: e_.dma_start(out=lato.rearrange("c p t -> p c t"),
                                           in_=lat_sb[:, :].rearrange("p (c t) -> p c t", c=4)))
        P.final_barrier()
        blk = e(nc.Block())
        P.emit_into_block(blk)
    return nc


def build_launch_D():
    nc = bass.Bass("TRN2", target_bir_lowering=False, debug=False)
    hi_ = nc.dram_tensor("hi_", [DC, 128, T], F32, kind="ExternalInput").ap()
    qmTi = nc.dram_tensor("qmTi", [DC, 128, T], BF16, kind="ExternalInput").ap()
    latTfi = nc.dram_tensor("latTfi", [4, 128, S], BF16, kind="ExternalInput").ap()
    wuk = nc.dram_tensor("wuk", [512, 1024], F32, kind="ExternalInput").ap()
    wuv = nc.dram_tensor("wuv", [512, 1024], F32, kind="ExternalInput").ap()
    wom = nc.dram_tensor("wom", [1024, 1024], F32, kind="ExternalInput").ap()
    mci = nc.dram_tensor("mci", [8, 4, 128, 256], BF16, kind="ExternalInput").ap()
    sli = nc.dram_tensor("sli", [1, 128], BF16, kind="ExternalInput").ap()
    shi = nc.dram_tensor("shi", [1, 128], BF16, kind="ExternalInput").ap()
    ho1 = nc.dram_tensor("ho1", [DC, 128, T], F32, kind="ExternalOutput").ap()

    from contextlib import ExitStack
    with ExitStack() as ctx:
        e = ctx.enter_context
        P = Prog(nc)
        P.sem_handles(ctx)
        latTf = Buf(e(nc.sbuf_tensor("latTf", [128, 4 * S], BF16)), "latTf")
        wuk_sb = Buf(e(nc.sbuf_tensor("wuk_sb", [128, 4 * 1024], BF16)), "wuk_sb")
        wuv_sb = Buf(e(nc.sbuf_tensor("wuv_sb", [128, 4 * 1024], BF16)), "wuv_sb")
        wom_sb = Buf(e(nc.sbuf_tensor("wom_sb", [128, DC * 1024], BF16)), "wom_sb")
        qTh = Buf(e(nc.sbuf_tensor("qTh", [128, 4 * T], BF16)), "qTh")
        Kh = Buf(e(nc.sbuf_tensor("Kh", [128, 4 * S], BF16)), "Kh")
        Vh = Buf(e(nc.sbuf_tensor("Vh", [128, KC * 520], BF16)), "Vh")
        msk = Buf(e(nc.sbuf_tensor("msk", [128, 8 * 4 * 256], BF16)), "msk")
        sel_lo = Buf(e(nc.sbuf_tensor("sel_lo", [1, 128], BF16)), "sel_lo")
        sel_hi = Buf(e(nc.sbuf_tensor("sel_hi", [1, 128], BF16)), "sel_hi")
        aA = Buf(e(nc.sbuf_tensor("aA", [128, 256], BF16)), "aA")
        aB = Buf(e(nc.sbuf_tensor("aB", [128, 256], BF16)), "aB")
        den = Buf(e(nc.sbuf_tensor("den", [1, 512], F32)), "den")
        rden = Buf(e(nc.sbuf_tensor("rden", [1, 512], BF16)), "rden")
        outT = Buf(e(nc.sbuf_tensor("outT", [128, 2048], BF16)), "outT")
        outTn = Buf(e(nc.sbuf_tensor("outTn", [128, 2048], BF16)), "outTn")
        rb = Buf(e(nc.sbuf_tensor("rb", [128, 256], BF16)), "rb")
        cst = Buf(e(nc.sbuf_tensor("cst", [128, 4], F32)), "cst")
        hsA = Buf(e(nc.sbuf_tensor("hsA", [128, 256], F32)), "hsA")
        hsB = Buf(e(nc.sbuf_tensor("hsB", [128, 256], F32)), "hsB")
        hin = Buf(hi_, "hin")
        hout = Buf(ho1, "hout")
        bks = {}
        for n in ("pA", "pB", "pC", "pD", "pE", "pF"):
            bks[n] = Buf(e(nc.psum_tensor(n, [128, 512], F32)), n)

        with P.step("sp", writes=[latTf, msk, sel_lo, sel_hi], dma=True) as st:
            st.add(lambda e_: e_.dma_start(out=latTf[:, :].rearrange("p (c s) -> p c s", c=4),
                                           in_=latTfi.rearrange("c p s -> p c s")))
            st.add(lambda e_: e_.dma_start(out=msk[:, :].rearrange("p (g j q) -> p g j q", g=8, j=4),
                                           in_=mci.rearrange("g j p q -> p g j q")))
            st.add(lambda e_: e_.dma_start(out=sel_lo[:, :], in_=sli))
            st.add(lambda e_: e_.dma_start(out=sel_hi[:, :], in_=shi))
        with P.step("pool", writes=[wuk_sb], dma=True) as st:
            st.add(lambda e_: e_.dma_start(out=wuk_sb[:, :].rearrange("p (c n) -> p c n", c=4),
                                           in_=wuk.rearrange("(c p) n -> p c n", c=4)))
        with P.step("pool", writes=[wuv_sb], dma=True) as st:
            st.add(lambda e_: e_.dma_start(out=wuv_sb[:, :].rearrange("p (c n) -> p c n", c=4),
                                           in_=wuv.rearrange("(c p) n -> p c n", c=4)))
        with P.step("pool", writes=[wom_sb], dma=True) as st:
            st.add(lambda e_: e_.dma_start(out=wom_sb[:, :].rearrange("p (c n) -> p c n", c=DC),
                                           in_=wom.rearrange("(c p) n -> p c n", c=DC)))
        with P.step("dve", writes=[cst]) as st:
            st.add(lambda e_: e_.memset(cst[:, 3:4], 0.0))

        pA, pB = bks["pA"], bks["pB"]
        for ps in range(2):
            with P.step("sp", writes=[qTh], dma=True) as st:
                st.add(lambda e_, p_=ps: e_.dma_start(
                    out=qTh[:, :].rearrange("p (c t) -> p c t", c=4),
                    in_=qmTi[p_ * 4:(p_ + 1) * 4].rearrange("c p t -> p c t")))
            for kc in range(4):
                for r in range(8):
                    pk = pA if r % 2 == 0 else pB
                    with P.step("pe", reads=[latTf, wuk_sb], writes=[pk]) as st:
                        for lc in range(4):
                            st.add(lambda e_, l=lc, k_=kc, r_=r, p_=ps, pk_=pk: e_.matmul(
                                pk_[:, :], wuk_sb[:, l * 1024 + (p_ * 4 + k_) * 128: l * 1024 + (p_ * 4 + k_) * 128 + 128],
                                latTf[:, l * S + r_ * 512: l * S + r_ * 512 + 512],
                                start=(l == 0), stop=(l == 3)))
                    with P.step("dve", reads=[pk], writes=[Kh]) as st:
                        st.add(lambda e_, k_=kc, r_=r, pk_=pk: e_.tensor_copy(
                            Kh[:, k_ * S + r_ * 512: k_ * S + r_ * 512 + 512], pk_[:, :]))
            for c in range(KC):
                pk = pA if c % 2 == 0 else pB
                with P.step("pe", reads=[latTf, wuv_sb], writes=[pk]) as st:
                    for lc in range(4):
                        st.add(lambda e_, l=lc, c_=c, p_=ps, pk_=pk: e_.matmul(
                            pk_[:, :], latTf[:, l * S + c_ * 128: l * S + c_ * 128 + 128],
                            wuv_sb[:, l * 1024 + p_ * 512: l * 1024 + p_ * 512 + 512],
                            start=(l == 0), stop=(l == 3)))
                with P.step("dve", reads=[pk], writes=[Vh]) as st:
                    st.add(lambda e_, c_=c, pk_=pk: e_.tensor_copy(
                        bass.AP(Vh.t, c_ * 520, [[KC * 520, 128], [65, 8], [1, 64]]),
                        pk_[:, :].rearrange("p (h f) -> p h f", h=8)))
            with P.step("dve", writes=[Vh]) as st:
                st.add(lambda e_: e_.memset(
                    bass.AP(Vh.t, 64, [[KC * 520, 128], [520, KC], [65, 8], [1, 1]]), 1.0))

            def emit_resid_stream(P_, oc, g, pP, ps_=ps):
                slab = hsA if (oc + g) % 2 == 0 else hsB
                src = hin if ps_ == 0 else hout
                with P_.step("sp", reads=[src], writes=[slab], dma=True) as st:
                    st.add(lambda e_, o_=oc, g_=g, s_=src: e_.dma_start(
                        out=slab[:, :], in_=s_.t[o_, :, g_ * 256:(g_ + 1) * 256]))
                with P_.step("dve", reads=[pP, slab], writes=[slab]) as st:
                    st.add(lambda e_, pP_=pP, sl_=slab: e_.tensor_add(
                        sl_[:, :], pP_[:, 0:256], sl_[:, :]))
                with P_.step("sp", reads=[slab], writes=[hout], dma=True) as st:
                    st.add(lambda e_, o_=oc, g_=g, sl_=slab: e_.dma_start(
                        out=ho1[o_, :, g_ * 256:(g_ + 1) * 256], in_=sl_[:, :]))

            cfg = {
                "heads": list(range(ps * 8, ps * 8 + 8)),
                "qslice": lambda h, g, p_=ps: qTh[(h % 2) * 64:(h % 2) * 64 + 64,
                                                  ((h - 8 * p_) // 2) * T + g * 256:((h - 8 * p_) // 2) * T + g * 256 + 256],
                "kslice": lambda h, c, p_=ps: Kh[(h % 2) * 64:(h % 2) * 64 + 64,
                                                 ((h - 8 * p_) // 2) * S + c * 128:((h - 8 * p_) // 2) * S + c * 128 + 128],
                "vslice": lambda h, c, p_=ps: Vh[:, c * 520 + (h - 8 * p_) * 65: c * 520 + (h - 8 * p_) * 65 + 65],
                "mask": msk,
                "mslice": lambda g, j: msk[:, (g * 4 + j) * 256:(g * 4 + j + 1) * 256],
                "reads_qkv": [qTh, Kh, Vh],
                "banks": bks,
                "scratch": {"aA": aA, "aB": aB, "den": den, "rden": rden, "outT": outT,
                            "rb": rb, "cst": cst},
                "sel_lo": sel_lo, "sel_hi": sel_hi,
                "wo_sb": wom_sb,
                "emit_resid": emit_resid_stream,
                "scale": 0.125,
                "nch_of_g": lambda g: 4 * g + 4,
                "chunk_of": lambda g, j: j,
                "proj_chunks": [(lc, ps * 4 + lc) for lc in range(4)],
            }
            emit_attention_causal(P, cfg)

        P.final_barrier()
        blk = e(nc.Block())
        P.emit_into_block(blk)
    return nc


def build_launch_E():
    nc = bass.Bass("TRN2", target_bir_lowering=False, debug=False)
    h1i = nc.dram_tensor("h1i", [DC, 128, T], F32, kind="ExternalInput").ap()
    x1i = nc.dram_tensor("x1i", [DC, 128, T], F32, kind="ExternalInput").ap()
    g2i = nc.dram_tensor("g2i", [1024], F32, kind="ExternalInput").ap()
    gwi = nc.dram_tensor("gwi", [1024, 8], F32, kind="ExternalInput").ap()
    w1 = nc.dram_tensor("w1", [8, 1024, 2048], F32, kind="ExternalInput").ap()
    w2 = nc.dram_tensor("w2", [8, 2048, 1024], F32, kind="ExternalInput").ap()
    idi = nc.dram_tensor("idi", [128, 128], BF16, kind="ExternalInput").ap()
    onci = nc.dram_tensor("onci", [128, 1], BF16, kind="ExternalInput").ap()
    onri = nc.dram_tensor("onri", [1, 128], BF16, kind="ExternalInput").ap()
    x2o = nc.dram_tensor("x2o", [DC, 128, T], F32, kind="ExternalOutput").ap()

    from contextlib import ExitStack
    with ExitStack() as ctx:
        e = ctx.enter_context
        P = Prog(nc)
        P.sem_handles(ctx)
        xr = Buf(e(nc.sbuf_tensor("xr", [128, DC * T], F32)), "xr")
        xn = Buf(e(nc.sbuf_tensor("xn", [128, DC * T], BF16)), "xn")
        gw_sb = Buf(e(nc.sbuf_tensor("gw_sb", [128, DC * 8], BF16)), "gw_sb")
        g2_sb = Buf(e(nc.sbuf_tensor("g2_sb", [128, DC], F32)), "g2_sb")
        ident = Buf(e(nc.sbuf_tensor("ident", [128, 128], BF16)), "ident")
        w1A = Buf(e(nc.sbuf_tensor("w1A", [128, 8 * 1024], BF16)), "w1A")
        w1B = Buf(e(nc.sbuf_tensor("w1B", [128, 8 * 1024], BF16)), "w1B")
        w2A = Buf(e(nc.sbuf_tensor("w2A", [128, 8 * 1024], BF16)), "w2A")
        w2B = Buf(e(nc.sbuf_tensor("w2B", [128, 8 * 1024], BF16)), "w2B")
        h1T = Buf(e(nc.sbuf_tensor("h1T", [128, 8 * 512], BF16)), "h1T")
        lg = Buf(e(nc.sbuf_tensor("lg", [128, 8], F32)), "lg")
        m1 = Buf(e(nc.sbuf_tensor("m1", [128, 1], F32)), "m1")
        m2 = Buf(e(nc.sbuf_tensor("m2", [128, 1], F32)), "m2")
        eqm = Buf(e(nc.sbuf_tensor("eqm", [128, 8], F32)), "eqm")
        mskd = Buf(e(nc.sbuf_tensor("mskd", [128, 8], F32)), "mskd")
        E8 = Buf(e(nc.sbuf_tensor("E8", [128, 8], F32)), "E8")
        mm12 = Buf(e(nc.sbuf_tensor("mm12", [128, 2], F32)), "mm12")
        rsum = Buf(e(nc.sbuf_tensor("rsum", [128, 1], F32)), "rsum")
        comb = Buf(e(nc.sbuf_tensor("comb", [128, NT * 8], BF16)), "comb")
        combT = Buf(e(nc.sbuf_tensor("combT", [128, T], BF16)), "combT")
        ybf = Buf(e(nc.sbuf_tensor("ybf", [128, 512], BF16)), "ybf")
        ytmp = Buf(e(nc.sbuf_tensor("ytmp", [128, 512], BF16)), "ytmp")
        slab = Buf(e(nc.sbuf_tensor("slab", [128, 512], F32)), "slab")
        slab2 = Buf(e(nc.sbuf_tensor("slab2", [128, 512], F32)), "slab2")
        onc = Buf(e(nc.sbuf_tensor("onc", [128, 1], BF16)), "onc")
        onr = Buf(e(nc.sbuf_tensor("onr", [1, 128], BF16)), "onr")
        pQ = Buf(e(nc.psum_tensor("pQ", [128, 512], F32)), "pQ")
        pY = Buf(e(nc.psum_tensor("pY", [128, 512], F32)), "pY")
        pG = Buf(e(nc.psum_tensor("pG", [128, 512], F32)), "pG")
        nsc = alloc_norm_scratch(nc, ctx, P)
        consts = {"ones_col": onc, "ones1r": onr}

        with P.step("sp", writes=[xr], dma=True) as st:
            st.add(lambda e_: e_.dma_start(out=xr[:, :].rearrange("p (c t) -> p c t", c=DC),
                                           in_=h1i.rearrange("c p t -> p c t")))
        with P.step("sp", writes=[onc, onr, g2_sb, ident], dma=True) as st:
            st.add(lambda e_: e_.dma_start(out=onc[:, :], in_=onci))
            st.add(lambda e_: e_.dma_start(out=onr[:, :], in_=onri))
            st.add(lambda e_: e_.dma_start(out=ident[:, :], in_=idi))
            for ch in range(DC):
                st.add(lambda e_, c=ch: e_.dma_start(
                    out=g2_sb[:, c:c + 1],
                    in_=g2i[c * 128:(c + 1) * 128].rearrange("(p o) -> p o", o=1)))
        with P.step("pool", writes=[gw_sb], dma=True) as st:
            st.add(lambda e_: e_.dma_start(out=gw_sb[:, :].rearrange("p (c n) -> p c n", c=DC),
                                           in_=gwi.rearrange("(c p) n -> p c n", c=DC)))

        emit_norm(P, ctx, xr, xn, "rms", consts, 1e-6, gain=g2_sb, scratch=nsc)

        for i in range(NT):
            with P.step("pe", reads=[xn, gw_sb], writes=[pG]) as st:
                for ch in range(DC):
                    st.add(lambda e_, c=ch, ii=i: e_.matmul(
                        pG[:, 0:8], xn[:, c * T + ii * 128: c * T + ii * 128 + 128],
                        gw_sb[:, c * 8:(c + 1) * 8],
                        start=(c == 0), stop=(c == DC - 1)))
            with P.step("dve", reads=[pG], writes=[lg]) as st:
                st.add(lambda e_: e_.tensor_copy(lg[:, :], pG[:, 0:8]))
            with P.step("dve", reads=[lg], writes=[m1]) as st:
                st.add(lambda e_: e_.reduce_max(m1[:, :], lg[:, :], axis=AX.X))
            with P.step("dve", reads=[lg, m1], writes=[eqm]) as st:
                st.add(lambda e_: e_.tensor_scalar(eqm[:, :], lg[:, :], m1[:, :], None, ALU.is_ge))
            with P.step("dve", reads=[lg, eqm], writes=[mskd]) as st:
                st.add(lambda e_: e_.scalar_tensor_tensor(mskd[:, :], eqm[:, :], -1e30, lg[:, :],
                                                          ALU.mult, ALU.add))
            with P.step("dve", reads=[mskd], writes=[m2]) as st:
                st.add(lambda e_: e_.reduce_max(m2[:, :], mskd[:, :], axis=AX.X))
            with P.step("act", reads=[lg], writes=[E8]) as st:
                st.add(lambda e_: e_.activation(E8[:, :], lg[:, :], AF.Exp))
            with P.step("dve", reads=[m1], writes=[mm12]) as st:
                st.add(lambda e_: e_.tensor_copy(mm12[:, 0:1], m1[:, :]))
            with P.step("dve", reads=[m2], writes=[mm12]) as st:
                st.add(lambda e_: e_.tensor_copy(mm12[:, 1:2], m2[:, :]))
            with P.step("act", reads=[mm12], writes=[mm12]) as st:
                st.add(lambda e_: e_.activation(mm12[:, :], mm12[:, :], AF.Exp))
            with P.step("dve", reads=[mm12], writes=[rsum]) as st:
                st.add(lambda e_: e_.reduce_sum(rsum[:, :], mm12[:, :], axis=AX.X))
            with P.step("dve", reads=[rsum], writes=[rsum]) as st:
                st.add(lambda e_: e_.reciprocal(rsum[:, :], rsum[:, :]))
            with P.step("dve", reads=[lg, m2], writes=[eqm]) as st:
                st.add(lambda e_: e_.tensor_scalar(eqm[:, :], lg[:, :], m2[:, :], None, ALU.is_ge))
            with P.step("dve", reads=[E8, eqm], writes=[E8]) as st:
                st.add(lambda e_: e_.tensor_mul(E8[:, :], E8[:, :], eqm[:, :]))
            with P.step("dve", reads=[E8, rsum, comb], writes=[comb]) as st:
                st.add(lambda e_, ii=i: e_.tensor_scalar(comb[:, ii * 8:(ii + 1) * 8],
                                                         E8[:, :], rsum[:, :], None, ALU.mult))

        wbufs = [(w1A, w2A), (w1B, w2B)]

        def emit_wload(idx):
            ee, half = divmod(idx, 2)
            w1b, w2b = wbufs[idx % 2]
            with P.step("pool", writes=[w1b], dma=True) as st:
                st.add(lambda e_, E_=ee, H_=half: e_.dma_start(
                    out=w1b[:, :].rearrange("p (c n) -> p c n", c=DC),
                    in_=w1[E_, :, H_ * 1024:(H_ + 1) * 1024].rearrange("(c p) n -> p c n", c=DC)))
            with P.step("pool", writes=[w2b], dma=True) as st:
                st.add(lambda e_, E_=ee, H_=half: e_.dma_start(
                    out=w2b[:, :].rearrange("p (c n) -> p c n", c=DC),
                    in_=w2[E_, H_ * 1024:(H_ + 1) * 1024, :].rearrange("(c p) n -> p c n", c=DC)))

        emit_wload(0)
        for idx in range(16):
            ee, half = divmod(idx, 2)
            w1b, w2b = wbufs[idx % 2]
            if idx + 1 < 16:
                emit_wload(idx + 1)
            if half == 0:
                for i in range(NT):
                    with P.step("pe", reads=[comb, ident], writes=[pG]) as st:
                        st.add(lambda e_, ii=i, E_=ee: e_.matmul(
                            pG[:, 0:128], comb[:, ii * 8 + E_: ii * 8 + E_ + 1].broadcast_to([128, 128]),
                            ident[:, :], start=True, stop=True))
                    with P.step("dve", reads=[pG], writes=[combT]) as st:
                        st.add(lambda e_, ii=i: e_.tensor_copy(
                            combT[:, ii * 128:(ii + 1) * 128], pG[:, 0:128]))
            for tr in range(4):
                for ec in range(8):
                    with P.step("pe", reads=[xn, w1b], writes=[pQ]) as st:
                        for dc in range(DC):
                            st.add(lambda e_, d=dc, E2=ec, t_=tr, wb=w1b: e_.matmul(
                                pQ[:, :], wb[:, d * 1024 + E2 * 128: d * 1024 + E2 * 128 + 128],
                                xn[:, d * T + t_ * 512: d * T + t_ * 512 + 512],
                                start=(d == 0), stop=(d == DC - 1)))
                    with P.step("act", reads=[pQ], writes=[h1T]) as st:
                        st.add(lambda e_, E2=ec: e_.activation(
                            h1T[:, E2 * 512:(E2 + 1) * 512], pQ[:, :], AF.Silu))
                for oc in range(DC):
                    with P.step("pe", reads=[h1T, w2b], writes=[pY]) as st:
                        for ec in range(8):
                            st.add(lambda e_, E2=ec, o_=oc, wb=w2b: e_.matmul(
                                pY[:, :], wb[:, E2 * 1024 + o_ * 128: E2 * 1024 + o_ * 128 + 128],
                                h1T[:, E2 * 512:(E2 + 1) * 512],
                                start=(E2 == 0), stop=(E2 == 7)))
                    with P.step("dve", reads=[pY], writes=[ybf]) as st:
                        st.add(lambda e_: e_.tensor_copy(ybf[:, :], pY[:, :]))
                    with P.step("pool", reads=[ybf, combT], writes=[ytmp]) as st:
                        st.add(lambda e_, t_=tr: e_.tensor_mul(
                            ytmp[:, :], ybf[:, :], combT[:, t_ * 512:(t_ + 1) * 512]))
                    with P.step("pool", reads=[ytmp, xr], writes=[xr]) as st:
                        st.add(lambda e_, o_=oc, t_=tr: e_.tensor_add(
                            xr[:, o_ * T + t_ * 512: o_ * T + t_ * 512 + 512], ytmp[:, :],
                            xr[:, o_ * T + t_ * 512: o_ * T + t_ * 512 + 512]))

        for oc in range(DC):
            for tr in range(4):
                sl = slab if (oc * 4 + tr) % 2 == 0 else slab2
                with P.step("sp", writes=[sl], dma=True) as st:
                    st.add(lambda e_, o_=oc, t_=tr, s_=sl: e_.dma_start(
                        out=s_[:, :], in_=x1i[o_, :, t_ * 512:(t_ + 1) * 512]))
                with P.step("dve", reads=[xr, sl], writes=[sl]) as st:
                    st.add(lambda e_, o_=oc, t_=tr, s_=sl: e_.tensor_add(
                        s_[:, :], xr[:, o_ * T + t_ * 512: o_ * T + t_ * 512 + 512], s_[:, :]))
                with P.step("sp", reads=[sl], dma=True) as st:
                    st.add(lambda e_, o_=oc, t_=tr, s_=sl: e_.dma_start(
                        out=x2o[o_, :, t_ * 512:(t_ + 1) * 512], in_=s_[:, :]))
        P.final_barrier()
        blk = e(nc.Block())
        P.emit_into_block(blk)
    return nc


L1_HEADS = [0, 4, 1, 5, 2, 6, 3, 7, 8, 12, 9, 13, 10, 14, 11, 15]


def emit_qperm_evac(P, pQ, q_tm, nr):
    with P.step("dve", reads=[pQ], writes=[q_tm]) as st:
        st.add(lambda e_, n=nr: e_.tensor_copy(
            bass.AP(q_tm.t, n * 512, [[q_tm.t.shape[1], 128], [64, 2], [128, 4], [1, 64]]),
            pQ[:, :].rearrange("p (q j f) -> p q j f", q=2, j=4)))


def emit_wo_perm_load(P, eng_step_kwargs, wo_sb, wo_dram):
    with P.step("pool", writes=[wo_sb], dma=True) as st:
        for m in range(8):
            h0 = H0[m]
            h64 = H64[m]
            st.add(lambda e_, m_=m, h_=h0: e_.dma_start(
                out=wo_sb.t[0:64, m_ * 1024:(m_ + 1) * 1024],
                in_=wo_dram[h_ * 64:(h_ + 1) * 64, :]))
            st.add(lambda e_, m_=m, h_=h64: e_.dma_start(
                out=wo_sb.t[64:128, m_ * 1024:(m_ + 1) * 1024],
                in_=wo_dram[h_ * 64:(h_ + 1) * 64, :]))


def build_launch_F():
    nc = bass.Bass("TRN2", target_bir_lowering=False, debug=False)
    x2i = nc.dram_tensor("x2i", [DC, 128, T], F32, kind="ExternalInput").ap()
    n1wi = nc.dram_tensor("n1wi", [1024], F32, kind="ExternalInput").ap()
    gwq = nc.dram_tensor("gwq", [1024, 1024], F32, kind="ExternalInput").ap()
    gkvd = nc.dram_tensor("gkvd", [1024, 256], F32, kind="ExternalInput").ap()
    Crgi = nc.dram_tensor("Crgi", [NT, 128, 64], BF16, kind="ExternalInput").ap()
    Srgi = nc.dram_tensor("Srgi", [NT, 128, 64], BF16, kind="ExternalInput").ap()
    onci = nc.dram_tensor("onci", [128, 1], BF16, kind="ExternalInput").ap()
    onri = nc.dram_tensor("onri", [1, 128], BF16, kind="ExternalInput").ap()
    hBo = nc.dram_tensor("hBo", [DC, 128, T], F32, kind="ExternalOutput").ap()
    qgo = nc.dram_tensor("qgo", [NT, 128, 1024], BF16, kind="ExternalOutput").ap()
    latgo = nc.dram_tensor("latgo", [2, 128, T], BF16, kind="ExternalOutput").ap()

    from contextlib import ExitStack
    with ExitStack() as ctx:
        e = ctx.enter_context
        P = Prog(nc)
        P.sem_handles(ctx)
        xr = Buf(e(nc.sbuf_tensor("xr", [128, DC * T], F32)), "xr")
        xn = Buf(e(nc.sbuf_tensor("xn", [128, DC * T], BF16)), "xn")
        wq_sb = Buf(e(nc.sbuf_tensor("wq_sb", [128, DC * 1024], BF16)), "wq_sb")
        wkv_sb = Buf(e(nc.sbuf_tensor("wkv_sb", [128, DC * 256], BF16)), "wkv_sb")
        nw_sb = Buf(e(nc.sbuf_tensor("nw_sb", [128, DC], F32)), "nw_sb")
        q_tm = Buf(e(nc.sbuf_tensor("q_tm", [128, 1024], BF16)), "q_tm")
        lat_sb = Buf(e(nc.sbuf_tensor("lat_sb", [128, 2 * T], BF16)), "lat_sb")
        r1 = Buf(e(nc.sbuf_tensor("r1", [128, 1024], BF16)), "r1")
        r2 = Buf(e(nc.sbuf_tensor("r2", [128, 1024], BF16)), "r2")
        ss = Buf(e(nc.sbuf_tensor("ss", [128, 16], F32)), "ss")
        tC = Buf(e(nc.sbuf_tensor("tC", [128, NT * 64], BF16)), "tC")
        tS = Buf(e(nc.sbuf_tensor("tS", [128, NT * 64], BF16)), "tS")
        onc = Buf(e(nc.sbuf_tensor("onc", [128, 1], BF16)), "onc")
        onr = Buf(e(nc.sbuf_tensor("onr", [1, 128], BF16)), "onr")
        pQ = Buf(e(nc.psum_tensor("pQ", [128, 512], F32)), "pQ")
        nsc = alloc_norm_scratch(nc, ctx, P)
        consts = {"ones_col": onc, "ones1r": onr}
        rsc = {"r1": r1, "r2": r2, "ss": ss, "cst": nsc["cst"]}

        with P.step("sp", writes=[xr], dma=True) as st:
            st.add(lambda e_: e_.dma_start(out=xr[:, :].rearrange("p (c t) -> p c t", c=DC),
                                           in_=x2i.rearrange("c p t -> p c t")))
        with P.step("sp", writes=[onc, onr, nw_sb, tC, tS], dma=True) as st:
            st.add(lambda e_: e_.dma_start(out=onc[:, :], in_=onci))
            st.add(lambda e_: e_.dma_start(out=onr[:, :], in_=onri))
            st.add(lambda e_: e_.dma_start(out=tC[:, :].rearrange("p (i f) -> p i f", i=NT),
                                           in_=Crgi.rearrange("i p f -> p i f")))
            st.add(lambda e_: e_.dma_start(out=tS[:, :].rearrange("p (i f) -> p i f", i=NT),
                                           in_=Srgi.rearrange("i p f -> p i f")))
            for ch in range(DC):
                st.add(lambda e_, c=ch: e_.dma_start(
                    out=nw_sb[:, c:c + 1],
                    in_=n1wi[c * 128:(c + 1) * 128].rearrange("(p o) -> p o", o=1)))
        with P.step("pool", writes=[wq_sb], dma=True) as st:
            st.add(lambda e_: e_.dma_start(out=wq_sb[:, :].rearrange("p (c n) -> p c n", c=DC),
                                           in_=gwq.rearrange("(c p) n -> p c n", c=DC)))
        with P.step("pool", writes=[wkv_sb], dma=True) as st:
            st.add(lambda e_: e_.dma_start(out=wkv_sb[:, :].rearrange("p (c n) -> p c n", c=DC),
                                           in_=gkvd.rearrange("(c p) n -> p c n", c=DC)))

        emit_norm(P, ctx, xr, xn, "ln32", consts, 1e-5, scratch=nsc)
        with P.step("sp", reads=[xr], dma=True) as st:
            st.add(lambda e_: e_.dma_start(out=hBo.rearrange("c p t -> p c t"),
                                           in_=xr[:, :].rearrange("p (c t) -> p c t", c=DC)))
        emit_norm(P, ctx, xr, xn, "rms", consts, 1e-6, gain=nw_sb, scratch=nsc)

        for i in range(NT):
            for nr in range(2):
                with P.step("pe", reads=[xn, wq_sb], writes=[pQ]) as st:
                    for ch in range(DC):
                        st.add(lambda e_, c=ch, n=nr, ii=i: e_.matmul(
                            pQ[:, :], xn[:, c * T + ii * 128: c * T + ii * 128 + 128],
                            wq_sb[:, c * 1024 + n * 512: c * 1024 + n * 512 + 512],
                            start=(c == 0), stop=(c == DC - 1)))
                emit_qperm_evac(P, pQ, q_tm, nr)
            emit_rope(P, q_tm, tC, tS, i * 64, 16, "inter", rsc)
            emit_qknorm(P, q_tm, 16, rsc)
            with P.step("sp", reads=[q_tm], dma=True) as st:
                st.add(lambda e_, ii=i: e_.dma_start(out=qgo[ii], in_=q_tm[:, :]))
        for lc in range(2):
            for r in range(4):
                with P.step("pe", reads=[xn, wkv_sb], writes=[pQ]) as st:
                    for ch in range(DC):
                        st.add(lambda e_, c=ch, l=lc, r_=r: e_.matmul(
                            pQ[:, :], wkv_sb[:, c * 256 + l * 128: c * 256 + l * 128 + 128],
                            xn[:, c * T + r_ * 512: c * T + r_ * 512 + 512],
                            start=(c == 0), stop=(c == DC - 1)))
                with P.step("dve", reads=[pQ], writes=[lat_sb]) as st:
                    st.add(lambda e_, l=lc, r_=r: e_.tensor_copy(
                        lat_sb[:, l * T + r_ * 512: l * T + r_ * 512 + 512], pQ[:, :]))
        with P.step("sp", reads=[lat_sb], dma=True) as st:
            st.add(lambda e_: e_.dma_start(out=latgo.rearrange("c p t -> p c t"),
                                           in_=lat_sb[:, :].rearrange("p (c t) -> p c t", c=2)))
        P.final_barrier()
        blk = e(nc.Block())
        P.emit_into_block(blk)
    return nc


def build_launch_G():
    nc = bass.Bass("TRN2", target_bir_lowering=False, debug=False)
    hBi = nc.dram_tensor("hBi", [DC, 128, T], F32, kind="ExternalInput").ap()
    qgTi = nc.dram_tensor("qgTi", [DC, 128, T], BF16, kind="ExternalInput").ap()
    latgfi = nc.dram_tensor("latgfi", [2, 128, S], BF16, kind="ExternalInput").ap()
    gkup = nc.dram_tensor("gkup", [256, 256], F32, kind="ExternalInput").ap()
    gvup = nc.dram_tensor("gvup", [256, 256], F32, kind="ExternalInput").ap()
    gwo = nc.dram_tensor("gwo", [1024, 1024], F32, kind="ExternalInput").ap()
    mci = nc.dram_tensor("mci", [8, 4, 128, 256], BF16, kind="ExternalInput").ap()
    Crgfi = nc.dram_tensor("Crgfi", [KC, 128, 64], BF16, kind="ExternalInput").ap()
    Srgfi = nc.dram_tensor("Srgfi", [KC, 128, 64], BF16, kind="ExternalInput").ap()
    idi = nc.dram_tensor("idi", [128, 128], BF16, kind="ExternalInput").ap()
    sli = nc.dram_tensor("sli", [1, 128], BF16, kind="ExternalInput").ap()
    shi = nc.dram_tensor("shi", [1, 128], BF16, kind="ExternalInput").ap()
    hB1o = nc.dram_tensor("hB1o", [DC, 128, T], F32, kind="ExternalOutput").ap()

    from contextlib import ExitStack
    with ExitStack() as ctx:
        e = ctx.enter_context
        P = Prog(nc)
        P.sem_handles(ctx)
        xr = Buf(e(nc.sbuf_tensor("xr", [128, DC * T], F32)), "xr")
        qT = Buf(e(nc.sbuf_tensor("qT", [128, DC * T], BF16)), "qT")
        latgf = Buf(e(nc.sbuf_tensor("latgf", [128, 2 * S], BF16)), "latgf")
        kup_sb = Buf(e(nc.sbuf_tensor("kup_sb", [128, 2 * 256], BF16)), "kup_sb")
        vup_sb = Buf(e(nc.sbuf_tensor("vup_sb", [128, 2 * 256], BF16)), "vup_sb")
        wo_sb = Buf(e(nc.sbuf_tensor("wo_sb", [128, DC * 1024], BF16)), "wo_sb")
        Kfm = Buf(e(nc.sbuf_tensor("Kfm", [128, 2 * S], BF16)), "Kfm")
        Vp = Buf(e(nc.sbuf_tensor("Vp", [128, KC * 260], BF16)), "Vp")
        msk = Buf(e(nc.sbuf_tensor("msk", [128, 8 * 4 * 256], BF16)), "msk")
        tCf = Buf(e(nc.sbuf_tensor("tCf", [128, KC * 64], BF16)), "tCf")
        tSf = Buf(e(nc.sbuf_tensor("tSf", [128, KC * 64], BF16)), "tSf")
        ident = Buf(e(nc.sbuf_tensor("ident", [128, 128], BF16)), "ident")
        k_tm = Buf(e(nc.sbuf_tensor("k_tm", [128, 256], BF16)), "k_tm")
        v_tm = Buf(e(nc.sbuf_tensor("v_tm", [128, 256], BF16)), "v_tm")
        r1 = Buf(e(nc.sbuf_tensor("r1", [128, 1024], BF16)), "r1")
        r2 = Buf(e(nc.sbuf_tensor("r2", [128, 1024], BF16)), "r2")
        ss = Buf(e(nc.sbuf_tensor("ss", [128, 16], F32)), "ss")
        sel_lo = Buf(e(nc.sbuf_tensor("sel_lo", [1, 128], BF16)), "sel_lo")
        sel_hi = Buf(e(nc.sbuf_tensor("sel_hi", [1, 128], BF16)), "sel_hi")
        aA = Buf(e(nc.sbuf_tensor("aA", [128, 256], BF16)), "aA")
        aB = Buf(e(nc.sbuf_tensor("aB", [128, 256], BF16)), "aB")
        den = Buf(e(nc.sbuf_tensor("den", [1, 512], F32)), "den")
        rden = Buf(e(nc.sbuf_tensor("rden", [1, 512], BF16)), "rden")
        outT = Buf(e(nc.sbuf_tensor("outT", [128, 2048], BF16)), "outT")
        outTn = Buf(e(nc.sbuf_tensor("outTn", [128, 2048], BF16)), "outTn")
        rb = Buf(e(nc.sbuf_tensor("rb", [128, 256], BF16)), "rb")
        cst = Buf(e(nc.sbuf_tensor("cst", [128, 4], F32)), "cst")
        bks = {}
        for n in ("pA", "pB", "pC", "pD", "pE"):
            bks[n] = Buf(e(nc.psum_tensor(n, [128, 512], F32)), n)
        pT = Buf(e(nc.psum_tensor("pT", [128, 1024], BF16)), "pT")
        rsc = {"r1": r1, "r2": r2, "ss": ss, "cst": cst}

        with P.step("sp", writes=[xr], dma=True) as st:
            st.add(lambda e_: e_.dma_start(out=xr[:, :].rearrange("p (c t) -> p c t", c=DC),
                                           in_=hBi.rearrange("c p t -> p c t")))
        with P.step("sp", writes=[qT, latgf, msk, tCf, tSf, ident, sel_lo, sel_hi], dma=True) as st:
            st.add(lambda e_: e_.dma_start(out=qT[:, :].rearrange("p (c t) -> p c t", c=DC),
                                           in_=qgTi.rearrange("c p t -> p c t")))
            st.add(lambda e_: e_.dma_start(out=latgf[:, :].rearrange("p (c s) -> p c s", c=2),
                                           in_=latgfi.rearrange("c p s -> p c s")))
            st.add(lambda e_: e_.dma_start(out=msk[:, :].rearrange("p (g j q) -> p g j q", g=8, j=4),
                                           in_=mci.rearrange("g j p q -> p g j q")))
            st.add(lambda e_: e_.dma_start(out=tCf[:, :].rearrange("p (i f) -> p i f", i=KC),
                                           in_=Crgfi.rearrange("i p f -> p i f")))
            st.add(lambda e_: e_.dma_start(out=tSf[:, :].rearrange("p (i f) -> p i f", i=KC),
                                           in_=Srgfi.rearrange("i p f -> p i f")))
            st.add(lambda e_: e_.dma_start(out=ident[:, :], in_=idi))
            st.add(lambda e_: e_.dma_start(out=sel_lo[:, :], in_=sli))
            st.add(lambda e_: e_.dma_start(out=sel_hi[:, :], in_=shi))
        with P.step("pool", writes=[kup_sb], dma=True) as st:
            st.add(lambda e_: e_.dma_start(out=kup_sb[:, :].rearrange("p (c n) -> p c n", c=2),
                                           in_=gkup.rearrange("(c p) n -> p c n", c=2)))
        with P.step("pool", writes=[vup_sb], dma=True) as st:
            st.add(lambda e_: e_.dma_start(out=vup_sb[:, :].rearrange("p (c n) -> p c n", c=2),
                                           in_=gvup.rearrange("(c p) n -> p c n", c=2)))
        emit_wo_perm_load(P, None, wo_sb, gwo)
        with P.step("dve", writes=[cst]) as st:
            st.add(lambda e_: e_.memset(cst[:, 3:4], 0.0))

        pA = bks["pA"]
        for c in range(KC):
            with P.step("pe", reads=[latgf, kup_sb], writes=[pA]) as st:
                for lc in range(2):
                    st.add(lambda e_, l=lc, c_=c: e_.matmul(
                        pA[:, 0:256], latgf[:, l * S + c_ * 128: l * S + c_ * 128 + 128],
                        kup_sb[:, l * 256:(l + 1) * 256],
                        start=(l == 0), stop=(l == 1)))
            with P.step("dve", reads=[pA], writes=[k_tm]) as st:
                st.add(lambda e_: e_.tensor_copy(k_tm[:, :], pA[:, 0:256]))
            emit_rope(P, k_tm, tCf, tSf, c * 64, 4, "inter", rsc)
            emit_qknorm(P, k_tm, 4, rsc)
            for half in range(2):
                with P.step("pe", reads=[k_tm, ident], writes=[pT]) as st:
                    st.add(lambda e_, h_=half: e_.transpose(
                        pT[:, 0:128], k_tm[:, h_ * 128:(h_ + 1) * 128], ident[:, :]))
                with P.step("dve", reads=[pT], writes=[Kfm]) as st:
                    st.add(lambda e_, h_=half, c_=c: e_.tensor_copy(
                        Kfm[:, h_ * S + c_ * 128:h_ * S + c_ * 128 + 128], pT[:, 0:128]))
            with P.step("pe", reads=[latgf, vup_sb], writes=[pA]) as st:
                for lc in range(2):
                    st.add(lambda e_, l=lc, c_=c: e_.matmul(
                        pA[:, 0:256], latgf[:, l * S + c_ * 128: l * S + c_ * 128 + 128],
                        vup_sb[:, l * 256:(l + 1) * 256],
                        start=(l == 0), stop=(l == 1)))
            with P.step("dve", reads=[pA], writes=[Vp]) as st:
                st.add(lambda e_, c_=c: e_.tensor_copy(
                    bass.AP(Vp.t, c_ * 260, [[KC * 260, 128], [65, 4], [1, 64]]),
                    pA[:, 0:256].rearrange("p (k f) -> p k f", k=4)))
        with P.step("dve", writes=[Vp]) as st:
            st.add(lambda e_: e_.memset(
                bass.AP(Vp.t, 64, [[KC * 260, 128], [260, KC], [65, 4], [1, 1]]), 1.0))

        cfg = {
            "heads": list(L1_HEADS),
            "qslice": lambda h, g: qT[(gqa_pos(h) % 2) * 64:(gqa_pos(h) % 2) * 64 + 64,
                                      (gqa_pos(h) // 2) * T + g * 256:(gqa_pos(h) // 2) * T + g * 256 + 256],
            "kslice": lambda h, c: Kfm[((h // 4) % 2) * 64:((h // 4) % 2) * 64 + 64,
                                       ((h // 4) // 2) * S + c * 128:((h // 4) // 2) * S + c * 128 + 128],
            "vslice": lambda h, c: Vp[:, c * 260 + (h // 4) * 65: c * 260 + (h // 4) * 65 + 65],
            "mask": msk,
            "mslice": lambda g, j: msk[:, (g * 4 + j) * 256:(g * 4 + j + 1) * 256],
            "reads_qkv": [qT, Kfm, Vp],
            "banks": bks,
            "scratch": {"aA": aA, "aB": aB, "den": den, "rden": rden, "outT": outT,
                        "rb": rb, "cst": cst},
            "sel_lo": sel_lo, "sel_hi": sel_hi,
            "wo_sb": wo_sb,
            "emit_resid": (lambda P_, oc, g, pP: _resid_add(P_, xr, oc, g, pP)),
            "scale": 0.125,
            "nch_of_g": lambda g: 4 * g + 4,
            "chunk_of": lambda g, j: j,
        }
        emit_attention_causal(P, cfg)

        with P.step("sp", reads=[xr], dma=True) as st:
            st.add(lambda e_: e_.dma_start(out=hB1o.rearrange("c p t -> p c t"),
                                           in_=xr[:, :].rearrange("p (c t) -> p c t", c=DC)))
        P.final_barrier()
        blk = e(nc.Block())
        P.emit_into_block(blk)
    return nc


def build_launch_H():
    nc = bass.Bass("TRN2", target_bir_lowering=False, debug=False)
    hB1i = nc.dram_tensor("hB1i", [DC, 128, T], F32, kind="ExternalInput").ap()
    n2wi = nc.dram_tensor("n2wi", [1024], F32, kind="ExternalInput").ap()
    lwq = nc.dram_tensor("lwq", [1024, 1024], F32, kind="ExternalInput").ap()
    lwk = nc.dram_tensor("lwk", [1024, 256], F32, kind="ExternalInput").ap()
    lwv = nc.dram_tensor("lwv", [1024, 256], F32, kind="ExternalInput").ap()
    Crli = nc.dram_tensor("Crli", [NT, 128, 64], BF16, kind="ExternalInput").ap()
    Srli = nc.dram_tensor("Srli", [NT, 128, 64], BF16, kind="ExternalInput").ap()
    onci = nc.dram_tensor("onci", [128, 1], BF16, kind="ExternalInput").ap()
    onri = nc.dram_tensor("onri", [1, 128], BF16, kind="ExternalInput").ap()
    qlo = nc.dram_tensor("qlo", [NT, 128, 1024], BF16, kind="ExternalOutput").ap()
    klo = nc.dram_tensor("klo", [NT, 128, 256], BF16, kind="ExternalOutput").ap()
    vlo = nc.dram_tensor("vlo", [NT, 128, 256], BF16, kind="ExternalOutput").ap()

    from contextlib import ExitStack
    with ExitStack() as ctx:
        e = ctx.enter_context
        P = Prog(nc)
        P.sem_handles(ctx)
        xr = Buf(e(nc.sbuf_tensor("xr", [128, DC * T], F32)), "xr")
        xn = Buf(e(nc.sbuf_tensor("xn", [128, DC * T], BF16)), "xn")
        wq_sb = Buf(e(nc.sbuf_tensor("wq_sb", [128, DC * 1024], BF16)), "wq_sb")
        wkv_sb = Buf(e(nc.sbuf_tensor("wkv_sb", [128, DC * 512], BF16)), "wkv_sb")
        nw_sb = Buf(e(nc.sbuf_tensor("nw_sb", [128, DC], F32)), "nw_sb")
        q_tm = Buf(e(nc.sbuf_tensor("q_tm", [128, 1024], BF16)), "q_tm")
        k_tm = Buf(e(nc.sbuf_tensor("k_tm", [128, 256], BF16)), "k_tm")
        v_tm = Buf(e(nc.sbuf_tensor("v_tm", [128, 256], BF16)), "v_tm")
        r1 = Buf(e(nc.sbuf_tensor("r1", [128, 1024], BF16)), "r1")
        r2 = Buf(e(nc.sbuf_tensor("r2", [128, 1024], BF16)), "r2")
        ss = Buf(e(nc.sbuf_tensor("ss", [128, 16], F32)), "ss")
        tC = Buf(e(nc.sbuf_tensor("tC", [128, NT * 64], BF16)), "tC")
        tS = Buf(e(nc.sbuf_tensor("tS", [128, NT * 64], BF16)), "tS")
        onc = Buf(e(nc.sbuf_tensor("onc", [128, 1], BF16)), "onc")
        onr = Buf(e(nc.sbuf_tensor("onr", [1, 128], BF16)), "onr")
        pQ = Buf(e(nc.psum_tensor("pQ", [128, 512], F32)), "pQ")
        nsc = alloc_norm_scratch(nc, ctx, P)
        consts = {"ones_col": onc, "ones1r": onr}
        rsc = {"r1": r1, "r2": r2, "ss": ss, "cst": nsc["cst"]}

        with P.step("sp", writes=[xr], dma=True) as st:
            st.add(lambda e_: e_.dma_start(out=xr[:, :].rearrange("p (c t) -> p c t", c=DC),
                                           in_=hB1i.rearrange("c p t -> p c t")))
        with P.step("sp", writes=[onc, onr, nw_sb, tC, tS], dma=True) as st:
            st.add(lambda e_: e_.dma_start(out=onc[:, :], in_=onci))
            st.add(lambda e_: e_.dma_start(out=onr[:, :], in_=onri))
            st.add(lambda e_: e_.dma_start(out=tC[:, :].rearrange("p (i f) -> p i f", i=NT),
                                           in_=Crli.rearrange("i p f -> p i f")))
            st.add(lambda e_: e_.dma_start(out=tS[:, :].rearrange("p (i f) -> p i f", i=NT),
                                           in_=Srli.rearrange("i p f -> p i f")))
            for ch in range(DC):
                st.add(lambda e_, c=ch: e_.dma_start(
                    out=nw_sb[:, c:c + 1],
                    in_=n2wi[c * 128:(c + 1) * 128].rearrange("(p o) -> p o", o=1)))
        with P.step("pool", writes=[wq_sb], dma=True) as st:
            st.add(lambda e_: e_.dma_start(out=wq_sb[:, :].rearrange("p (c n) -> p c n", c=DC),
                                           in_=lwq.rearrange("(c p) n -> p c n", c=DC)))
        with P.step("pool", writes=[wkv_sb], dma=True) as st:
            st.add(lambda e_: e_.dma_start(
                out=bass.AP(wkv_sb.t, 0, [[DC * 512, 128], [512, DC], [1, 256]]),
                in_=lwk.rearrange("(c p) n -> p c n", c=DC)))
            st.add(lambda e_: e_.dma_start(
                out=bass.AP(wkv_sb.t, 256, [[DC * 512, 128], [512, DC], [1, 256]]),
                in_=lwv.rearrange("(c p) n -> p c n", c=DC)))

        emit_norm(P, ctx, xr, xn, "rms", consts, 1e-6, gain=nw_sb, scratch=nsc)

        for i in range(NT):
            for nr in range(3):
                with P.step("pe", reads=[xn, wq_sb, wkv_sb], writes=[pQ]) as st:
                    for ch in range(DC):
                        if nr < 2:
                            w_ap = wq_sb[:, ch * 1024 + nr * 512: ch * 1024 + nr * 512 + 512]
                        else:
                            w_ap = wkv_sb[:, ch * 512: ch * 512 + 512]
                        st.add(lambda e_, c=ch, w=w_ap, ii=i: e_.matmul(
                            pQ[:, :], xn[:, c * T + ii * 128: c * T + ii * 128 + 128], w,
                            start=(c == 0), stop=(c == DC - 1)))
                if nr < 2:
                    emit_qperm_evac(P, pQ, q_tm, nr)
                else:
                    with P.step("dve", reads=[pQ], writes=[k_tm, v_tm]) as st:
                        st.add(lambda e_: e_.tensor_copy(k_tm[:, :], pQ[:, 0:256]))
                        st.add(lambda e_: e_.tensor_copy(v_tm[:, :], pQ[:, 256:512]))
            emit_rope(P, q_tm, tC, tS, i * 64, 16, "inter", rsc)
            emit_qknorm(P, q_tm, 16, rsc)
            emit_rope(P, k_tm, tC, tS, i * 64, 4, "inter", rsc)
            emit_qknorm(P, k_tm, 4, rsc)
            with P.step("sp", reads=[q_tm, k_tm, v_tm], dma=True) as st:
                st.add(lambda e_, ii=i: e_.dma_start(out=qlo[ii], in_=q_tm[:, :]))
                st.add(lambda e_, ii=i: e_.dma_start(out=klo[ii], in_=k_tm[:, :]))
                st.add(lambda e_, ii=i: e_.dma_start(out=vlo[ii], in_=v_tm[:, :]))
        P.final_barrier()
        blk = e(nc.Block())
        P.emit_into_block(blk)
    return nc


def build_launch_I():
    nc = bass.Bass("TRN2", target_bir_lowering=False, debug=False)
    hB1i = nc.dram_tensor("hB1i", [DC, 128, T], F32, kind="ExternalInput").ap()
    x2i = nc.dram_tensor("x2i", [DC, 128, T], F32, kind="ExternalInput").ap()
    qlTi = nc.dram_tensor("qlTi", [DC, 128, T], BF16, kind="ExternalInput").ap()
    Klfmi = nc.dram_tensor("Klfmi", [2, 128, S], BF16, kind="ExternalInput").ap()
    Vltmi = nc.dram_tensor("Vltmi", [KC, 128, 256], BF16, kind="ExternalInput").ap()
    mli = nc.dram_tensor("mli", [NT, 4, 128, 128], BF16, kind="ExternalInput").ap()
    lwo = nc.dram_tensor("lwo", [1024, 1024], F32, kind="ExternalInput").ap()
    w4i = nc.dram_tensor("w4i", [1024, 1024], F32, kind="ExternalInput").ap()
    idi = nc.dram_tensor("idi", [128, 128], BF16, kind="ExternalInput").ap()
    onci = nc.dram_tensor("onci", [128, 1], BF16, kind="ExternalInput").ap()
    onri = nc.dram_tensor("onri", [1, 128], BF16, kind="ExternalInput").ap()
    sli = nc.dram_tensor("sli", [1, 128], BF16, kind="ExternalInput").ap()
    shi = nc.dram_tensor("shi", [1, 128], BF16, kind="ExternalInput").ap()
    xfo = nc.dram_tensor("xfo", [DC, 128, T], F32, kind="ExternalOutput").ap()

    from contextlib import ExitStack
    with ExitStack() as ctx:
        e = ctx.enter_context
        P = Prog(nc)
        P.sem_handles(ctx)
        xr = Buf(e(nc.sbuf_tensor("xr", [128, DC * T], F32)), "xr")
        xnr = Buf(e(nc.sbuf_tensor("xnr", [128, DC * 512], BF16)), "xnr")
        qT = Buf(e(nc.sbuf_tensor("qT", [128, DC * T], BF16)), "qT")
        Kfm = Buf(e(nc.sbuf_tensor("Kfm", [128, 2 * S], BF16)), "Kfm")
        Vp = Buf(e(nc.sbuf_tensor("Vp", [128, KC * 260], BF16)), "Vp")
        msk = Buf(e(nc.sbuf_tensor("msk", [128, NT * 4 * 128], BF16)), "msk")
        wo_sb = Buf(e(nc.sbuf_tensor("wo_sb", [128, DC * 1024], BF16)), "wo_sb")
        ident = Buf(e(nc.sbuf_tensor("ident", [128, 128], BF16)), "ident")
        sel_lo = Buf(e(nc.sbuf_tensor("sel_lo", [1, 128], BF16)), "sel_lo")
        sel_hi = Buf(e(nc.sbuf_tensor("sel_hi", [1, 128], BF16)), "sel_hi")
        aA = Buf(e(nc.sbuf_tensor("aA", [128, 256], BF16)), "aA")
        aB = Buf(e(nc.sbuf_tensor("aB", [128, 256], BF16)), "aB")
        den = Buf(e(nc.sbuf_tensor("den", [1, 512], F32)), "den")
        rden = Buf(e(nc.sbuf_tensor("rden", [1, 512], BF16)), "rden")
        outT = Buf(e(nc.sbuf_tensor("outT", [128, 1024], BF16)), "outT")
        outTn = Buf(e(nc.sbuf_tensor("outTn", [128, 1024], BF16)), "outTn")
        rb = Buf(e(nc.sbuf_tensor("rb", [128, 256], BF16)), "rb")
        cst = Buf(e(nc.sbuf_tensor("cst", [128, 4], F32)), "cst")
        slab = Buf(e(nc.sbuf_tensor("slab", [128, 512], F32)), "slab")
        slab2 = slab
        onc = Buf(e(nc.sbuf_tensor("onc", [128, 1], BF16)), "onc")
        onr = Buf(e(nc.sbuf_tensor("onr", [1, 128], BF16)), "onr")
        bks = {}
        for n in ("pA", "pB", "pC", "pD", "pE"):
            bks[n] = Buf(e(nc.psum_tensor(n, [128, 512], F32)), n)
        pT = Buf(e(nc.psum_tensor("pT", [128, 1024], BF16)), "pT")
        nsc = alloc_norm_scratch(nc, ctx, P, banks=bks, with_xc=True)
        consts = {"ones_col": onc, "ones1r": onr}

        with P.step("sp", writes=[xr], dma=True) as st:
            st.add(lambda e_: e_.dma_start(out=xr[:, :].rearrange("p (c t) -> p c t", c=DC),
                                           in_=hB1i.rearrange("c p t -> p c t")))
        with P.step("sp", writes=[qT, Kfm, Vp, msk, ident, sel_lo, sel_hi, onc, onr], dma=True) as st:
            st.add(lambda e_: e_.dma_start(out=qT[:, :].rearrange("p (c t) -> p c t", c=DC),
                                           in_=qlTi.rearrange("c p t -> p c t")))
            st.add(lambda e_: e_.dma_start(out=Kfm[:, :].rearrange("p (c s) -> p c s", c=2),
                                           in_=Klfmi.rearrange("c p s -> p c s")))
            for k in range(4):
                st.add(lambda e_, k_=k: e_.dma_start(
                    out=bass.AP(Vp.t, k_ * 65, [[KC * 260, 128], [260, KC], [1, 64]]),
                    in_=Vltmi[:, :, k_ * 64:(k_ + 1) * 64].rearrange("c p f -> p c f")))
            st.add(lambda e_: e_.dma_start(out=msk[:, :].rearrange("p (g j q) -> p g j q", g=NT, j=4),
                                           in_=mli.rearrange("g j p q -> p g j q")))
            st.add(lambda e_: e_.dma_start(out=ident[:, :], in_=idi))
            st.add(lambda e_: e_.dma_start(out=sel_lo[:, :], in_=sli))
            st.add(lambda e_: e_.dma_start(out=sel_hi[:, :], in_=shi))
            st.add(lambda e_: e_.dma_start(out=onc[:, :], in_=onci))
            st.add(lambda e_: e_.dma_start(out=onr[:, :], in_=onri))
        emit_wo_perm_load(P, None, wo_sb, lwo)
        with P.step("dve", writes=[Vp]) as st:
            st.add(lambda e_: e_.memset(
                bass.AP(Vp.t, 64, [[KC * 260, 128], [260, KC], [65, 4], [1, 1]]), 1.0))
        with P.step("dve", writes=[cst]) as st:
            st.add(lambda e_: e_.memset(cst[:, 3:4], 0.0))

        cfg = {
            "heads": list(L1_HEADS),
            "qwidth": 128,
            "ngroups": NT,
            "qslice": lambda h, g: qT[(gqa_pos(h) % 2) * 64:(gqa_pos(h) % 2) * 64 + 64,
                                      (gqa_pos(h) // 2) * T + g * 128:(gqa_pos(h) // 2) * T + g * 128 + 128],
            "kslice": lambda h, c: Kfm[((h // 4) % 2) * 64:((h // 4) % 2) * 64 + 64,
                                       ((h // 4) // 2) * S + c * 128:((h // 4) // 2) * S + c * 128 + 128],
            "vslice": lambda h, c: Vp[:, c * 260 + (h // 4) * 65: c * 260 + (h // 4) * 65 + 65],
            "mask": msk,
            "mslice": lambda g, j: msk[:, (g * 4 + j) * 128:(g * 4 + j + 1) * 128],
            "reads_qkv": [qT, Kfm, Vp],
            "banks": bks,
            "scratch": {"aA": aA, "aB": aB, "den": den, "rden": rden, "outT": outT,
                        "rb": rb, "cst": cst},
            "sel_lo": sel_lo, "sel_hi": sel_hi,
            "wo_sb": wo_sb,
            "emit_resid": (lambda P_, oc, g, pP: _resid_add(P_, xr, oc, g, pP, w=128)),
            "scale": 0.125,
            "nch_of_g": lambda g: 4,
            "chunk_of": lambda g, j: max(0, 2 * g - 2 + j),
        }
        emit_attention_causal(P, cfg)

        for oc in range(DC):
            for tr in range(4):
                sl = slab if (oc * 4 + tr) % 2 == 0 else slab2
                with P.step("sp", writes=[sl], dma=True) as st:
                    st.add(lambda e_, o_=oc, t_=tr, s_=sl: e_.dma_start(
                        out=s_[:, :], in_=x2i[o_, :, t_ * 512:(t_ + 1) * 512]))
                with P.step("dve", reads=[xr, sl], writes=[xr]) as st:
                    st.add(lambda e_, o_=oc, t_=tr, s_=sl: e_.tensor_add(
                        xr[:, o_ * T + t_ * 512: o_ * T + t_ * 512 + 512],
                        xr[:, o_ * T + t_ * 512: o_ * T + t_ * 512 + 512], s_[:, :]))

        emit_norm(P, ctx, xr, None, "ln_stats", consts, 1e-5, scratch=nsc)

        w4_sb, w4T = qT, Kfm
        with P.step("pool", writes=[w4_sb], dma=True) as st:
            st.add(lambda e_: e_.dma_start(out=w4_sb[:, 0:8192].rearrange("p (c n) -> p c n", c=DC),
                                           in_=w4i.rearrange("(c p) n -> p c n", c=DC)))
        for ch in range(DC):
            with P.step("act", reads=[w4_sb], writes=[w4_sb]) as st:
                st.add(lambda e_, c=ch: e_.activation(
                    w4_sb[:, c * 1024:(c + 1) * 1024], w4_sb[:, c * 1024:(c + 1) * 1024], AF.Sign))
        for oc in range(DC):
            for ic in range(DC):
                with P.step("pe", reads=[w4_sb, ident], writes=[pT]) as st:
                    st.add(lambda e_, o_=oc, i_=ic: e_.transpose(
                        pT[:, 0:128], w4_sb[:, o_ * 1024 + i_ * 128: o_ * 1024 + i_ * 128 + 128],
                        ident[:, :]))
                with P.step("dve", reads=[pT], writes=[w4T]) as st:
                    st.add(lambda e_, o_=oc, i_=ic: e_.tensor_copy(
                        w4T[:, i_ * 1024 + o_ * 128: i_ * 1024 + o_ * 128 + 128], pT[:, 0:128]))
        pA, pB = bks["pA"], bks["pB"]
        pB1, pB2 = nsc["pB1"], nsc["pB2"]
        srstd, smr, rbn, mrbn, onrB = nsc["srstd"], nsc["smr"], nsc["rb"], nsc["mrb"], consts["ones1r"]
        for r in range(4):
            with P.step("pe", reads=[srstd, onr], writes=[pB1]) as st:
                st.add(lambda e_, r_=r: e_.matmul(pB1[:, :], onr[0:1, :],
                                                  srstd[0:1, r_ * 512:(r_ + 1) * 512],
                                                  start=True, stop=True))
            with P.step("dve", reads=[pB1], writes=[rbn]) as st:
                st.add(lambda e_: e_.tensor_copy(rbn[:, :], pB1[:, :]))
            with P.step("pe", reads=[smr, onr], writes=[pB2]) as st:
                st.add(lambda e_, r_=r: e_.matmul(pB2[:, :], onr[0:1, :],
                                                  smr[0:1, r_ * 512:(r_ + 1) * 512],
                                                  start=True, stop=True))
            with P.step("dve", reads=[pB2], writes=[mrbn]) as st:
                st.add(lambda e_: e_.tensor_copy(mrbn[:, :], pB2[:, :]))
            for ic in range(DC):
                with P.step("dve", reads=[xr, rbn], writes=[xnr]) as st:
                    st.add(lambda e_, i_=ic, r_=r: e_.tensor_mul(
                        xnr[:, i_ * 512:(i_ + 1) * 512],
                        xr[:, i_ * T + r_ * 512: i_ * T + r_ * 512 + 512], rbn[:, :]))
                with P.step("dve", reads=[xnr, mrbn], writes=[xnr]) as st:
                    st.add(lambda e_, i_=ic: e_.tensor_sub(
                        xnr[:, i_ * 512:(i_ + 1) * 512], xnr[:, i_ * 512:(i_ + 1) * 512], mrbn[:, :]))
            for oc in range(DC):
                pP = pA if oc % 2 == 0 else pB
                with P.step("pe", reads=[xnr, w4T], writes=[pP]) as st:
                    for ic in range(DC):
                        st.add(lambda e_, i_=ic, o_=oc, pP_=pP: e_.matmul(
                            pP_[:, :], w4T[:, i_ * 1024 + o_ * 128: i_ * 1024 + o_ * 128 + 128],
                            xnr[:, i_ * 512:(i_ + 1) * 512],
                            start=(i_ == 0), stop=(i_ == DC - 1)))
                with P.step("dve", reads=[pP, xr], writes=[xr]) as st:
                    st.add(lambda e_, o_=oc, r_=r, pP_=pP: e_.tensor_add(
                        xr[:, o_ * T + r_ * 512: o_ * T + r_ * 512 + 512], pP_[:, :],
                        xr[:, o_ * T + r_ * 512: o_ * T + r_ * 512 + 512]))

        with P.step("sp", reads=[xr], dma=True) as st:
            st.add(lambda e_: e_.dma_start(out=xfo.rearrange("c p t -> p c t"),
                                           in_=xr[:, :].rearrange("p (c t) -> p c t", c=DC)))
        P.final_barrier()
        blk = e(nc.Block())
        P.emit_into_block(blk)
    return nc



def build_fused(stop_after=None):
    nc = bass.Bass("TRN2", target_bir_lowering=False, debug=False)
    A_ = lambda *a, **k: nc.dram_tensor(*a, **k).ap()
    xin = A_("xin", [DC, 128, T], F32, kind="ExternalInput")
    wq0 = A_("wq0", [1024, 1024], F32, kind="ExternalInput")
    wk0 = A_("wk0", [1024, 256], F32, kind="ExternalInput")
    wv0 = A_("wv0", [1024, 256], F32, kind="ExternalInput")
    wo0 = A_("wo0", [1024, 1024], F32, kind="ExternalInput")
    g1i = A_("g1i", [1024], F32, kind="ExternalInput")
    wqm = A_("wqm", [1024, 1024], F32, kind="ExternalInput")
    wdkv = A_("wdkv", [1024, 512], F32, kind="ExternalInput")
    wuk = A_("wuk", [512, 1024], F32, kind="ExternalInput")
    wuv = A_("wuv", [512, 1024], F32, kind="ExternalInput")
    wom = A_("wom", [1024, 1024], F32, kind="ExternalInput")
    g2i = A_("g2i", [1024], F32, kind="ExternalInput")
    gwi = A_("gwi", [1024, 8], F32, kind="ExternalInput")
    w1d = A_("w1d", [8, 1024, 2048], F32, kind="ExternalInput")
    w2d = A_("w2d", [8, 2048, 1024], F32, kind="ExternalInput")
    n1wi = A_("n1wi", [1024], F32, kind="ExternalInput")
    n2wi = A_("n2wi", [1024], F32, kind="ExternalInput")
    gwq = A_("gwq", [1024, 1024], F32, kind="ExternalInput")
    gkvd = A_("gkvd", [1024, 256], F32, kind="ExternalInput")
    gkup = A_("gkup", [256, 256], F32, kind="ExternalInput")
    gvup = A_("gvup", [256, 256], F32, kind="ExternalInput")
    gwo = A_("gwo", [1024, 1024], F32, kind="ExternalInput")
    lwq = A_("lwq", [1024, 1024], F32, kind="ExternalInput")
    lwk = A_("lwk", [1024, 256], F32, kind="ExternalInput")
    lwv = A_("lwv", [1024, 256], F32, kind="ExternalInput")
    lwo = A_("lwo", [1024, 1024], F32, kind="ExternalInput")
    w4i = A_("w4i", [1024, 1024], F32, kind="ExternalInput")
    Cr0i = A_("Cr0i", [NT, 128, 64], BF16, kind="ExternalInput")
    Sr0i = A_("Sr0i", [NT, 128, 64], BF16, kind="ExternalInput")
    Crgi = A_("Crgi", [NT, 128, 64], BF16, kind="ExternalInput")
    Srgi = A_("Srgi", [NT, 128, 64], BF16, kind="ExternalInput")
    Crli = A_("Crli", [NT, 128, 64], BF16, kind="ExternalInput")
    Srli = A_("Srli", [NT, 128, 64], BF16, kind="ExternalInput")
    Crgfi = A_("Crgfi", [KC, 128, 64], BF16, kind="ExternalInput")
    Srgfi = A_("Srgfi", [KC, 128, 64], BF16, kind="ExternalInput")
    mci = A_("mci", [4, 8, 128, 512], BF16, kind="ExternalInput")
    mlii = A_("mlii", [NT, 4, 128, 128], BF16, kind="ExternalInput")
    idi = A_("idi", [128, 128], BF16, kind="ExternalInput")
    onci = A_("onci", [128, 1], BF16, kind="ExternalInput")
    onri = A_("onri", [1, 128], BF16, kind="ExternalInput")
    sli = A_("sli", [1, 128], BF16, kind="ExternalInput")
    shi = A_("shi", [1, 128], BF16, kind="ExternalInput")
    xfo = A_("xfo", [DC, 128, T], F32, kind="ExternalOutput")

    xd1 = Buf(nc.dram_tensor("xd1", [DC, 128, T], F32, kind="Internal").ap(), "xd1")
    xd2 = Buf(nc.dram_tensor("xd2", [DC, 128, T], F32, kind="Internal").ap(), "xd2")
    hd = Buf(nc.dram_tensor("hd", [DC, 128, T], F32, kind="Internal").ap(), "hd")
    hd1 = Buf(nc.dram_tensor("hd1", [DC, 128, T], F32, kind="Internal").ap(), "hd1")
    cc0i = Buf(nc.dram_tensor("cc0i", [T, 512], BF16, kind="Internal").ap(), "cc0i")
    cc0o = Buf(nc.dram_tensor("cc0o", [2 * T, 512], BF16, kind="Internal").ap(), "cc0o")
    ccMi = Buf(nc.dram_tensor("ccMi", [512, T], BF16, kind="Internal").ap(), "ccMi")
    ccMo = Buf(nc.dram_tensor("ccMo", [1024, T], BF16, kind="Internal").ap(), "ccMo")
    ccGi = Buf(nc.dram_tensor("ccGi", [256, T], BF16, kind="Internal").ap(), "ccGi")
    ccGo = Buf(nc.dram_tensor("ccGo", [512, T], BF16, kind="Internal").ap(), "ccGo")
    ccLi = Buf(nc.dram_tensor("ccLi", [T, 512], BF16, kind="Internal").ap(), "ccLi")
    ccLo = Buf(nc.dram_tensor("ccLo", [2 * T, 512], BF16, kind="Internal").ap(), "ccLo")
    PAIRS = [[0, 1], [2, 3], [4, 5], [6, 7]]

    from contextlib import ExitStack
    with ExitStack() as OU:
        e = OU.enter_context
        P = Prog(nc)
        P.sem_handles(OU)
        ccsems = [P.cc_sem(OU) for _ in range(4)]
        xr = Buf(e(nc.sbuf_tensor("xr", [128, DC * T], F32)), "xr")
        qTa = Buf(e(nc.sbuf_tensor("qTa", [128, DC * T], BF16)), "qTa")

        ident = Buf(e(nc.sbuf_tensor("ident", [128, 128], BF16)), "ident")
        onc = Buf(e(nc.sbuf_tensor("onc", [128, 1], BF16)), "onc")
        onr = Buf(e(nc.sbuf_tensor("onr", [1, 128], BF16)), "onr")
        sel_lo = Buf(e(nc.sbuf_tensor("sel_lo", [1, 128], BF16)), "sel_lo")
        sel_hi = Buf(e(nc.sbuf_tensor("sel_hi", [1, 128], BF16)), "sel_hi")
        cst = Buf(e(nc.sbuf_tensor("cst", [128, 4], F32)), "cst")
        bks = {}
        for n in ("pA", "pB", "pC", "pD", "pE", "pF"):
            bks[n] = Buf(e(nc.psum_tensor(n, [128, 512], F32)), n)
        pT = Buf(e(nc.psum_tensor("pT", [128, 1024], BF16)), "pT")
        pT2 = Buf(e(nc.psum_tensor("pT2", [128, 1024], BF16)), "pT2")
        consts = {"ones_col": onc, "ones1r": onr}

        with P.step("sp", writes=[xr], dma=True) as st:
            st.add(lambda e_: e_.dma_start(out=xr[:, :].rearrange("p (c t) -> p c t", c=DC),
                                           in_=xin.rearrange("c p t -> p c t")))
        with P.step("sp", writes=[ident, onc, onr, sel_lo, sel_hi], dma=True) as st:
            st.add(lambda e_: e_.dma_start(out=ident[:, :], in_=idi))
            st.add(lambda e_: e_.dma_start(out=onc[:, :], in_=onci))
            st.add(lambda e_: e_.dma_start(out=onr[:, :], in_=onri))
            st.add(lambda e_: e_.dma_start(out=sel_lo[:, :], in_=sli))
            st.add(lambda e_: e_.dma_start(out=sel_hi[:, :], in_=shi))
        with P.step("dve", writes=[cst]) as st:
            st.add(lambda e_: e_.memset(cst[:, 3:4], 0.0))

        def load_tabs(tabC, tabS, Ci, Si, n=NT):
            with P.step("sp", writes=[tabC, tabS], dma=True) as st:
                st.add(lambda e_: e_.dma_start(
                    out=tabC[:, 0:n * 64].rearrange("p (i f) -> p i f", i=n),
                    in_=Ci.rearrange("i p f -> p i f")))
                st.add(lambda e_: e_.dma_start(
                    out=tabS[:, 0:n * 64].rearrange("p (i f) -> p i f", i=n),
                    in_=Si.rearrange("i p f -> p i f")))


        def load_w(dst, src, nch, ncol):
            with P.step("pool", writes=[dst], dma=True) as st:
                st.add(lambda e_: e_.dma_start(
                    out=dst[:, 0:nch * ncol].rearrange("p (c n) -> p c n", c=nch),
                    in_=src.rearrange("(c p) n -> p c n", c=nch)))

        def load_gain(dst, src):
            with P.step("sp", writes=[dst], dma=True) as st:
                for ch in range(DC):
                    st.add(lambda e_, c=ch: e_.dma_start(
                        out=dst[:, c:c + 1],
                        in_=src[c * 128:(c + 1) * 128].rearrange("(p o) -> p o", o=1)))

        def q_transpose_out(PP, q_tm, i):
            for ch in range(DC):
                pX = pT if ch % 2 == 0 else pT2
                with PP.step("pe", reads=[q_tm, ident], writes=[pX]) as st:
                    st.add(lambda e_, c=ch, p_=pX: e_.transpose(
                        p_[:, 0:128], q_tm[:, c * 128:(c + 1) * 128], ident[:, :]))
                with PP.step("dve", reads=[pX], writes=[qTa]) as st:
                    st.add(lambda e_, c=ch, ii=i, p_=pX: e_.tensor_copy(
                        qTa[:, c * T + ii * 128: c * T + ii * 128 + 128], p_[:, 0:128]))

        def _phase1():
            with ExitStack() as SC:
                es = SC.enter_context
                xn = Buf(es(nc.sbuf_tensor("A_xn", [128, DC * T], BF16)), "xn")
                wq_sb = Buf(es(nc.sbuf_tensor("A_wq", [128, DC * 1024], BF16)), "wq_sb")
                wkv_sb = Buf(es(nc.sbuf_tensor("A_wkv", [128, DC * 512], BF16)), "wkv_sb")
                q_tmP = [Buf(es(nc.sbuf_tensor(f"A_qtm{x}", [128, 1024], BF16)), "q_tm") for x in range(2)]
                k_tmP = [Buf(es(nc.sbuf_tensor(f"A_ktm{x}", [128, 256], BF16)), "k_tm") for x in range(2)]
                v_tmP = [Buf(es(nc.sbuf_tensor(f"A_vtm{x}", [128, 256], BF16)), "v_tm") for x in range(2)]
                rscP = [{"r1": Buf(es(nc.sbuf_tensor(f"A_r1{x}", [128, 1024], BF16)), "r1"),
                         "r2": Buf(es(nc.sbuf_tensor(f"A_r2{x}", [128, 1024], BF16)), "r2"),
                         "cst": cst} for x in range(2)]
                tabC = Buf(es(nc.sbuf_tensor("A_tC", [128, 1024], BF16)), "tabC")
                tabS = Buf(es(nc.sbuf_tensor("A_tS", [128, 1024], BF16)), "tabS")
                nsc = alloc_norm_scratch(nc, SC, P, banks=bks, with_xc=False)
                load_tabs(tabC, tabS, Cr0i, Sr0i)
                load_w(wq_sb, wq0, DC, 1024)
                with P.step("pool", writes=[wkv_sb], dma=True) as st:
                    st.add(lambda e_: e_.dma_start(
                        out=bass.AP(wkv_sb.t, 0, [[DC * 512, 128], [512, DC], [1, 256]]),
                        in_=wk0.rearrange("(c p) n -> p c n", c=DC)))
                    st.add(lambda e_: e_.dma_start(
                        out=bass.AP(wkv_sb.t, 256, [[DC * 512, 128], [512, DC], [1, 256]]),
                        in_=wv0.rearrange("(c p) n -> p c n", c=DC)))
                emit_norm(P, SC, xr, xn, "ln", consts, 1e-5, scratch=nsc)
                for i in range(NT):
                    q_tm, k_tm, v_tm, rsc = q_tmP[i % 2], k_tmP[i % 2], v_tmP[i % 2], rscP[i % 2]
                    for nr in range(3):
                        pQ = bks["pE"] if nr % 2 == 0 else bks["pF"]
                        with P.step("pe", reads=[xn, wq_sb, wkv_sb], writes=[pQ]) as st:
                            for ch in range(DC):
                                if nr < 2:
                                    w_ap = wq_sb[:, ch * 1024 + nr * 512: ch * 1024 + nr * 512 + 512]
                                else:
                                    w_ap = wkv_sb[:, ch * 512: ch * 512 + 512]
                                st.add(lambda e_, c=ch, w=w_ap, ii=i, p_=pQ: e_.matmul(
                                    p_[:, :], xn[:, c * T + ii * 128: c * T + ii * 128 + 128], w,
                                    start=(c == 0), stop=(c == DC - 1)))
                        if nr < 2:
                            with P.step("dve", reads=[pQ], writes=[q_tm]) as st:
                                st.add(lambda e_, n=nr, p_=pQ, q_=q_tm: e_.tensor_copy(
                                    q_[:, n * 512:(n + 1) * 512], p_[:, :]))
                        else:
                            with P.step("dve", reads=[pQ], writes=[k_tm, v_tm]) as st:
                                st.add(lambda e_, p_=pQ, k_=k_tm: e_.tensor_copy(k_[:, :], p_[:, 0:256]))
                                st.add(lambda e_, p_=pQ, v_=v_tm: e_.tensor_copy(v_[:, :], p_[:, 256:512]))
                    emit_rope(P, q_tm, tabC, tabS, i * 64, 16, "half", rsc)
                    emit_rope(P, k_tm, tabC, tabS, i * 64, 4, "half", rsc)
                    q_transpose_out(P, q_tm, i)
                    with P.step("sp", reads=[k_tm, v_tm], writes=[cc0i], dma=True) as st:
                        st.add(lambda e_, ii=i, k_=k_tm: e_.dma_start(
                            out=cc0i.t[ii * 128:(ii + 1) * 128, 0:256], in_=k_[:, :]))
                        st.add(lambda e_, ii=i, v_=v_tm: e_.dma_start(
                            out=cc0i.t[ii * 128:(ii + 1) * 128, 256:512], in_=v_[:, :]))
                with P.step("pool", reads=[cc0i], writes=[cc0o], collective_sem=ccsems[0]) as st:
                    st.add(lambda e_: e_.collective_compute(
                        "AllGather", ALU.bypass, replica_groups=PAIRS,
                        ins=[cc0i.t], outs=[cc0o.t]))
        _phase1()
        P.phase_barrier()
        if stop_after == "A":
            with P.step("sp", reads=[xr], dma=True) as st:
                st.add(lambda e_: e_.dma_start(out=xfo.rearrange("c p t -> p c t"),
                                               in_=xr[:, :].rearrange("p (c t) -> p c t", c=DC)))
            P.final_barrier()
            blk = e(nc.Block())
            P.emit_into_block(blk)
            return nc

        def _phase2():
            with ExitStack() as SC:
                es = SC.enter_context
                Kfm = Buf(es(nc.sbuf_tensor("B_K", [128, 2 * S], BF16)), "Kfm")
                Vp = Buf(es(nc.sbuf_tensor("B_V", [128, KC * 260], BF16)), "Vp")
                wo_sb = Buf(es(nc.sbuf_tensor("B_wo", [128, DC * 1024], BF16)), "wo_sb")
                ktst = Buf(es(nc.sbuf_tensor("B_kt", [128, 256], BF16)), "ktst")
                aP4 = [Buf(es(nc.sbuf_tensor(f"B_a{x}", [128, 1024], BF16)), "a") for x in range(3)]
                aA, aB, aC = aP4[0], aP4[1], aP4[2]
                den = Buf(es(nc.sbuf_tensor("B_den", [1, 1024], F32)), "den")
                den2 = Buf(es(nc.sbuf_tensor("B_den2", [1, 1024], F32)), "den2")
                rden = Buf(es(nc.sbuf_tensor("B_rden", [1, 1024], BF16)), "rden")
                rden2 = Buf(es(nc.sbuf_tensor("B_rden2", [1, 1024], BF16)), "rden2")
                outT = Buf(es(nc.sbuf_tensor("B_oT", [128, 4096], BF16)), "outT")
                rb = Buf(es(nc.sbuf_tensor("B_rb", [128, 512], BF16)), "rb")
                mstA = Buf(es(nc.sbuf_tensor("B_mA", [128, 4096], BF16)), "mstA")
                mstB = Buf(es(nc.sbuf_tensor("B_mB", [128, 4096], BF16)), "mstB")

                def mask_load_c(P_, g):
                    m = mstA if g % 2 == 0 else mstB
                    with P_.step("sp", writes=[m], dma=True) as st:
                        st.add(lambda e_, g_=g, m_=m: e_.dma_start(
                            out=m_[:, :].rearrange("p (j q) -> p j q", j=8),
                            in_=mci[g_].rearrange("j p q -> p j q")))
                    return m
                load_w(wo_sb, wo0, DC, 1024)
                for pr in range(2):
                    for i in range(NT):
                        t = 2 * i + pr
                        with P.step("sp", reads=[cc0o], writes=[ktst], dma=True) as st:
                            st.add(lambda e_, p_=pr, ii=i: e_.dma_start(
                                out=ktst[:, :], in_=cc0o.t[p_ * T + ii * 128: p_ * T + ii * 128 + 128, 0:256]))
                        for half in range(2):
                            with P.step("pe", reads=[ktst, ident], writes=[pT]) as st:
                                st.add(lambda e_, h_=half: e_.transpose(
                                    pT[:, 0:128], ktst[:, h_ * 128:(h_ + 1) * 128], ident[:, :]))
                            with P.step("dve", reads=[pT], writes=[Kfm]) as st:
                                st.add(lambda e_, h_=half, t_=t: e_.tensor_copy(
                                    Kfm[:, h_ * S + t_ * 128:h_ * S + t_ * 128 + 128], pT[:, 0:128]))
                    for kv in range(4):
                        with P.step("sp", reads=[cc0o], writes=[Vp], dma=True) as st:
                            st.add(lambda e_, p_=pr, k_=kv: e_.dma_start(
                                out=bass.AP(Vp.t, p_ * 260 + k_ * 65, [[KC * 260, 128], [2 * 260, NT], [1, 64]]),
                                in_=cc0o.t[p_ * T:(p_ + 1) * T, 256 + k_ * 64: 256 + (k_ + 1) * 64]
                                .rearrange("(i p) f -> p i f", p=128)))
                with P.step("dve", writes=[Vp]) as st:
                    st.add(lambda e_: e_.memset(
                        bass.AP(Vp.t, 64, [[KC * 260, 128], [260, KC], [65, 4], [1, 1]]), 1.0))
                cfg = {
                    "heads": list(range(16)),
                    "qwidth": 512, "ngroups": 4, "nmask": 8,
                    "qslice": lambda h, g: qTa[(h % 2) * 64:(h % 2) * 64 + 64,
                                               (h // 2) * T + g * 512:(h // 2) * T + g * 512 + 512],
                    "kslice": lambda h, c: Kfm[((h % 4) % 2) * 64:((h % 4) % 2) * 64 + 64,
                                               ((h % 4) // 2) * S + c * 128:((h % 4) // 2) * S + c * 128 + 128],
                    "vslice": lambda h, c: Vp[:, c * 260 + (h % 4) * 65: c * 260 + (h % 4) * 65 + 65],
                    "mask_load": mask_load_c,
                    "reads_qkv": [qTa, Kfm, Vp],
                    "banks": bks,
                    "scratch": {"aA": aA, "aB": aB, "aC": aC, "den": den, "den2": den2, "rden": rden,
                                "rden2": rden2, "outT": outT, "rb": rb, "cst": cst},
                    "sel_lo": sel_lo, "sel_hi": sel_hi,
                    "wo_sb": wo_sb,
                    "emit_resid": (lambda P_, oc, g, pP: _resid_add(P_, xr, oc, g, pP, w=512)),
                    "scale": 0.125,
                    "nch_of_g": lambda g: 8 * g + 8,
                    "chunk_of": lambda g, j: j,
                }
                emit_attention_causal(P, cfg)
                with P.step("sp", reads=[xr], writes=[xd1], dma=True) as st:
                    st.add(lambda e_: e_.dma_start(out=xd1.t.rearrange("c p t -> p c t"),
                                                   in_=xr[:, :].rearrange("p (c t) -> p c t", c=DC)))
        _phase2()
        P.phase_barrier()
        if stop_after == "B":
            with P.step("sp", reads=[xr], dma=True) as st:
                st.add(lambda e_: e_.dma_start(out=xfo.rearrange("c p t -> p c t"),
                                               in_=xr[:, :].rearrange("p (c t) -> p c t", c=DC)))
            P.final_barrier()
            blk = e(nc.Block())
            P.emit_into_block(blk)
            return nc

        def _phase3():
            with ExitStack() as SC:
                es = SC.enter_context
                xn = Buf(es(nc.sbuf_tensor("C_xn", [128, DC * T], BF16)), "xn")
                wq_sb = Buf(es(nc.sbuf_tensor("C_wq", [128, DC * 1024], BF16)), "wq_sb")
                wdkv_sb = Buf(es(nc.sbuf_tensor("C_wdkv", [128, DC * 512], BF16)), "wdkv_sb")
                g1_sb = Buf(es(nc.sbuf_tensor("C_g1", [128, DC], F32)), "g1_sb")
                q_tmP = [Buf(es(nc.sbuf_tensor(f"C_qtm{x}", [128, 1024], BF16)), "q_tm") for x in range(2)]
                lat_sbP = [Buf(es(nc.sbuf_tensor(f"C_lat{x}", [128, 512], BF16)), "lat_sb") for x in range(2)]
                nsc = alloc_norm_scratch(nc, SC, P, banks=bks, with_xc=True)
                load_w(wq_sb, wqm, DC, 1024)
                load_w(wdkv_sb, wdkv, DC, 512)
                load_gain(g1_sb, g1i)
                emit_norm(P, SC, xr, xn, "ln32", consts, 1e-5, scratch=nsc)
                with P.step("sp", reads=[xr], writes=[hd], dma=True) as st:
                    st.add(lambda e_: e_.dma_start(out=hd.t.rearrange("c p t -> p c t"),
                                                   in_=xr[:, :].rearrange("p (c t) -> p c t", c=DC)))
                emit_norm(P, ctx if False else SC, xr, xn, "rms", consts, 1e-6, gain=g1_sb, scratch=nsc)
                for qc in range(DC):
                    for r in range(4):
                        pQ = bks["pE"] if (qc * 4 + r) % 2 == 0 else bks["pF"]
                        with P.step("pe", reads=[xn, wq_sb], writes=[pQ]) as st:
                            for ch in range(DC):
                                st.add(lambda e_, c=ch, q_=qc, r_=r, p_=pQ: e_.matmul(
                                    p_[:, :], wq_sb[:, c * 1024 + q_ * 128: c * 1024 + q_ * 128 + 128],
                                    xn[:, c * T + r_ * 512: c * T + r_ * 512 + 512],
                                    start=(c == 0), stop=(c == DC - 1)))
                        with P.step("dve", reads=[pQ], writes=[qTa]) as st:
                            st.add(lambda e_, q_=qc, r_=r, p_=pQ: e_.tensor_copy(
                                qTa[:, q_ * T + r_ * 512: q_ * T + r_ * 512 + 512], p_[:, :]))
                for lc in range(4):
                    for r in range(4):
                        pr_ = (lc * 4 + r) % 2
                        pQ = bks["pE"] if pr_ == 0 else bks["pF"]
                        lat_sb = lat_sbP[pr_]
                        with P.step("pe", reads=[xn, wdkv_sb], writes=[pQ]) as st:
                            for ch in range(DC):
                                st.add(lambda e_, c=ch, l=lc, r_=r, p_=pQ: e_.matmul(
                                    p_[:, :], wdkv_sb[:, c * 512 + l * 128: c * 512 + l * 128 + 128],
                                    xn[:, c * T + r_ * 512: c * T + r_ * 512 + 512],
                                    start=(c == 0), stop=(c == DC - 1)))
                        with P.step("dve", reads=[pQ], writes=[lat_sb]) as st:
                            st.add(lambda e_, p_=pQ, l_=lat_sb: e_.tensor_copy(l_[:, :], p_[:, :]))
                        with P.step("sp", reads=[lat_sb], writes=[ccMi], dma=True) as st:
                            st.add(lambda e_, l=lc, r_=r, l_=lat_sb: e_.dma_start(
                                out=ccMi.t[l * 128:(l + 1) * 128, r_ * 512:(r_ + 1) * 512],
                                in_=l_[:, :]))
                with P.step("pool", reads=[ccMi], writes=[ccMo], collective_sem=ccsems[1]) as st:
                    st.add(lambda e_: e_.collective_compute(
                        "AllGather", ALU.bypass, replica_groups=PAIRS,
                        ins=[ccMi.t], outs=[ccMo.t]))
        _phase3()
        P.phase_barrier()
        if stop_after == "C":
            with P.step("sp", reads=[xr], dma=True) as st:
                st.add(lambda e_: e_.dma_start(out=xfo.rearrange("c p t -> p c t"),
                                               in_=xr[:, :].rearrange("p (c t) -> p c t", c=DC)))
            P.final_barrier()
            blk = e(nc.Block())
            P.emit_into_block(blk)
            return nc

        for ps in range(2):
            def _phase4(ps=ps):
                with ExitStack() as SC:
                    es = SC.enter_context
                    Kh = Buf(es(nc.sbuf_tensor(f"D{ps}_Kh", [128, 4 * S], BF16)), "Kh")
                    Vh = Buf(es(nc.sbuf_tensor(f"D{ps}_Vh", [128, KC * 520], BF16)), "Vh")
                    aA = Buf(es(nc.sbuf_tensor(f"D{ps}_aA", [128, 1024], BF16)), "aA")
                    aB = Buf(es(nc.sbuf_tensor(f"D{ps}_aB", [128, 1024], BF16)), "aB")
                    den = Buf(es(nc.sbuf_tensor(f"D{ps}_den", [1, 1024], F32)), "den")
                    rden = Buf(es(nc.sbuf_tensor(f"D{ps}_rden", [1, 1024], BF16)), "rden")
                    outT = Buf(es(nc.sbuf_tensor(f"D{ps}_oT", [128, 2048], BF16)), "outT")
                    rb = Buf(es(nc.sbuf_tensor(f"D{ps}_rb", [128, 512], BF16)), "rb")

                    mstA = Buf(es(nc.sbuf_tensor(f"D{ps}_mA", [128, 4096], BF16)), "mstA")
                    mstB = mstA

                    def mask_load_c(P_, g):
                        m = mstA if g % 2 == 0 else mstB
                        with P_.step("sp", writes=[m], dma=True) as st:
                            st.add(lambda e_, g_=g, m_=m: e_.dma_start(
                                out=m_[:, :].rearrange("p (j q) -> p j q", j=8),
                                in_=mci[g_].rearrange("j p q -> p j q")))
                        return m
                    SCW = es(ExitStack())
                    wuk_sb = Buf(SCW.enter_context(nc.sbuf_tensor(f"D{ps}_wuk", [128, 4 * 1024], BF16)), "wuk_sb")
                    wuv_sb = Buf(SCW.enter_context(nc.sbuf_tensor(f"D{ps}_wuv", [128, 4 * 1024], BF16)), "wuv_sb")
                    latstK = Buf(SCW.enter_context(nc.sbuf_tensor(f"D{ps}_lstK", [128, 2048], BF16)), "latst")
                    latstP = [latstK,
                              Buf(SCW.enter_context(nc.sbuf_tensor(f"D{ps}_lstV", [128, 512], BF16)), "latstV")]
                    load_w(wuk_sb, wuk, 4, 1024)
                    load_w(wuv_sb, wuv, 4, 1024)
                    pQ = bks["pE"]
                    for r in range(8):
                        latst = latstK
                        with P.step("sp", reads=[ccMo], writes=[latst], dma=True) as st:
                            for lc in range(4):
                                for j in range(4):
                                    t = r * 4 + j
                                    st.add(lambda e_, l=lc, j_=j, t_=t, L_=latst: e_.dma_start(
                                        out=L_[:, l * 512 + j_ * 128: l * 512 + j_ * 128 + 128],
                                        in_=ccMo.t[(t_ % 2) * 512 + l * 128: (t_ % 2) * 512 + l * 128 + 128,
                                                   (t_ // 2) * 128:(t_ // 2) * 128 + 128]))
                        for kc in range(4):
                            pQ2 = bks["pE"] if kc % 2 == 0 else bks["pF"]
                            with P.step("pe", reads=[latst, wuk_sb], writes=[pQ2]) as st:
                                for lc in range(4):
                                    st.add(lambda e_, l=lc, k_=kc, p_=ps, L_=latst, q_=pQ2: e_.matmul(
                                        q_[:, :],
                                        wuk_sb[:, l * 1024 + (p_ * 4 + k_) * 128: l * 1024 + (p_ * 4 + k_) * 128 + 128],
                                        L_[:, l * 512:(l + 1) * 512],
                                        start=(l == 0), stop=(l == 3)))
                            with P.step("dve", reads=[pQ2], writes=[Kh]) as st:
                                st.add(lambda e_, k_=kc, r_=r, q_=pQ2: e_.tensor_copy(
                                    Kh[:, k_ * S + r_ * 512: k_ * S + r_ * 512 + 512], q_[:, :]))
                    for c in range(KC):
                        latst = latstP[c % 2]
                        with P.step("sp", reads=[ccMo], writes=[latst], dma=True) as st:
                            for lc in range(4):
                                st.add(lambda e_, l=lc, c_=c, L_=latst: e_.dma_start(
                                    out=L_[:, l * 128:(l + 1) * 128],
                                    in_=ccMo.t[(c_ % 2) * 512 + l * 128: (c_ % 2) * 512 + l * 128 + 128,
                                               (c_ // 2) * 128:(c_ // 2) * 128 + 128]))
                        pQ2 = bks["pE"] if c % 2 == 0 else bks["pF"]
                        with P.step("pe", reads=[latst, wuv_sb], writes=[pQ2]) as st:
                            for lc in range(4):
                                st.add(lambda e_, l=lc, p_=ps, L_=latst, q_=pQ2: e_.matmul(
                                    q_[:, :], L_[:, l * 128:(l + 1) * 128],
                                    wuv_sb[:, l * 1024 + p_ * 512: l * 1024 + p_ * 512 + 512],
                                    start=(l == 0), stop=(l == 3)))
                        with P.step("dve", reads=[pQ2], writes=[Vh]) as st:
                            st.add(lambda e_, c_=c, q_=pQ2: e_.tensor_copy(
                                bass.AP(Vh.t, c_ * 520, [[KC * 520, 128], [65, 8], [1, 64]]),
                                q_[:, :].rearrange("p (h f) -> p h f", h=8)))
                    with P.step("dve", writes=[Vh]) as st:
                        st.add(lambda e_: e_.memset(
                            bass.AP(Vh.t, 64, [[KC * 520, 128], [520, KC], [65, 8], [1, 1]]), 1.0))
                    SCW.close()
                    P.phase_barrier()
                    wom_sb = Buf(es(nc.sbuf_tensor(f"D{ps}_wom", [128, DC * 1024], BF16)), "wom_sb")
                    hsA = Buf(es(nc.sbuf_tensor(f"D{ps}_hsA", [128, 512], F32)), "hsA")
                    hsB = hsA
                    load_w(wom_sb, wom, DC, 1024)

                    def emit_resid_stream(P_, oc, g, pP, ps_=ps, hsA_=hsA, hsB_=hsB):
                        slab = hsA_ if (oc + g) % 2 == 0 else hsB_
                        src = hd if ps_ == 0 else hd1
                        with P_.step("sp", reads=[src], writes=[slab], dma=True) as st:
                            st.add(lambda e_, o_=oc, g_=g, s_=src, sl_=slab: e_.dma_start(
                                out=sl_[:, :], in_=s_.t[o_, :, g_ * 512:(g_ + 1) * 512]))
                        with P_.step("dve", reads=[pP, slab], writes=[slab]) as st:
                            st.add(lambda e_, pP_=pP, sl_=slab: e_.tensor_add(
                                sl_[:, :], pP_[:, 0:512], sl_[:, :]))
                        with P_.step("sp", reads=[slab], writes=[hd1], dma=True) as st:
                            st.add(lambda e_, o_=oc, g_=g, sl_=slab: e_.dma_start(
                                out=hd1.t[o_, :, g_ * 512:(g_ + 1) * 512], in_=sl_[:, :]))

                    cfg = {
                        "heads": list(range(ps * 8, ps * 8 + 8)),
                        "qwidth": 512, "ngroups": 4, "nmask": 8,
                        "qslice": lambda h, g, p_=ps: qTa[(h % 2) * 64:(h % 2) * 64 + 64,
                                                          (h // 2) * T + g * 512:(h // 2) * T + g * 512 + 512],
                        "kslice": lambda h, c, p_=ps: Kh[(h % 2) * 64:(h % 2) * 64 + 64,
                                                         ((h - 8 * p_) // 2) * S + c * 128:((h - 8 * p_) // 2) * S + c * 128 + 128],
                        "vslice": lambda h, c, p_=ps: Vh[:, c * 520 + (h - 8 * p_) * 65: c * 520 + (h - 8 * p_) * 65 + 65],
                        "mask_load": mask_load_c,
                        "reads_qkv": [qTa, Kh, Vh],
                        "banks": bks,
                        "scratch": {"aA": aA, "aB": aB, "den": den, "rden": rden,
                                    "outT": outT, "rb": rb, "cst": cst},
                        "sel_lo": sel_lo, "sel_hi": sel_hi,
                        "wo_sb": wom_sb,
                        "emit_resid": emit_resid_stream,
                        "scale": 0.125,
                        "nch_of_g": lambda g: 8 * g + 8,
                        "chunk_of": lambda g, j: j,
                        "proj_chunks": [(lc, ps * 4 + lc) for lc in range(4)],
                    }
                    emit_attention_causal(P, cfg)
            _phase4()
            P.phase_barrier()
        if stop_after == "D":
            with P.step("sp", reads=[xr], dma=True) as st:
                st.add(lambda e_: e_.dma_start(out=xfo.rearrange("c p t -> p c t"),
                                               in_=xr[:, :].rearrange("p (c t) -> p c t", c=DC)))
            P.final_barrier()
            blk = e(nc.Block())
            P.emit_into_block(blk)
            return nc

        def _phase5():
            with ExitStack() as SC:
                es = SC.enter_context
                xn = Buf(es(nc.sbuf_tensor("E_xn", [128, DC * T], BF16)), "xn")
                w1A = Buf(es(nc.sbuf_tensor("E_w1A", [128, 8 * 1024], BF16)), "w1A")
                w1B = w1A
                w2A = Buf(es(nc.sbuf_tensor("E_w2A", [128, 8 * 1024], BF16)), "w2A")
                h1T = Buf(es(nc.sbuf_tensor("E_h1T", [128, 8 * 512], BF16)), "h1T")
                gw_sb = Buf(es(nc.sbuf_tensor("E_gw", [128, DC * 8], BF16)), "gw_sb")
                g2_sb = Buf(es(nc.sbuf_tensor("E_g2", [128, DC], F32)), "g2_sb")
                lg = Buf(es(nc.sbuf_tensor("E_lg", [128, 8], F32)), "lg")
                m1 = Buf(es(nc.sbuf_tensor("E_m1", [128, 1], F32)), "m1")
                m2 = Buf(es(nc.sbuf_tensor("E_m2", [128, 1], F32)), "m2")
                eqm = Buf(es(nc.sbuf_tensor("E_eqm", [128, 8], F32)), "eqm")
                mskd = Buf(es(nc.sbuf_tensor("E_mskd", [128, 8], F32)), "mskd")
                E8 = Buf(es(nc.sbuf_tensor("E_E8", [128, 8], F32)), "E8")
                mm12 = Buf(es(nc.sbuf_tensor("E_mm12", [128, 2], F32)), "mm12")
                rsum = Buf(es(nc.sbuf_tensor("E_rsum", [128, 1], F32)), "rsum")
                comb = Buf(es(nc.sbuf_tensor("E_comb", [128, NT * 8], BF16)), "comb")
                combT = Buf(es(nc.sbuf_tensor("E_combT", [128, T], BF16)), "combT")
                ybf = Buf(es(nc.sbuf_tensor("E_ybf", [128, 512], BF16)), "ybf")
                ybf2 = Buf(es(nc.sbuf_tensor("E_ybf2", [128, 512], BF16)), "ybf2")
                slab = Buf(es(nc.sbuf_tensor("E_slab", [128, 512], F32)), "slab")
                nsc = alloc_norm_scratch(nc, SC, P, banks=bks, with_xc=False)
                with P.step("sp", writes=[xr], dma=True) as st:
                    st.add(lambda e_: e_.dma_start(out=xr[:, :].rearrange("p (c t) -> p c t", c=DC),
                                                   in_=hd1.t.rearrange("c p t -> p c t")))
                xr.writer.val = xr.writer.val
                load_gain(g2_sb, g2i)
                load_w(gw_sb, gwi, DC, 8)
                emit_norm(P, SC, xr, xn, "rms", consts, 1e-6, gain=g2_sb, scratch=nsc)
                pG = bks["pF"]
                for i in range(NT):
                    with P.step("pe", reads=[xn, gw_sb], writes=[pG]) as st:
                        for ch in range(DC):
                            st.add(lambda e_, c=ch, ii=i: e_.matmul(
                                pG[:, 0:8], xn[:, c * T + ii * 128: c * T + ii * 128 + 128],
                                gw_sb[:, c * 8:(c + 1) * 8],
                                start=(c == 0), stop=(c == DC - 1)))
                    with P.step("dve", reads=[pG], writes=[lg]) as st:
                        st.add(lambda e_: e_.tensor_copy(lg[:, :], pG[:, 0:8]))
                    with P.step("dve", reads=[lg], writes=[m1]) as st:
                        st.add(lambda e_: e_.reduce_max(m1[:, :], lg[:, :], axis=AX.X))
                    with P.step("dve", reads=[lg, m1], writes=[eqm]) as st:
                        st.add(lambda e_: e_.tensor_scalar(eqm[:, :], lg[:, :], m1[:, :], None, ALU.is_ge))
                    with P.step("dve", reads=[lg, eqm], writes=[mskd]) as st:
                        st.add(lambda e_: e_.scalar_tensor_tensor(mskd[:, :], eqm[:, :], -1e30, lg[:, :],
                                                                  ALU.mult, ALU.add))
                    with P.step("dve", reads=[mskd], writes=[m2]) as st:
                        st.add(lambda e_: e_.reduce_max(m2[:, :], mskd[:, :], axis=AX.X))
                    with P.step("act", reads=[lg], writes=[E8]) as st:
                        st.add(lambda e_: e_.activation(E8[:, :], lg[:, :], AF.Exp))
                    with P.step("dve", reads=[m1], writes=[mm12]) as st:
                        st.add(lambda e_: e_.tensor_copy(mm12[:, 0:1], m1[:, :]))
                    with P.step("dve", reads=[m2], writes=[mm12]) as st:
                        st.add(lambda e_: e_.tensor_copy(mm12[:, 1:2], m2[:, :]))
                    with P.step("act", reads=[mm12], writes=[mm12]) as st:
                        st.add(lambda e_: e_.activation(mm12[:, :], mm12[:, :], AF.Exp))
                    with P.step("dve", reads=[mm12], writes=[rsum]) as st:
                        st.add(lambda e_: e_.reduce_sum(rsum[:, :], mm12[:, :], axis=AX.X))
                    with P.step("dve", reads=[rsum], writes=[rsum]) as st:
                        st.add(lambda e_: e_.reciprocal(rsum[:, :], rsum[:, :]))
                    with P.step("dve", reads=[lg, m2], writes=[eqm]) as st:
                        st.add(lambda e_: e_.tensor_scalar(eqm[:, :], lg[:, :], m2[:, :], None, ALU.is_ge))
                    with P.step("dve", reads=[E8, eqm], writes=[E8]) as st:
                        st.add(lambda e_: e_.tensor_mul(E8[:, :], E8[:, :], eqm[:, :]))
                    with P.step("dve", reads=[E8, rsum, comb], writes=[comb]) as st:
                        st.add(lambda e_, ii=i: e_.tensor_scalar(comb[:, ii * 8:(ii + 1) * 8],
                                                                 E8[:, :], rsum[:, :], None, ALU.mult))
                w1bufs = [w1A, w1B]

                def emit_w1load(idx):
                    ee, half = divmod(idx, 2)
                    wb = w1bufs[idx % 2]
                    with P.step("pool", writes=[wb], dma=True) as st:
                        st.add(lambda e_, E_=ee, H_=half, w_=wb: e_.dma_start(
                            out=w_[:, :].rearrange("p (c n) -> p c n", c=DC),
                            in_=w1d[E_, :, H_ * 1024:(H_ + 1) * 1024].rearrange("(c p) n -> p c n", c=DC)))

                for idx in range(16):
                    ee, half = divmod(idx, 2)
                    w1b = w1bufs[idx % 2]
                    emit_w1load(idx)
                    with P.step("pool", writes=[w2A], dma=True) as st:
                        st.add(lambda e_, E_=ee, H_=half: e_.dma_start(
                            out=w2A[:, :].rearrange("p (c n) -> p c n", c=DC),
                            in_=w2d[E_, H_ * 1024:(H_ + 1) * 1024, :].rearrange("(c p) n -> p c n", c=DC)))
                    if half == 0:
                        for i in range(NT):
                            with P.step("pe", reads=[comb, ident], writes=[pG]) as st:
                                st.add(lambda e_, ii=i, E_=ee: e_.matmul(
                                    pG[:, 0:128], comb[:, ii * 8 + E_: ii * 8 + E_ + 1].broadcast_to([128, 128]),
                                    ident[:, :], start=True, stop=True))
                            with P.step("dve", reads=[pG], writes=[combT]) as st:
                                st.add(lambda e_, ii=i: e_.tensor_copy(
                                    combT[:, ii * 128:(ii + 1) * 128], pG[:, 0:128]))
                    for tr in range(4):
                        for ec in range(8):
                            pQ = bks["pE"] if ec % 2 == 0 else bks["pF"]
                            with P.step("pe", reads=[xn, w1b], writes=[pQ]) as st:
                                for dc in range(DC):
                                    st.add(lambda e_, d=dc, E2=ec, t_=tr, wb=w1b, p_=pQ: e_.matmul(
                                        p_[:, :], wb[:, d * 1024 + E2 * 128: d * 1024 + E2 * 128 + 128],
                                        xn[:, d * T + t_ * 512: d * T + t_ * 512 + 512],
                                        start=(d == 0), stop=(d == DC - 1)))
                            with P.step("act", reads=[pQ], writes=[h1T]) as st:
                                st.add(lambda e_, E2=ec, p_=pQ: e_.activation(
                                    h1T[:, E2 * 512:(E2 + 1) * 512], p_[:, :], AF.Silu))
                        for oc in range(DC):
                            pY = bks["pC"] if oc % 2 == 0 else bks["pD"]
                            yb = ybf if oc % 2 == 0 else ybf2
                            with P.step("pe", reads=[h1T, w2A], writes=[pY]) as st:
                                for ec in range(8):
                                    st.add(lambda e_, E2=ec, o_=oc, p_=pY: e_.matmul(
                                        p_[:, :], w2A[:, E2 * 1024 + o_ * 128: E2 * 1024 + o_ * 128 + 128],
                                        h1T[:, E2 * 512:(E2 + 1) * 512],
                                        start=(E2 == 0), stop=(E2 == 7)))
                            with P.step("dve", reads=[pY], writes=[yb]) as st:
                                st.add(lambda e_, p_=pY, y_=yb: e_.tensor_copy(y_[:, :], p_[:, :]))
                            with P.step("pool", reads=[yb, combT], writes=[yb]) as st:
                                st.add(lambda e_, t_=tr, y_=yb: e_.tensor_mul(
                                    y_[:, :], y_[:, :], combT[:, t_ * 512:(t_ + 1) * 512]))
                            with P.step("pool", reads=[yb, xr], writes=[xr]) as st:
                                st.add(lambda e_, o_=oc, t_=tr, y_=yb: e_.tensor_add(
                                    xr[:, o_ * T + t_ * 512: o_ * T + t_ * 512 + 512], y_[:, :],
                                    xr[:, o_ * T + t_ * 512: o_ * T + t_ * 512 + 512]))
                for oc in range(DC):
                    for tr in range(4):
                        with P.step("sp", reads=[xd1], writes=[slab], dma=True) as st:
                            st.add(lambda e_, o_=oc, t_=tr: e_.dma_start(
                                out=slab[:, :], in_=xd1.t[o_, :, t_ * 512:(t_ + 1) * 512]))
                        with P.step("dve", reads=[xr, slab], writes=[xr]) as st:
                            st.add(lambda e_, o_=oc, t_=tr: e_.tensor_add(
                                xr[:, o_ * T + t_ * 512: o_ * T + t_ * 512 + 512],
                                xr[:, o_ * T + t_ * 512: o_ * T + t_ * 512 + 512], slab[:, :]))
                        with P.step("sp", reads=[xr], writes=[xd2], dma=True) as st:
                            st.add(lambda e_, o_=oc, t_=tr: e_.dma_start(
                                out=xd2.t[o_, :, t_ * 512:(t_ + 1) * 512],
                                in_=xr[:, o_ * T + t_ * 512: o_ * T + t_ * 512 + 512]))
        _phase5()
        P.phase_barrier()
        if stop_after == "E":
            with P.step("sp", reads=[xr], dma=True) as st:
                st.add(lambda e_: e_.dma_start(out=xfo.rearrange("c p t -> p c t"),
                                               in_=xr[:, :].rearrange("p (c t) -> p c t", c=DC)))
            P.final_barrier()
            blk = e(nc.Block())
            P.emit_into_block(blk)
            return nc

        def _phase6():
            with ExitStack() as SC:
                es = SC.enter_context
                xn = Buf(es(nc.sbuf_tensor("F_xn", [128, DC * T], BF16)), "xn")
                wq_sb = Buf(es(nc.sbuf_tensor("F_wq", [128, DC * 1024], BF16)), "wq_sb")
                wkv_sb = Buf(es(nc.sbuf_tensor("F_wkv", [128, DC * 256], BF16)), "wkv_sb")
                nw_sb = Buf(es(nc.sbuf_tensor("F_nw", [128, DC], F32)), "nw_sb")
                q_tmP = [Buf(es(nc.sbuf_tensor(f"F_qtm{x}", [128, 1024], BF16)), "q_tm") for x in range(2)]
                lat_sbP = [Buf(es(nc.sbuf_tensor(f"F_lat{x}", [128, 512], BF16)), "lat_sb") for x in range(2)]
                rscP = [{"r1": Buf(es(nc.sbuf_tensor(f"F_r1{x}", [128, 1024], BF16)), "r1"),
                         "r2": Buf(es(nc.sbuf_tensor(f"F_r2{x}", [128, 1024], BF16)), "r2"),
                         "ss": Buf(es(nc.sbuf_tensor(f"F_ss{x}", [128, 16], F32)), "ss"),
                         "cst": cst} for x in range(2)]
                tabC = Buf(es(nc.sbuf_tensor("F_tC", [128, 1024], BF16)), "tabC")
                tabS = Buf(es(nc.sbuf_tensor("F_tS", [128, 1024], BF16)), "tabS")
                nsc = alloc_norm_scratch(nc, SC, P, banks=bks, with_xc=True)
                load_tabs(tabC, tabS, Crgi, Srgi)
                load_w(wq_sb, gwq, DC, 1024)
                load_w(wkv_sb, gkvd, DC, 256)
                load_gain(nw_sb, n1wi)
                emit_norm(P, SC, xr, xn, "ln32", consts, 1e-5, scratch=nsc)
                emit_norm(P, SC, xr, xn, "rms", consts, 1e-6, gain=nw_sb, scratch=nsc)
                for i in range(NT):
                    q_tm, rsc = q_tmP[i % 2], rscP[i % 2]
                    for nr in range(2):
                        pQ = bks["pE"] if nr % 2 == 0 else bks["pF"]
                        with P.step("pe", reads=[xn, wq_sb], writes=[pQ]) as st:
                            for ch in range(DC):
                                st.add(lambda e_, c=ch, n=nr, ii=i, p_=pQ: e_.matmul(
                                    p_[:, :], xn[:, c * T + ii * 128: c * T + ii * 128 + 128],
                                    wq_sb[:, c * 1024 + n * 512: c * 1024 + n * 512 + 512],
                                    start=(c == 0), stop=(c == DC - 1)))
                        emit_qperm_evac(P, pQ, q_tm, nr)
                    emit_rope(P, q_tm, tabC, tabS, i * 64, 16, "inter", rsc)
                    emit_qknorm(P, q_tm, 16, rsc)
                    q_transpose_out(P, q_tm, i)
                for lc in range(2):
                    for r in range(4):
                        pr_ = (lc * 4 + r) % 2
                        pQ = bks["pE"] if pr_ == 0 else bks["pF"]
                        lat_sb = lat_sbP[pr_]
                        with P.step("pe", reads=[xn, wkv_sb], writes=[pQ]) as st:
                            for ch in range(DC):
                                st.add(lambda e_, c=ch, l=lc, r_=r, p_=pQ: e_.matmul(
                                    p_[:, :], wkv_sb[:, c * 256 + l * 128: c * 256 + l * 128 + 128],
                                    xn[:, c * T + r_ * 512: c * T + r_ * 512 + 512],
                                    start=(c == 0), stop=(c == DC - 1)))
                        with P.step("dve", reads=[pQ], writes=[lat_sb]) as st:
                            st.add(lambda e_, p_=pQ, l_=lat_sb: e_.tensor_copy(l_[:, :], p_[:, :]))
                        with P.step("sp", reads=[lat_sb], writes=[ccGi], dma=True) as st:
                            st.add(lambda e_, l=lc, r_=r, l_=lat_sb: e_.dma_start(
                                out=ccGi.t[l * 128:(l + 1) * 128, r_ * 512:(r_ + 1) * 512],
                                in_=l_[:, :]))
                with P.step("pool", reads=[ccGi], writes=[ccGo], collective_sem=ccsems[2]) as st:
                    st.add(lambda e_: e_.collective_compute(
                        "AllGather", ALU.bypass, replica_groups=PAIRS,
                        ins=[ccGi.t], outs=[ccGo.t]))
        _phase6()
        P.phase_barrier()
        if stop_after == "F":
            with P.step("sp", reads=[xr], dma=True) as st:
                st.add(lambda e_: e_.dma_start(out=xfo.rearrange("c p t -> p c t"),
                                               in_=xr[:, :].rearrange("p (c t) -> p c t", c=DC)))
            P.final_barrier()
            blk = e(nc.Block())
            P.emit_into_block(blk)
            return nc

        def _phase7():
            with ExitStack() as SC:
                es = SC.enter_context
                Kfm = Buf(es(nc.sbuf_tensor("G_K", [128, 2 * S], BF16)), "Kfm")
                Vp = Buf(es(nc.sbuf_tensor("G_V", [128, KC * 260], BF16)), "Vp")
                latgf = Buf(es(nc.sbuf_tensor("G_lat", [128, 2 * S], BF16)), "latgf")
                kup_sb = Buf(es(nc.sbuf_tensor("G_kup", [128, 2 * 256], BF16)), "kup_sb")
                vup_sb = Buf(es(nc.sbuf_tensor("G_vup", [128, 2 * 256], BF16)), "vup_sb")
                wo_sb = Buf(es(nc.sbuf_tensor("G_wo", [128, DC * 1024], BF16)), "wo_sb")
                k_tmP = [Buf(es(nc.sbuf_tensor(f"G_ktm{x}", [128, 256], BF16)), "k_tm") for x in range(2)]
                rscP = [{"r1": Buf(es(nc.sbuf_tensor(f"G_r1{x}", [128, 256], BF16)), "r1"),
                         "r2": Buf(es(nc.sbuf_tensor(f"G_r2{x}", [128, 256], BF16)), "r2"),
                         "ss": Buf(es(nc.sbuf_tensor(f"G_ss{x}", [128, 16], F32)), "ss"),
                         "cst": cst} for x in range(2)]
                aP4 = [Buf(es(nc.sbuf_tensor(f"G_a{x}", [128, 1024], BF16)), "a") for x in range(2)]
                aP4 = aP4 + [aP4[0]]
                aA, aB, aC = aP4[0], aP4[1], aP4[2]
                den = Buf(es(nc.sbuf_tensor("G_den", [1, 1024], F32)), "den")
                den2 = Buf(es(nc.sbuf_tensor("G_den2", [1, 1024], F32)), "den2")
                rden = Buf(es(nc.sbuf_tensor("G_rden", [1, 1024], BF16)), "rden")
                rden2 = Buf(es(nc.sbuf_tensor("G_rden2", [1, 1024], BF16)), "rden2")
                outT = Buf(es(nc.sbuf_tensor("G_oT", [128, 4096], BF16)), "outT")
                rb = Buf(es(nc.sbuf_tensor("G_rb", [128, 512], BF16)), "rb")
                mstA = Buf(es(nc.sbuf_tensor("G_mA", [128, 4096], BF16)), "mstA")
                mstB = mstA

                def mask_load_c(P_, g):
                    m = mstA if g % 2 == 0 else mstB
                    with P_.step("sp", writes=[m], dma=True) as st:
                        st.add(lambda e_, g_=g, m_=m: e_.dma_start(
                            out=m_[:, :].rearrange("p (j q) -> p j q", j=8),
                            in_=mci[g_].rearrange("j p q -> p j q")))
                    return m
                tabC = Buf(es(nc.sbuf_tensor("G_tC", [128, 2048], BF16)), "tabC")
                tabS = Buf(es(nc.sbuf_tensor("G_tS", [128, 2048], BF16)), "tabS")
                load_tabs(tabC, tabS, Crgfi, Srgfi, n=KC)
                with P.step("sp", reads=[ccGo], writes=[latgf], dma=True) as st:
                    for pr in range(2):
                        for lc in range(2):
                            st.add(lambda e_, p_=pr, l=lc: e_.dma_start(
                                out=bass.AP(latgf.t, l * S + p_ * 128, [[2 * S, 128], [256, NT], [1, 128]]),
                                in_=ccGo.t[p_ * 256 + l * 128: p_ * 256 + l * 128 + 128, :]
                                .rearrange("p (i q) -> p i q", i=NT)))
                load_w(kup_sb, gkup, 2, 256)
                load_w(vup_sb, gvup, 2, 256)
                emit_wo_perm_load(P, None, wo_sb, gwo)
                for c in range(KC):
                    k_tm, rsc = k_tmP[c % 2], rscP[c % 2]
                    pK = bks["pE"] if c % 2 == 0 else bks["pF"]
                    pV = bks["pC"] if c % 2 == 0 else bks["pD"]
                    pX = pT if c % 2 == 0 else pT2
                    with P.step("pe", reads=[latgf, kup_sb], writes=[pK]) as st:
                        for lc in range(2):
                            st.add(lambda e_, l=lc, c_=c, p_=pK: e_.matmul(
                                p_[:, 0:256], latgf[:, l * S + c_ * 128: l * S + c_ * 128 + 128],
                                kup_sb[:, l * 256:(l + 1) * 256],
                                start=(l == 0), stop=(l == 1)))
                    with P.step("dve", reads=[pK], writes=[k_tm]) as st:
                        st.add(lambda e_, p_=pK, k_=k_tm: e_.tensor_copy(k_[:, :], p_[:, 0:256]))
                    emit_rope(P, k_tm, tabC, tabS, c * 64, 4, "inter", rsc)
                    emit_qknorm(P, k_tm, 4, rsc)
                    for half in range(2):
                        with P.step("pe", reads=[k_tm, ident], writes=[pX]) as st:
                            st.add(lambda e_, h_=half, k_=k_tm, p_=pX: e_.transpose(
                                p_[:, 0:128], k_[:, h_ * 128:(h_ + 1) * 128], ident[:, :]))
                        with P.step("dve", reads=[pX], writes=[Kfm]) as st:
                            st.add(lambda e_, h_=half, c_=c, p_=pX: e_.tensor_copy(
                                Kfm[:, h_ * S + c_ * 128:h_ * S + c_ * 128 + 128], p_[:, 0:128]))
                    with P.step("pe", reads=[latgf, vup_sb], writes=[pV]) as st:
                        for lc in range(2):
                            st.add(lambda e_, l=lc, c_=c, p_=pV: e_.matmul(
                                p_[:, 0:256], latgf[:, l * S + c_ * 128: l * S + c_ * 128 + 128],
                                vup_sb[:, l * 256:(l + 1) * 256],
                                start=(l == 0), stop=(l == 1)))
                    with P.step("dve", reads=[pV], writes=[Vp]) as st:
                        st.add(lambda e_, c_=c, p_=pV: e_.tensor_copy(
                            bass.AP(Vp.t, c_ * 260, [[KC * 260, 128], [65, 4], [1, 64]]),
                            p_[:, 0:256].rearrange("p (k f) -> p k f", k=4)))
                with P.step("dve", writes=[Vp]) as st:
                    st.add(lambda e_: e_.memset(
                        bass.AP(Vp.t, 64, [[KC * 260, 128], [260, KC], [65, 4], [1, 1]]), 1.0))
                cfg = {
                    "heads": list(L1_HEADS),
                    "qwidth": 512, "ngroups": 4, "nmask": 8,
                    "qslice": lambda h, g: qTa[(gqa_pos(h) % 2) * 64:(gqa_pos(h) % 2) * 64 + 64,
                                               (gqa_pos(h) // 2) * T + g * 512:(gqa_pos(h) // 2) * T + g * 512 + 512],
                    "kslice": lambda h, c: Kfm[((h // 4) % 2) * 64:((h // 4) % 2) * 64 + 64,
                                               ((h // 4) // 2) * S + c * 128:((h // 4) // 2) * S + c * 128 + 128],
                    "vslice": lambda h, c: Vp[:, c * 260 + (h // 4) * 65: c * 260 + (h // 4) * 65 + 65],
                    "mask_load": mask_load_c,
                    "reads_qkv": [qTa, Kfm, Vp],
                    "banks": bks,
                    "scratch": {"aA": aA, "aB": aB, "den": den, "den2": den2, "rden": rden,
                                "rden2": rden2, "outT": outT, "rb": rb, "cst": cst},
                    "sel_lo": sel_lo, "sel_hi": sel_hi,
                    "wo_sb": wo_sb,
                    "emit_resid": (lambda P_, oc, g, pP: _resid_add(P_, xr, oc, g, pP, w=512)),
                    "scale": 0.125,
                    "nch_of_g": lambda g: 8 * g + 8,
                    "chunk_of": lambda g, j: j,
                }
                emit_attention_causal(P, cfg)
        _phase7()
        P.phase_barrier()
        if stop_after == "G":
            with P.step("sp", reads=[xr], dma=True) as st:
                st.add(lambda e_: e_.dma_start(out=xfo.rearrange("c p t -> p c t"),
                                               in_=xr[:, :].rearrange("p (c t) -> p c t", c=DC)))
            P.final_barrier()
            blk = e(nc.Block())
            P.emit_into_block(blk)
            return nc

        def _phase8():
            with ExitStack() as SC:
                es = SC.enter_context
                xn = Buf(es(nc.sbuf_tensor("H_xn", [128, DC * T], BF16)), "xn")
                wq_sb = Buf(es(nc.sbuf_tensor("H_wq", [128, DC * 1024], BF16)), "wq_sb")
                wkv_sb = Buf(es(nc.sbuf_tensor("H_wkv", [128, DC * 512], BF16)), "wkv_sb")
                nw_sb = Buf(es(nc.sbuf_tensor("H_nw", [128, DC], F32)), "nw_sb")
                q_tmP = [Buf(es(nc.sbuf_tensor(f"H_qtm{x}", [128, 1024], BF16)), "q_tm") for x in range(2)]
                k_tmP = [Buf(es(nc.sbuf_tensor(f"H_ktm{x}", [128, 256], BF16)), "k_tm") for x in range(2)]
                v_tmP = [Buf(es(nc.sbuf_tensor(f"H_vtm{x}", [128, 256], BF16)), "v_tm") for x in range(2)]
                rscP = [{"r1": Buf(es(nc.sbuf_tensor(f"H_r1{x}", [128, 1024], BF16)), "r1"),
                         "r2": Buf(es(nc.sbuf_tensor(f"H_r2{x}", [128, 1024], BF16)), "r2"),
                         "ss": Buf(es(nc.sbuf_tensor(f"H_ss{x}", [128, 16], F32)), "ss"),
                         "cst": cst} for x in range(2)]
                tabC = Buf(es(nc.sbuf_tensor("H_tC", [128, 1024], BF16)), "tabC")
                tabS = Buf(es(nc.sbuf_tensor("H_tS", [128, 1024], BF16)), "tabS")
                nsc = alloc_norm_scratch(nc, SC, P, banks=bks, with_xc=False)
                load_tabs(tabC, tabS, Crli, Srli)
                load_w(wq_sb, lwq, DC, 1024)
                with P.step("pool", writes=[wkv_sb], dma=True) as st:
                    st.add(lambda e_: e_.dma_start(
                        out=bass.AP(wkv_sb.t, 0, [[DC * 512, 128], [512, DC], [1, 256]]),
                        in_=lwk.rearrange("(c p) n -> p c n", c=DC)))
                    st.add(lambda e_: e_.dma_start(
                        out=bass.AP(wkv_sb.t, 256, [[DC * 512, 128], [512, DC], [1, 256]]),
                        in_=lwv.rearrange("(c p) n -> p c n", c=DC)))
                load_gain(nw_sb, n2wi)
                emit_norm(P, SC, xr, xn, "rms", consts, 1e-6, gain=nw_sb, scratch=nsc)
                for i in range(NT):
                    q_tm, k_tm, v_tm, rsc = q_tmP[i % 2], k_tmP[i % 2], v_tmP[i % 2], rscP[i % 2]
                    for nr in range(3):
                        pQ = bks["pE"] if nr % 2 == 0 else bks["pF"]
                        with P.step("pe", reads=[xn, wq_sb, wkv_sb], writes=[pQ]) as st:
                            for ch in range(DC):
                                if nr < 2:
                                    w_ap = wq_sb[:, ch * 1024 + nr * 512: ch * 1024 + nr * 512 + 512]
                                else:
                                    w_ap = wkv_sb[:, ch * 512: ch * 512 + 512]
                                st.add(lambda e_, c=ch, w=w_ap, ii=i, p_=pQ: e_.matmul(
                                    p_[:, :], xn[:, c * T + ii * 128: c * T + ii * 128 + 128], w,
                                    start=(c == 0), stop=(c == DC - 1)))
                        if nr < 2:
                            emit_qperm_evac(P, pQ, q_tm, nr)
                        else:
                            with P.step("dve", reads=[pQ], writes=[k_tm, v_tm]) as st:
                                st.add(lambda e_, p_=pQ, k_=k_tm: e_.tensor_copy(k_[:, :], p_[:, 0:256]))
                                st.add(lambda e_, p_=pQ, v_=v_tm: e_.tensor_copy(v_[:, :], p_[:, 256:512]))
                    emit_rope(P, q_tm, tabC, tabS, i * 64, 16, "inter", rsc)
                    emit_qknorm(P, q_tm, 16, rsc)
                    emit_rope(P, k_tm, tabC, tabS, i * 64, 4, "inter", rsc)
                    emit_qknorm(P, k_tm, 4, rsc)
                    q_transpose_out(P, q_tm, i)
                    with P.step("sp", reads=[k_tm, v_tm], writes=[ccLi], dma=True) as st:
                        st.add(lambda e_, ii=i, k_=k_tm: e_.dma_start(
                            out=ccLi.t[ii * 128:(ii + 1) * 128, 0:256], in_=k_[:, :]))
                        st.add(lambda e_, ii=i, v_=v_tm: e_.dma_start(
                            out=ccLi.t[ii * 128:(ii + 1) * 128, 256:512], in_=v_[:, :]))
                with P.step("pool", reads=[ccLi], writes=[ccLo], collective_sem=ccsems[3]) as st:
                    st.add(lambda e_: e_.collective_compute(
                        "AllGather", ALU.bypass, replica_groups=PAIRS,
                        ins=[ccLi.t], outs=[ccLo.t]))
        _phase8()
        P.phase_barrier()
        if stop_after == "H":
            with P.step("sp", reads=[xr], dma=True) as st:
                st.add(lambda e_: e_.dma_start(out=xfo.rearrange("c p t -> p c t"),
                                               in_=xr[:, :].rearrange("p (c t) -> p c t", c=DC)))
            P.final_barrier()
            blk = e(nc.Block())
            P.emit_into_block(blk)
            return nc

        def _phase9():
            with ExitStack() as SC:
                es = SC.enter_context
                Kfm = Buf(es(nc.sbuf_tensor("I_K", [128, 2 * S], BF16)), "Kfm")
                Vp = Buf(es(nc.sbuf_tensor("I_V", [128, KC * 260], BF16)), "Vp")
                wo_sb = Buf(es(nc.sbuf_tensor("I_wo", [128, DC * 1024], BF16)), "wo_sb")
                ktst = Buf(es(nc.sbuf_tensor("I_kt", [128, 256], BF16)), "ktst")
                aP4 = [Buf(es(nc.sbuf_tensor(f"I_a{x}", [128, 512], BF16)), "a") for x in range(3)]
                aA, aB, aC = aP4[0], aP4[1], aP4[2]
                den = Buf(es(nc.sbuf_tensor("I_den", [1, 512], F32)), "den")
                den2 = Buf(es(nc.sbuf_tensor("I_den2", [1, 512], F32)), "den2")
                rden = Buf(es(nc.sbuf_tensor("I_rden", [1, 512], BF16)), "rden")
                rden2 = Buf(es(nc.sbuf_tensor("I_rden2", [1, 512], BF16)), "rden2")
                outT = Buf(es(nc.sbuf_tensor("I_oT", [128, 1024], BF16)), "outT")
                rb = Buf(es(nc.sbuf_tensor("I_rb", [128, 256], BF16)), "rb")
                xnr = Buf(es(nc.sbuf_tensor("I_xnr", [128, DC * 512], BF16)), "xnr")
                mstA = Buf(es(nc.sbuf_tensor("I_mA", [128, 512], BF16)), "mstA")
                mstB = Buf(es(nc.sbuf_tensor("I_mB", [128, 512], BF16)), "mstB")

                def mask_load_l(P_, g):
                    m = mstA if g % 2 == 0 else mstB
                    with P_.step("sp", writes=[m], dma=True) as st:
                        st.add(lambda e_, g_=g, m_=m: e_.dma_start(
                            out=m_[:, 0:512].rearrange("p (j q) -> p j q", j=4),
                            in_=mlii[g_].rearrange("j p q -> p j q")))
                    return m
                slab = Buf(es(nc.sbuf_tensor("I_slab", [128, 512], F32)), "slab")
                nsc = alloc_norm_scratch(nc, SC, P, banks=bks, with_xc=True)
                emit_wo_perm_load(P, None, wo_sb, lwo)
                for pr in range(2):
                    for i in range(NT):
                        t = 2 * i + pr
                        with P.step("sp", reads=[ccLo], writes=[ktst], dma=True) as st:
                            st.add(lambda e_, p_=pr, ii=i: e_.dma_start(
                                out=ktst[:, :], in_=ccLo.t[p_ * T + ii * 128: p_ * T + ii * 128 + 128, 0:256]))
                        for half in range(2):
                            with P.step("pe", reads=[ktst, ident], writes=[pT]) as st:
                                st.add(lambda e_, h_=half: e_.transpose(
                                    pT[:, 0:128], ktst[:, h_ * 128:(h_ + 1) * 128], ident[:, :]))
                            with P.step("dve", reads=[pT], writes=[Kfm]) as st:
                                st.add(lambda e_, h_=half, t_=t: e_.tensor_copy(
                                    Kfm[:, h_ * S + t_ * 128:h_ * S + t_ * 128 + 128], pT[:, 0:128]))
                    for kv in range(4):
                        with P.step("sp", reads=[ccLo], writes=[Vp], dma=True) as st:
                            st.add(lambda e_, p_=pr, k_=kv: e_.dma_start(
                                out=bass.AP(Vp.t, p_ * 260 + k_ * 65, [[KC * 260, 128], [2 * 260, NT], [1, 64]]),
                                in_=ccLo.t[p_ * T:(p_ + 1) * T, 256 + k_ * 64: 256 + (k_ + 1) * 64]
                                .rearrange("(i p) f -> p i f", p=128)))
                with P.step("dve", writes=[Vp]) as st:
                    st.add(lambda e_: e_.memset(
                        bass.AP(Vp.t, 64, [[KC * 260, 128], [260, KC], [65, 4], [1, 1]]), 1.0))
                cfg = {
                    "heads": list(L1_HEADS),
                    "qwidth": 128,
                    "ngroups": NT, "nmask": 4,
                    "qslice": lambda h, g: qTa[(gqa_pos(h) % 2) * 64:(gqa_pos(h) % 2) * 64 + 64,
                                               (gqa_pos(h) // 2) * T + g * 128:(gqa_pos(h) // 2) * T + g * 128 + 128],
                    "kslice": lambda h, c: Kfm[((h // 4) % 2) * 64:((h // 4) % 2) * 64 + 64,
                                               ((h // 4) // 2) * S + c * 128:((h // 4) // 2) * S + c * 128 + 128],
                    "vslice": lambda h, c: Vp[:, c * 260 + (h // 4) * 65: c * 260 + (h // 4) * 65 + 65],
                    "mask_load": mask_load_l,
                    "reads_qkv": [qTa, Kfm, Vp],
                    "banks": bks,
                    "scratch": {"aA": aA, "aB": aB, "aC": aC, "den": den, "den2": den2, "rden": rden,
                                "rden2": rden2, "outT": outT, "rb": rb, "cst": cst},
                    "sel_lo": sel_lo, "sel_hi": sel_hi,
                    "wo_sb": wo_sb,
                    "emit_resid": (lambda P_, oc, g, pP: _resid_add(P_, xr, oc, g, pP, w=128)),
                    "scale": 0.125,
                    "nch_of_g": lambda g: 4,
                    "chunk_of": lambda g, j: max(0, 2 * g - 2 + j),
                }
                emit_attention_causal(P, cfg)
                for oc in range(DC):
                    for tr in range(4):
                        with P.step("sp", reads=[xd2], writes=[slab], dma=True) as st:
                            st.add(lambda e_, o_=oc, t_=tr: e_.dma_start(
                                out=slab[:, :], in_=xd2.t[o_, :, t_ * 512:(t_ + 1) * 512]))
                        with P.step("dve", reads=[xr, slab], writes=[xr]) as st:
                            st.add(lambda e_, o_=oc, t_=tr: e_.tensor_add(
                                xr[:, o_ * T + t_ * 512: o_ * T + t_ * 512 + 512],
                                xr[:, o_ * T + t_ * 512: o_ * T + t_ * 512 + 512], slab[:, :]))
                emit_norm(P, SC, xr, None, "ln_stats", consts, 1e-5, scratch=nsc)
                w4h = Buf(qTa.t, "w4h")
                w4h.writer = qTa.writer
                w4h.readers = qTa.readers
                with P.step("pool", reads=[qTa], writes=[qTa], dma=True) as st:
                    st.add(lambda e_: e_.dma_start(
                        out=qTa[:, 0:DC * 1024].rearrange("p (c n) -> p c n", c=DC),
                        in_=w4i.rearrange("(c p) n -> p c n", c=DC)))
                for ch in range(DC):
                    with P.step("act", reads=[qTa], writes=[qTa]) as st:
                        st.add(lambda e_, c=ch: e_.activation(
                            qTa[:, c * 1024:(c + 1) * 1024], qTa[:, c * 1024:(c + 1) * 1024], AF.Sign))
                for oc in range(DC):
                    for ic in range(DC):
                        with P.step("pe", reads=[qTa, ident], writes=[pT]) as st:
                            st.add(lambda e_, o_=oc, i_=ic: e_.transpose(
                                pT[:, 0:128], qTa[:, o_ * 1024 + i_ * 128: o_ * 1024 + i_ * 128 + 128],
                                ident[:, :]))
                        with P.step("dve", reads=[pT], writes=[qTa]) as st:
                            st.add(lambda e_, o_=oc, i_=ic: e_.tensor_copy(
                                qTa[:, 8192 + i_ * 1024 + o_ * 128: 8192 + i_ * 1024 + o_ * 128 + 128],
                                pT[:, 0:128]))
                pA_, pB_ = bks["pA"], bks["pB"]
                pB1, pB2 = nsc["pB1"], nsc["pB2"]
                srstd, smr, rbn, mrbn = nsc["srstd"], nsc["smr"], nsc["rb"], nsc["mrb"]
                for r in range(4):
                    with P.step("pe", reads=[srstd, onr], writes=[pB1]) as st:
                        st.add(lambda e_, r_=r: e_.matmul(pB1[:, :], onr[0:1, :],
                                                          srstd[0:1, r_ * 512:(r_ + 1) * 512],
                                                          start=True, stop=True))
                    with P.step("dve", reads=[pB1], writes=[rbn]) as st:
                        st.add(lambda e_: e_.tensor_copy(rbn[:, :], pB1[:, :]))
                    with P.step("pe", reads=[smr, onr], writes=[pB2]) as st:
                        st.add(lambda e_, r_=r: e_.matmul(pB2[:, :], onr[0:1, :],
                                                          smr[0:1, r_ * 512:(r_ + 1) * 512],
                                                          start=True, stop=True))
                    with P.step("dve", reads=[pB2], writes=[mrbn]) as st:
                        st.add(lambda e_: e_.tensor_copy(mrbn[:, :], pB2[:, :]))
                    for ic in range(DC):
                        with P.step("dve", reads=[xr, rbn], writes=[xnr]) as st:
                            st.add(lambda e_, i_=ic, r_=r: e_.tensor_mul(
                                xnr[:, i_ * 512:(i_ + 1) * 512],
                                xr[:, i_ * T + r_ * 512: i_ * T + r_ * 512 + 512], rbn[:, :]))
                        with P.step("dve", reads=[xnr, mrbn], writes=[xnr]) as st:
                            st.add(lambda e_, i_=ic: e_.tensor_sub(
                                xnr[:, i_ * 512:(i_ + 1) * 512], xnr[:, i_ * 512:(i_ + 1) * 512], mrbn[:, :]))
                    for oc in range(DC):
                        pP = pA_ if oc % 2 == 0 else pB_
                        with P.step("pe", reads=[xnr, qTa], writes=[pP]) as st:
                            for ic in range(DC):
                                st.add(lambda e_, i_=ic, o_=oc, pP_=pP: e_.matmul(
                                    pP_[:, :], qTa[:, 8192 + i_ * 1024 + o_ * 128: 8192 + i_ * 1024 + o_ * 128 + 128],
                                    xnr[:, i_ * 512:(i_ + 1) * 512],
                                    start=(i_ == 0), stop=(i_ == DC - 1)))
                        with P.step("dve", reads=[pP, xr], writes=[xr]) as st:
                            st.add(lambda e_, o_=oc, r_=r, pP_=pP: e_.tensor_add(
                                xr[:, o_ * T + r_ * 512: o_ * T + r_ * 512 + 512], pP_[:, :],
                                xr[:, o_ * T + r_ * 512: o_ * T + r_ * 512 + 512]))
                with P.step("sp", reads=[xr], dma=True) as st:
                    st.add(lambda e_: e_.dma_start(out=xfo.rearrange("c p t -> p c t"),
                                                   in_=xr[:, :].rearrange("p (c t) -> p c t", c=DC)))
        _phase9()
        P.final_barrier()
        blk = e(nc.Block())
        P.emit_into_block(blk)
    return nc


def fused_in_maps(x0, W, CST, onc, onr):
    maps = []
    for c in range(8):
        b, p = c // 2, c % 2
        K = CST[p]
        maps.append(dict(
            xin=shard_x(x0, b, p),
            wq0=W["wq"], wk0=W["wk"], wv0=W["wv"], wo0=W["wo"],
            g1i=W["g1"], wqm=W["mWq"], wdkv=W["mWdkv"], wuk=W["mWuk"],
            wuv=W["mWuv"], wom=W["mWo"], g2i=W["g2"], gwi=W["gate"],
            w1d=W["w1"], w2d=W["w2"], n1wi=W["n1w"], n2wi=W["n2w"],
            gwq=W["gwq"], gkvd=W["gkvd"], gkup=W["gkup"], gvup=W["gvup"],
            gwo=W["gwo"], lwq=W["lwq"], lwk=W["lwk"], lwv=W["lwv"],
            lwo=W["lwo"], w4i=W["w4"],
            Cr0i=K["Cr0"], Sr0i=K["Sr0"], Crgi=K["Crg"], Srgi=K["Srg"],
            Crli=K["Crl"], Srli=K["Srl"], Crgfi=K["Crgf"], Srgfi=K["Srgf"],
            mci=K["mask_c"], mlii=K["mask_l"], idi=K["ident"],
            onci=onc, onri=onr, sli=K["sel_lo"], shi=K["sel_hi"],
        ))
    return maps


def kernel_fused(L0_attn_x, L0_attn_wq, L0_attn_wk, L0_attn_wv, L0_attn_wo,
                 L0_ffn_g1, L0_ffn_Wq, L0_ffn_Wdkv, L0_ffn_Wuk, L0_ffn_Wuv,
                 L0_ffn_Wo, L0_ffn_g2, L0_ffn_gate_w, L0_ffn_w1, L0_ffn_w2,
                 L1_attn_norm1_w, L1_attn_norm2_w, L1_attn_g_wq, L1_attn_g_kv_down,
                 L1_attn_g_k_up, L1_attn_g_v_up, L1_attn_g_wo, L1_attn_l_wq,
                 L1_attn_l_wk, L1_attn_l_wv, L1_attn_l_wo, L1_ffn_weight):
    f32 = np.float32
    x0 = np.ascontiguousarray(np.asarray(L0_attn_x, f32))
    W = {k: np.ascontiguousarray(np.asarray(v, f32)) for k, v in dict(
        wq=L0_attn_wq, wk=L0_attn_wk, wv=L0_attn_wv, wo=L0_attn_wo,
        g1=L0_ffn_g1, mWq=L0_ffn_Wq, mWdkv=L0_ffn_Wdkv, mWuk=L0_ffn_Wuk,
        mWuv=L0_ffn_Wuv, mWo=L0_ffn_Wo, g2=L0_ffn_g2, gate=L0_ffn_gate_w,
        w1=L0_ffn_w1, w2=L0_ffn_w2, n1w=L1_attn_norm1_w, n2w=L1_attn_norm2_w,
        gwq=L1_attn_g_wq, gkvd=L1_attn_g_kv_down, gkup=L1_attn_g_k_up,
        gvup=L1_attn_g_v_up, gwo=L1_attn_g_wo, lwq=L1_attn_l_wq,
        lwk=L1_attn_l_wk, lwv=L1_attn_l_wv, lwo=L1_attn_l_wo,
        w4=L1_ffn_weight).items()}
    onc, onr = norm_consts()
    CST = {p: core_constants(p) for p in (0, 1)}
    nc = build_fused()
    res = run_bass_kernel_spmd(nc, fused_in_maps(x0, W, CST, onc, onr),
                               core_ids=list(range(8)))
    outs = [np.asarray(res.results[c]["xfo"]).reshape(DC, 128, T) for c in range(8)]
    return unshard_x(outs)


kernel = kernel_fused


kernel = kernel_fused
